# version 05
# baseline (speedup 1.0000x reference)
import numpy as np
from contextlib import ExitStack

import concourse.bass as bass
import concourse.mybir as mybir
from concourse.bass_utils import run_bass_kernel_spmd
from concourse.library_overlay import lower_extended_insts
from concourse.bass_isa import ReduceOp
from concourse.dve_spec import Spec, Src0, Src1, C0, C1, C2, Zero, maxx, minn
from concourse.dve_op import DveOp
from concourse import dve_ops as _dvo
import concourse.dve_spec as _dsp
from concourse.library_config import all_libraries

F32 = mybir.dt.float32
BF16 = mybir.dt.bfloat16
AF = mybir.ActivationFunctionType
ALU = mybir.AluOpType
AX = mybir.AxisListType

MAGIC = 12582912.0

def _register_op(name, spec, ref):
    if name in _dvo._SUB_OPCODE_FOR_NAME:
        return next(o for o in _dvo.OPS if o.name == name)
    lowered = _dsp.lower(spec, ver="v3", perf_en=False, name=name)
    op = DveOp(name, spec, subdim=False, uops_sha={"v3": lowered.sha("v3")},
               reference=ref)
    row = max(_dvo._SUB_OPCODE_FOR_NAME.values()) + 1
    assert row < 0x20
    _dvo.OPS.append(op)
    _dvo._SUB_OPCODE_FOR_NAME[op.name] = row
    _dvo.CUSTOM_DVE_SPECS[op.name] = spec
    _dvo.CUSTOM_DVE_REFS[op.name] = ref
    return op

RND_TT = _register_op(
    "RND_TT_ANT", Spec(body=(Src0 * Src1 + C0) - C0),
    lambda in0, in1, s0, s1, imm2: np.rint(in0 * in1))
TERN_TS = _register_op(
    "TERN_TS_ANT", Spec(body=minn(maxx(((Src0 * C1) + C0) - C0, Zero - C2), C2)),
    lambda in0, in1, s0, s1, imm2: np.minimum(np.maximum(np.rint(in0 * s1), -imm2), imm2))

def _hadamard(n):
    H = np.ones((1, 1), np.float32)
    while H.shape[0] < n:
        H = np.block([[H, H], [H, -H]])
    return H

H128 = _hadamard(128)
H16 = _hadamard(16)
H64 = _hadamard(64)
M1 = np.zeros((128, 128), np.float32)
for c in range(16):
    for a in range(8):
        for cp in range(16):
            M1[c * 8 + a, cp * 8 + a] = H16[c, cp]
M2 = np.zeros((128, 128), np.float32)
for c in range(64):
    for a in range(2):
        for cp in range(64):
            M2[c * 2 + a, cp * 2 + a] = H64[c, cp]
L1K = np.triu(np.ones((128, 128), np.float32), 0)
PEN = np.tril(np.ones((128, 128), np.float32), -1) * (-10000.0)
I128 = np.eye(128, dtype=np.float32)
OH16 = np.zeros((16, 16 * 128), np.float32)
for j in range(16):
    OH16[j, j * 128:(j + 1) * 128] = 1.0
OH64 = np.zeros((64, 64 * 128), np.float32)
for h in range(64):
    OH64[h, h * 128:(h + 1) * 128] = 1.0

NCORE = 8
TOWN = 2048
HALO = 256
TT = TOWN + HALO
NCH = TT // 128
NOWNCH = TOWN // 128
DM = 2048
CONVD = 5120
DI = 4096
DS = 64
NG = 8
NH = 64
PH = 64
FFI = 8192


class P:
    ENGINES = ("sync", "gpsimd", "tensor", "vector", "scalar")
    CHAIN = {"vector": "ov", "scalar": "oc"}

    def __init__(self, nc, stack):
        self.nc = nc
        self.plans = {e: [] for e in self.ENGINES}
        self.sems = {}
        self.counts = {}
        self.stack = stack

    def sem(self, name):
        if name not in self.sems:
            self.sems[name] = self.stack.enter_context(self.nc.semaphore(name))
            self.counts[name] = 0
        return self.sems[name]

    def emit(self, eng, fn, waits=(), inc=None):
        w = list(waits)
        if eng in self.CHAIN:
            assert inc is None
            sname = self.CHAIN[eng]
            self.sem(sname)
            if self.counts[sname] > 0:
                w.append((sname, self.counts[sname]))
            self.counts[sname] += 1
            inc = (sname, 1)
            out_count = self.counts[sname]
        elif inc is not None:
            sname, amt = inc
            self.sem(sname)
            self.counts[sname] += amt
            out_count = self.counts[sname]
        else:
            out_count = None
        for sname2, _v in w:
            self.sem(sname2)
        self.plans[eng].append((w, fn, inc))
        return out_count

    def cur(self, sname):
        self.sem(sname)
        return self.counts[sname]

    def barrier(self, waits):
        for eng in self.ENGINES:
            if eng in self.CHAIN:
                self.emit(eng, lambda e: e.nop(),
                          waits=[x for x in waits if x[0] != self.CHAIN[eng]])
            else:
                self.emit(eng, lambda e: e.nop(), waits=list(waits))

    def build_block(self, block, final_waits):
        for eng_name in self.ENGINES:
            plan = self.plans[eng_name]
            dec = getattr(block, eng_name)

            def mk(eng_name, plan):
                def body(eng):
                    for waits, fn, inc in plan:
                        for sname, v in waits:
                            eng.wait_ge(self.sems[sname], v)
                        r = fn(eng)
                        if inc is not None:
                            r.then_inc(self.sems[inc[0]], inc[1])
                    if eng_name == "sync":
                        for sname, v in final_waits:
                            eng.wait_ge(self.sems[sname], v)
                return body
            dec(mk(eng_name, plan))


def build_program(upto_phase=None):
    import os
    UPTO = int(os.environ.get("K_UPTO", "9")) if upto_phase is None else upto_phase
    BSUB = int(os.environ.get("K_BSUB", "9"))
    nc = bass.Bass("TRN2", target_bir_lowering=False, debug=False)

    x_fm = nc.dram_tensor("x_fm", [DM, TT], F32, kind="ExternalInput").ap()
    wt_in = nc.dram_tensor("wt_in", [DM, 9280], F32, kind="ExternalInput").ap()
    convw = nc.dram_tensor("convw", [128, 40 * 4], F32, kind="ExternalInput").ap()
    convb = nc.dram_tensor("convb", [128, 40], F32, kind="ExternalInput").ap()
    zb = nc.dram_tensor("zb", [128, 32], F32, kind="ExternalInput").ap()
    dvec = nc.dram_tensor("dvec", [1, 64], F32, kind="ExternalInput").ap()
    wt_out = nc.dram_tensor("wt_out", [DI, DM], F32, kind="ExternalInput").ap()
    wg_p = nc.dram_tensor("wg_p", [DM, FFI], F32, kind="ExternalInput").ap()
    wu_p = nc.dram_tensor("wu_p", [DM, FFI], F32, kind="ExternalInput").ap()
    wd_p = nc.dram_tensor("wd_p", [FFI, DM], F32, kind="ExternalInput").ap()
    hm_fm = nc.dram_tensor("hm_fm", [128, 1], F32, kind="ExternalInput").ap()
    out_fm = nc.dram_tensor("out_fm", [DM, TOWN], F32, kind="ExternalOutput").ap()

    xbc_sp = nc.dram_tensor("xbc_sp", [CONVD, TT], F32, kind="Internal").ap()
    z_sp = nc.dram_tensor("z_sp", [DI, TT], F32, kind="Internal").ap()
    ghi_sp = nc.dram_tensor("ghi_sp", [DI, TOWN], BF16, kind="Internal").ap()
    glo_sp = nc.dram_tensor("glo_sp", [DI, TOWN], BF16, kind="Internal").ap()
    y1_sp = nc.dram_tensor("y1_sp", [DM, TOWN], F32, kind="Internal").ap()
    y1f_sc = nc.dram_tensor("y1f_sc", [16, 128, TOWN], F32, kind="Internal").ap()
    sam_sc = nc.dram_tensor("sam_sc", [64, 128, TOWN], F32, kind="Internal").ap()
    wdt_sp = nc.dram_tensor("wdt_sp", [FFI, DM], BF16, kind="Internal").ap()
    dbg = nc.dram_tensor("dbg", [128, 2 * TT], F32, kind="ExternalOutput").ap()

    h128_c = nc.inline_tensor(H128, name="h128c").ap()
    m1_c = nc.inline_tensor(M1, name="m1c").ap()
    m2_c = nc.inline_tensor(M2, name="m2c").ap()
    l1_c = nc.inline_tensor(L1K, name="l1c").ap()
    pen_c = nc.inline_tensor(PEN, name="penc").ap()
    i128_c = nc.inline_tensor(I128, name="i128c").ap()
    oh_c = nc.inline_tensor(OH64, name="ohc").ap()
    noh_c = nc.inline_tensor(-OH64, name="nohc").ap()
    oh16_c = nc.inline_tensor(OH16, name="oh16c").ap()
    onesr_c = nc.inline_tensor(np.ones((1, 128), np.float32), name="onesrc").ap()
    onescol_c = nc.inline_tensor(np.ones((128, 1), np.float32), name="onescolc").ap()

    with ExitStack() as stack:
        def sbuf(name, shape, dtype=F32, st=None):
            return (st or stack).enter_context(nc.sbuf_tensor(name, shape, dtype))
        def psum(name, shape, dtype=F32, st=None):
            return (st or stack).enter_context(nc.psum_tensor(name, shape, dtype))

        p = P(nc, stack)
        block = stack.enter_context(nc.Block())

        s_h128 = sbuf("s_h128", [128, 128])
        s_m1 = sbuf("s_m1", [128, 128])
        s_m2 = sbuf("s_m2", [128, 128])
        s_l1 = sbuf("s_l1", [128, 128])
        s_pen_bf = sbuf("s_pen_bf", [128, 128], BF16)
        s_i128 = sbuf("s_i128", [128, 128])
        s_i128bf = sbuf("s_i128bf", [128, 128], BF16)
        s_onesr = sbuf("s_onesr", [1, 128])
        s_onesr_bf = sbuf("s_onesr_bf", [1, 128], BF16)
        s_onescol = sbuf("s_onescol", [128, 1])
        s_convw = sbuf("s_convw", [128, 40 * 4])
        s_convb = sbuf("s_convb", [128, 40])
        s_zb = sbuf("s_zb", [128, 32])
        s_dvec = sbuf("s_dvec", [1, 64])
        s_alog = sbuf("s_alog", [64, TT])
        BANKS = [psum(f"pbk{i}", [128, 512]) for i in range(8)]
        pA = BANKS[:4]

        for src, dst in [(h128_c, s_h128), (m1_c, s_m1), (m2_c, s_m2), (l1_c, s_l1),
                         (i128_c, s_i128), (onesr_c, s_onesr), (onescol_c, s_onescol),
                         (convw, s_convw), (convb, s_convb), (zb, s_zb), (dvec, s_dvec)]:
            p.emit("sync", lambda e, s=src, d=dst: e.dma_start(out=d[:], in_=s[:]),
                   inc=("qC", 16))
        for src, dst in [(pen_c, s_pen_bf), (i128_c, s_i128bf), (onesr_c, s_onesr_bf)]:
            p.emit("gpsimd", lambda e, s=src, d=dst: e.dma_start(out=d[:], in_=s[:]),
                   inc=("qCg", 16))
        consts_qS = p.cur("qC")
        consts_qG = p.cur("qCg")
        p.barrier([("qC", consts_qS), ("qCg", consts_qG)])

        stA = ExitStack()
        EW = 128
        HLEN0, HOFF1, HLEN1 = 1152, 1149, 1155
        HL = max(HLEN0, HLEN1)
        s_xhi = sbuf("s_xhi", [128, 16 * HL], BF16, st=stA)
        s_xlo = sbuf("s_xlo", [128, 16 * HL], BF16, st=stA)
        s_xf = sbuf("s_xf", [128, 2 * HL], st=stA)
        s_wf = sbuf("s_wf", [128, 2 * 16 * EW], st=stA)
        s_whi = sbuf("s_whi", [128, 2 * 16 * EW], BF16, st=stA)
        s_wlo = sbuf("s_wlo", [128, 2 * 16 * EW], BF16, st=stA)
        s_acc = sbuf("s_acc", [128, 2 * HL], st=stA)
        s_cv = sbuf("s_cv", [128, HL], st=stA)
        s_out = sbuf("s_out", [128, 2 * HL], st=stA)

        xv = x_fm.rearrange("(k p) t -> p k t", p=128)
        wv = wt_in.rearrange("(k p) e -> p k e", p=128)
        ET = [("xbc", i) for i in range(40)] + [("z", i) for i in range(32)] + [("a", 0)]
        grp_seq = 0
        spq_count = {0: 0, 1: 0}

        last_half_ov = 0
        evac_rec = {}
        wsplit_done = {}
        for th, (toff, tlen) in enumerate([(0, HLEN0), (HOFF1, HLEN1)]):
            xsplit_done = {}
            for k in range(16):
                buf = k % 2
                waits = [("ov", xsplit_done[k - 2])] if k >= 2 else (
                    [("ov", last_half_ov)] if th else [])
                c_q = p.emit("sync", lambda e, k=k, b=buf, toff=toff, tlen=tlen: e.dma_start(
                    out=s_xf[:, b * HL:b * HL + tlen], in_=xv[:, k, toff:toff + tlen]),
                    waits=waits, inc=(f"xq{buf}", 16))
                p.emit("vector", lambda e, k=k, b=buf, tlen=tlen: e.tensor_copy(
                    s_xhi[:, k * HL:k * HL + tlen], s_xf[:, b * HL:b * HL + tlen]),
                    waits=[(f"xq{buf}", c_q)])
                xsplit_done[k] = p.emit("vector", lambda e, k=k, b=buf, tlen=tlen:
                    e.scalar_tensor_tensor(
                        s_xlo[:, k * HL:k * HL + tlen], s_xhi[:, k * HL:k * HL + tlen],
                        -1.0, s_xf[:, b * HL:b * HL + tlen], ALU.mult, ALU.add))
            ngw = [512] * (tlen // 512) + ([tlen % 512] if tlen % 512 else [])
            wsplit_done = wsplit_done if th else {}

            for ei, (kind, idx) in enumerate(ET):
                gei = th * len(ET) + ei
                if kind == "xbc":
                    e0, ewid = idx * 128, 128
                elif kind == "z":
                    e0, ewid = 5120 + idx * 128, 128
                else:
                    e0, ewid = 9216, 64
                wb = gei % 2
                wbase = wb * 16 * EW
                waits_wf = [("ov", wsplit_done[gei - 2])] if gei >= 2 else []
                waits_wb = [("ot", 3 * (gei - 1))] if gei >= 2 else []
                cS = p.emit("sync", lambda e, e0=e0, ewid=ewid, wbase=wbase: e.dma_start(
                    out=s_wf[:, wbase:wbase + 16 * ewid].rearrange("p (k e) -> p k e", k=16),
                    in_=wv[:, :, e0:e0 + ewid]), waits=waits_wf, inc=(f"wfq{wb}", 16))
                cG = p.emit("gpsimd", lambda e, e0=e0, ewid=ewid, wbase=wbase: e.dma_start(
                    out=s_whi[:, wbase:wbase + 16 * ewid].rearrange("p (k e) -> p k e", k=16),
                    in_=wv[:, :, e0:e0 + ewid]), waits=waits_wb, inc=(f"whq{wb}", 16))
                cWrd = p.emit("vector", lambda e, ewid=ewid, wbase=wbase: e.scalar_tensor_tensor(
                    s_wlo[:, wbase:wbase + 16 * ewid], s_whi[:, wbase:wbase + 16 * ewid],
                    -1.0, s_wf[:, wbase:wbase + 16 * ewid], ALU.mult, ALU.add),
                    waits=[(f"wfq{wb}", cS), (f"whq{wb}", cG)])
                wsplit_done[gei] = cWrd

                t0 = 0
                for tg, tw in enumerate(ngw):
                    bank = grp_seq % 4
                    waits_mm = [("ov", cWrd)]
                    if grp_seq >= 4:
                        waits_mm.append(evac_rec[grp_seq - 4])
                    first = True
                    for k in range(16):
                        for pi, (wsb, xsb) in enumerate(((s_whi, s_xhi), (s_whi, s_xlo),
                                                         (s_wlo, s_xhi))):
                            w_ap = wsb[:, wbase + k * ewid: wbase + (k + 1) * ewid]
                            x_ap = xsb[:, k * HL + t0: k * HL + t0 + tw]
                            last = (k == 15 and pi == 2)
                            teA_c = p.emit("tensor", lambda e, w=w_ap, x=x_ap, b=bank,
                                   ewid=ewid, tw=tw, f=first, l=last:
                                   e.matmul(pA[b][0:ewid, 0:tw], w, x, start=f, stop=l),
                                   waits=(waits_mm if first else []),
                                   inc=(("ot", 1) if last else None))
                            first = False
                    ob = (gei % 2) * HL
                    if kind == "a":
                        cv_ = p.emit("vector", lambda e, b=bank, t0=t0, tw=tw, toff=toff:
                               e.tensor_copy(s_alog[:, toff + t0: toff + t0 + tw],
                                             pA[b][0:64, 0:tw]),
                               waits=[("ot", teA_c)])
                        evac_rec[grp_seq] = ("ov", cv_)
                    elif kind == "z":
                        wz = [("ot", teA_c)]
                        if spq_count[gei % 2] > 0:
                            wz.append((f"spq{gei % 2}", 16 * spq_count[gei % 2]))
                        cs_ = p.emit("scalar", lambda e, b=bank, t0=t0, tw=tw, ob=ob, idx=idx:
                               e.activation(s_out[:, ob + t0: ob + t0 + tw], pA[b][:, 0:tw],
                                            AF.Silu, bias=s_zb[:, idx:idx + 1]),
                               waits=wz)
                        evac_rec[grp_seq] = ("oc", cs_)
                    else:
                        cv_ = p.emit("vector", lambda e, b=bank, t0=t0, tw=tw, ob=ob:
                               e.tensor_copy(s_acc[:, ob + t0: ob + t0 + tw],
                                             pA[b][:, 0:tw]),
                               waits=[("ot", teA_c)])
                        evac_rec[grp_seq] = ("ov", cv_)
                    t0 += tw
                    grp_seq += 1

                evA_after = [evac_rec[g] for g in range(grp_seq - len(ngw), grp_seq)]
                ob = (gei % 2) * HL
                sp0 = 0 if th == 0 else HLEN0
                lcl0 = 0 if th == 0 else 3
                splen = tlen - lcl0
                if kind == "xbc":
                    w3 = s_convw[:, idx * 4 + 3: idx * 4 + 4]
                    p.emit("vector", lambda e, ob=ob, w3=w3, tlen=tlen: e.tensor_scalar(
                        s_cv[:, 0:tlen], s_acc[:, ob:ob + tlen], w3, None, ALU.mult),
                        waits=list(evA_after))
                    cv_last = None
                    for kk, sh in ((2, 1), (1, 2), (0, 3)):
                        wk = s_convw[:, idx * 4 + kk: idx * 4 + kk + 1]
                        cv_last = p.emit("vector", lambda e, ob=ob, wk=wk, sh=sh, tlen=tlen:
                               e.scalar_tensor_tensor(
                                   s_cv[:, sh:tlen], s_acc[:, ob: ob + tlen - sh], wk,
                                   s_cv[:, sh:tlen], ALU.mult, ALU.add))
                    wsl = [("ov", cv_last)]
                    if spq_count[gei % 2] > 0:
                        wsl.append((f"spq{gei % 2}", 16 * spq_count[gei % 2]))
                    cs = p.emit("scalar", lambda e, ob=ob, idx=idx, tlen=tlen: e.activation(
                        s_out[:, ob:ob + tlen], s_cv[:, 0:tlen], AF.Silu,
                        bias=s_convb[:, idx:idx + 1]),
                        waits=wsl)
                    p.emit("sync", lambda e, ob=ob, idx=idx, sp0=sp0, lcl0=lcl0,
                           splen=splen: e.dma_start(
                        out=xbc_sp[idx * 128:(idx + 1) * 128, sp0:sp0 + splen],
                        in_=s_out[:, ob + lcl0: ob + lcl0 + splen]),
                        waits=[("oc", cs)], inc=(f"spq{gei % 2}", 16))
                    spq_count[gei % 2] += 1
                elif kind == "z":
                    p.emit("sync", lambda e, ob=ob, idx=idx, sp0=sp0, lcl0=lcl0,
                           splen=splen: e.dma_start(
                        out=z_sp[idx * 128:(idx + 1) * 128, sp0:sp0 + splen],
                        in_=s_out[:, ob + lcl0: ob + lcl0 + splen]),
                        waits=list(evA_after), inc=(f"spq{gei % 2}", 16))
                    spq_count[gei % 2] += 1

            last_half_ov = p.cur("ov")
        p.barrier([("spq0", p.cur("spq0")), ("spq1", p.cur("spq1")),
                   ("ov", p.cur("ov")), ("oc", p.cur("oc")), ("ot", p.cur("ot"))])
        stA.close()

        if UPTO >= 2:
            stB = ExitStack()
            s_dtT = sbuf("s_dtT", [128, NCH * 64], st=stB)
            s_DdT = sbuf("s_DdT", [128, NCH * 64], st=stB)
            s_dstT = sbuf("s_dstT", [128, NCH * 64], st=stB)
            s_ahi = sbuf("s_ahi", [64, NCH * 128], BF16, st=stB)
            s_alo = sbuf("s_alo", [64, NCH * 128], BF16, st=stB)
            s_nahi = sbuf("s_nahi", [64, NCH * 128], BF16, st=stB)
            s_nalo = sbuf("s_nalo", [64, NCH * 128], BF16, st=stB)
            s_ash = sbuf("s_ash", [128, NCH * 64], st=stB)
            s_dout = sbuf("s_dout", [128, NCH * 64], st=stB)
            s_d32 = sbuf("s_d32", [128, NCH * 32], st=stB)
            s_dbc = sbuf("s_dbc", [128, 64], st=stB)
            s_ohbf = sbuf("s_ohbf", [64, 64 * 128], BF16, st=stB)
            s_spk = sbuf("s_spk", [128, 32 * 64], st=stB)
            s_u1 = sbuf("s_u1", [128, 128], st=stB)
            s_dtr = sbuf("s_dtr", [1, NCH * 64], st=stB)
            s_hm = sbuf("s_hm", [128, 1], st=stB)

            stPre = ExitStack()
            s_prb = sbuf("s_prb", [128, NCH * 64], st=stPre)
            s_ahsf = sbuf("s_ahsf", [64, NCH * 128], st=stPre)
            bank_rec = {}

            def bank_wait(i):
                return [bank_rec[i]] if i in bank_rec else []

            cg1 = p.emit("gpsimd", lambda e: e.dma_start(out=s_ohbf[:], in_=oh_c[:]),
                         inc=("qBg", 16))
            p.emit("gpsimd", lambda e: e.dma_start(out=s_hm[:], in_=hm_fm[:]),
                   inc=("qBg", 16))
            p.emit("vector", lambda e: e.tensor_scalar(s_u1[:], s_l1[:], -1.0, 1.0,
                                                       ALU.mult, ALU.add))
            p.emit("vector", lambda e: e.memset(s_spk[:], 0.0))
            p.barrier([("qBg", p.cur("qBg"))])

            for c in range(NCH):
                b = c % 2
                tw = bank_wait(b)
                cT = p.emit("tensor", lambda e, c=c, b=b: e.transpose(
                    BANKS[b][:, 0:64], s_alog[:, c * 128:(c + 1) * 128],
                    s_i128[0:64, 0:64]), waits=tw, inc=("ot", 1))
                cv = p.emit("vector", lambda e, c=c, b=b: e.tensor_copy(
                    s_prb[:, c * 64:(c + 1) * 64], BANKS[b][:, 0:64]),
                    waits=[("ot", cT)])
                bank_rec[b] = ("ov", cv)
            c1 = p.emit("scalar", lambda e: e.activation(s_dtT[:], s_prb[:], AF.Abs),
                        waits=[("ov", p.cur("ov"))])
            c2 = p.emit("scalar", lambda e: e.activation(s_dtT[:], s_dtT[:], AF.Exp, scale=-1.0))
            c3 = p.emit("vector", lambda e: e.tensor_scalar_add(s_dtT[:], s_dtT[:], 1.0),
                        waits=[("oc", c2)])
            c4 = p.emit("scalar", lambda e: e.activation(s_dtT[:], s_dtT[:], AF.Ln),
                        waits=[("ov", c3)])
            c5 = p.emit("scalar", lambda e: e.activation(s_DdT[:], s_prb[:], AF.Relu))
            c6 = p.emit("vector", lambda e: e.tensor_add(s_dtT[:], s_dtT[:], s_DdT[:]),
                        waits=[("oc", c5), ("oc", c4)])
            c7 = p.emit("vector", lambda e: e.reciprocal(s_DdT[:], s_dtT[:]))
            cT = p.emit("tensor", lambda e: e.matmul(BANKS[0][:, 0:64], s_onesr[:],
                                                     s_dvec[:], start=True, stop=True),
                        waits=bank_wait(0), inc=("ot", 1))
            cv = p.emit("vector", lambda e: e.tensor_copy(s_dbc[:], BANKS[0][:, 0:64]),
                        waits=[("ot", cT)])
            bank_rec[0] = ("ov", cv)
            c8 = p.emit("vector", lambda e: e.tensor_tensor(
                s_DdT[:].rearrange("p (c h) -> p c h", c=NCH),
                s_DdT[:].rearrange("p (c h) -> p c h", c=NCH),
                s_dbc[:].unsqueeze(1).broadcast_to([128, NCH, 64]), ALU.mult))
            c9 = p.emit("vector", lambda e: e.tensor_scalar_mul(s_prb[:], s_dtT[:], -1.0))
            for c in range(NCH):
                b0, b1 = c % 2, 2 + (c % 2)
                ndt = s_prb[:, c * 64:(c + 1) * 64]
                w0 = bank_wait(b0) + ([("ov", c9)] if c <= 1 else [])
                cT0 = p.emit("tensor", lambda e, ndt=ndt, b0=b0: e.matmul(
                    BANKS[b0][0:64, 0:128], ndt, s_l1[:], start=True, stop=True),
                    waits=w0, inc=("ot", 1))
                cv0 = p.emit("vector", lambda e, c=c, b0=b0: e.tensor_copy(
                    s_ahsf[:, c * 128:(c + 1) * 128], BANKS[b0][0:64, 0:128]),
                    waits=[("ot", cT0)])
                bank_rec[b0] = ("ov", cv0)
                w1 = bank_wait(b1) + ([("ov", c9)] if c <= 1 else [])
                cT1 = p.emit("tensor", lambda e, ndt=ndt, b1=b1: e.matmul(
                    BANKS[b1][:, 0:64], s_l1[:], ndt, start=True, stop=True),
                    waits=w1, inc=("ot", 1))
                cT2 = p.emit("tensor", lambda e, ndt=ndt, b1=b1: e.matmul(
                    BANKS[b1][:, 64:128], s_u1[:], ndt, start=True, stop=True),
                    inc=("ot", 1))
                cv1 = p.emit("vector", lambda e, c=c, b1=b1: e.tensor_copy(
                    s_ash[:, c * 64:(c + 1) * 64], BANKS[b1][:, 0:64]),
                    waits=[("ot", cT2)])
                cv2 = p.emit("vector", lambda e, c=c, b1=b1: e.tensor_copy(
                    s_dstT[:, c * 64:(c + 1) * 64], BANKS[b1][:, 64:128]))
                bank_rec[b1] = ("ov", cv2)
            cdd = p.emit("scalar", lambda e: e.activation(s_dstT[:], s_dstT[:], AF.Exp),
                         waits=[("ov", p.cur("ov"))])
            cd1 = p.emit("scalar", lambda e: e.activation(s_dout[:], s_ash[:], AF.Exp))
            for j in range(3):
                b = j % 2
                tw = bank_wait(b)
                cT = p.emit("tensor", lambda e, j=j, b=b: e.matmul(
                    BANKS[b][0:1, 0:384], s_onescol[:], s_prb[:, j * 384:(j + 1) * 384],
                    start=True, stop=True), waits=tw, inc=("ot", 1))
                cv = p.emit("vector", lambda e, j=j, b=b: e.tensor_copy(
                    s_dtr[:, j * 384:(j + 1) * 384], BANKS[b][0:1, 0:384]),
                    waits=[("ot", cT)])
                bank_rec[b] = ("ov", cv)
            cd2 = p.emit("scalar", lambda e: e.activation(s_dtr[:], s_dtr[:], AF.Exp),
                         waits=[("ov", p.cur("ov"))])
            ch1 = p.emit("vector", lambda e: e.tensor_copy(s_ahi[:], s_ahsf[:]))
            ch2 = p.emit("vector", lambda e: e.scalar_tensor_tensor(
                s_alo[:], s_ahi[:], -1.0, s_ahsf[:], ALU.mult, ALU.add))
            ch3 = p.emit("vector", lambda e: e.tensor_scalar_mul(s_nahi[:], s_ahi[:], -1.0))
            ch4 = p.emit("vector", lambda e: e.tensor_scalar_mul(s_nalo[:], s_alo[:], -1.0))
            for c in range(NCH):
                b = c % 2
                tw = bank_wait(b) + ([("oc", cd2)] if c <= 1 else [])
                r0 = s_dtr[:, c * 64:(c + 1) * 64].rearrange("o (g m) -> o g m", g=8)
                cT0 = p.emit("tensor", lambda e, r0=r0, b=b: e.matmul(
                    BANKS[b][0:64, 0:32], s_onesr[:, 0:64], r0[:, 0::2, :],
                    start=True, stop=True), waits=tw, inc=("ot", 1))
                cT1 = p.emit("tensor", lambda e, r0=r0, b=b: e.matmul(
                    BANKS[b][64:128, 0:32], s_onesr[:, 0:64], r0[:, 1::2, :], start=True,
                    stop=True, tile_position=(0, 64)), inc=("ot", 1))
                cv = p.emit("vector", lambda e, c=c, b=b: e.tensor_copy(
                    s_d32[:, c * 32:(c + 1) * 32], BANKS[b][:, 0:32]), waits=[("ot", cT1)])
                bank_rec[b] = ("ov", cv)
            p.barrier([("ov", p.cur("ov")), ("oc", p.cur("oc")), ("ot", p.cur("ot"))])
            stPre.close()

            s_xbcr = sbuf("s_xbcr", [128, 40 * 128], st=stB)
            s_zr = sbuf("s_zr", [128, 32 * 128], st=stB)
            s_xtm = sbuf("s_xtm", [128, 32 * 128], st=stB)
            s_btm = sbuf("s_btm", [128, 4 * 128], st=stB)
            s_xd = sbuf("s_xd", [128, 32 * 128], st=stB)
            s_sct = sbuf("s_sct", [128, 8 * 128], st=stB)
            s_lmq = sbuf("s_lmq", [128, 2 * 512], st=stB)
            s_lsq = sbuf("s_lsq", [128, 2 * 512], st=stB)
            s_tmp1 = sbuf("s_tmp1", [128, 512], st=stB)
            s_yf = sbuf("s_yf", [128, 2 * 512], st=stB)
            s_gat = sbuf("s_gat", [128, 32 * 128], st=stB)
            s_ghi = sbuf("s_ghi", [128, 32 * 128], BF16, st=stB)
            s_glo = sbuf("s_glo", [128, 32 * 128], BF16, st=stB)

            xbv = xbc_sp.rearrange("(i p) t -> p i t", p=128)
            zv = z_sp.rearrange("(i p) t -> p i t", p=128)
            gh_v = ghi_sp.rearrange("(i p) t -> p i t", p=128)
            gl_v = glo_sp.rearrange("(i p) t -> p i t", p=128)
            xb_rel = {}
            z_rel = {}
            gat_rel = {}
            gsp_cnt = 0
            for c in range(NCH):
                own = c >= 2
                wx = [xb_rel[c - 1]] if c >= 1 else []
                cxq = p.emit("sync", lambda e, c=c: e.dma_start(
                    out=s_xbcr[:].rearrange("p (i t) -> p i t", i=40),
                    in_=xbv[:, :, c * 128:(c + 1) * 128]),
                    waits=wx, inc=("xbq", 16))
                if own:
                    wz = [z_rel[c - 1]] if c >= 3 else []
                    czq = p.emit("sync", lambda e, c=c: e.dma_start(
                        out=s_zr[:].rearrange("p (i t) -> p i t", i=32),
                        in_=zv[:, :, c * 128:(c + 1) * 128]),
                        waits=wz, inc=("zbq", 16))

                if c < 2:
                    chm = p.emit("vector", lambda e: e.tensor_scalar_mul(
                        s_xbcr[:], s_xbcr[:], s_hm[:, 0:1]), waits=[("xbq", cxq)])
                    xwait = [("ov", chm)]
                else:
                    xwait = [("xbq", cxq)]

                for grp in range(9 if BSUB >= 2 else 0):
                    b = grp % 2
                    tw = bank_wait(b)
                    cT = None
                    for j in range(4):
                        i = grp * 4 + j
                        src = s_xbcr[:, i * 128:(i + 1) * 128]
                        cT = p.emit("tensor", lambda e, src=src, b=b, j=j: e.transpose(
                            BANKS[b][:, j * 128:(j + 1) * 128], src, s_i128[:]),
                            waits=((tw + xwait) if j == 0 else []),
                            inc=(("ot", 1) if j == 3 else None))
                    dst = (s_xtm[:, grp * 512:(grp + 1) * 512] if grp < 8
                           else s_btm[:, 0:512])
                    cv = p.emit("vector", lambda e, dst=dst, b=b: e.tensor_copy(
                        dst, BANKS[b][:, 0:512]), waits=[("ot", cT)])
                    bank_rec[b] = ("ov", cv)
                xtm_done = p.cur("ov")

                if BSUB < 2:
                    xb_rel[c] = ("xbq", cxq)
                    if own:
                        z_rel[c] = ("zbq", czq)
                    continue
                cxd = p.emit("vector", lambda e, c=c: e.tensor_tensor(
                    s_xd[:].rearrange("p (h q) -> p h q", h=64),
                    s_xtm[:].rearrange("p (h q) -> p h q", h=64),
                    s_dstT[:, c * 64:(c + 1) * 64].unsqueeze(2).broadcast_to([128, 64, 64]),
                    ALU.mult))

                if own and BSUB >= 3:
                    for par in range(2):
                        b = 2 + par
                        tw = bank_wait(b)
                        cT = None
                        for j in range(4):
                            ti_b = 32 + j
                            ti_c = 36 + j
                            pr = par * 64
                            lb = s_xbcr[pr:pr + 64, ti_b * 128:(ti_b + 1) * 128]
                            rc_ = s_xbcr[pr:pr + 64, ti_c * 128:(ti_c + 1) * 128]
                            cT = p.emit("tensor", lambda e, lb=lb, rc_=rc_, b=b, j=j:
                                        e.matmul(BANKS[b][:, j * 128:(j + 1) * 128], lb, rc_,
                                                 start=True, stop=True),
                                        waits=(tw if j == 0 else []),
                                        inc=(("ot", 1) if j == 3 else None))
                        dst = s_sct[:].rearrange("p (j q) -> p j q", j=4)[
                            :, :, par * 128:(par + 1) * 128]
                        cv = p.emit("vector", lambda e, dst=dst, b=b: e.tensor_copy(
                            dst, BANKS[b][:, 0:512].rearrange("p (j s) -> p j s", j=4)),
                            waits=[("ot", cT)])
                        bank_rec[b] = ("ov", cv)
                    sct_done = p.cur("ov")

                    for q in range(16 if BSUB >= 4 else 0):
                        qb = q % 2
                        b = qb
                        tw = bank_wait(b)
                        cT = None
                        for j in range(4):
                            h = q * 4 + j
                            sl = slice(j * 128, (j + 1) * 128)
                            ohs = s_ohbf[:, h * 128:(h + 1) * 128]
                            ahi = s_ahi[:, c * 128:(c + 1) * 128]
                            alo = s_alo[:, c * 128:(c + 1) * 128]
                            nahi = s_nahi[:, c * 128:(c + 1) * 128]
                            nalo = s_nalo[:, c * 128:(c + 1) * 128]
                            p.emit("tensor", lambda e, ohs=ohs, ahi=ahi, b=b, sl=sl:
                                   e.matmul(BANKS[b][:, sl], ohs, ahi, start=True, stop=False),
                                   waits=(tw if j == 0 else []))
                            p.emit("tensor", lambda e, ohs=ohs, alo=alo, b=b, sl=sl:
                                   e.matmul(BANKS[b][:, sl], ohs, alo, start=False, stop=False))
                            p.emit("tensor", lambda e, ohs=ohs, nahi=nahi, b=b, sl=sl:
                                   e.matmul(BANKS[b][:, sl], nahi, ohs, start=False, stop=False))
                            p.emit("tensor", lambda e, ohs=ohs, nalo=nalo, b=b, sl=sl:
                                   e.matmul(BANKS[b][:, sl], nalo, ohs, start=False, stop=False))
                            cT = p.emit("tensor", lambda e, b=b, sl=sl:
                                        e.matmul(BANKS[b][:, sl], s_i128bf[:], s_pen_bf[:],
                                                 start=False, stop=True),
                                        inc=(("ot", 1) if j == 3 else None))
                        lq = (q % 2) * 512
                        cs = p.emit("scalar", lambda e, b=b, lq=lq: e.activation(
                            s_lmq[:, lq:lq + 512], BANKS[b][:, 0:512], AF.Exp),
                            waits=[("ot", cT)])
                        bank_rec[b] = ("oc", cs)
                        g = q >> 1
                        cls = p.emit("vector", lambda e, g=g, lq=lq: e.tensor_tensor(
                            s_lsq[:, lq:lq + 512].rearrange("p (j s) -> p j s", j=4),
                            s_lmq[:, lq:lq + 512].rearrange("p (j s) -> p j s", j=4),
                            s_sct[:, g * 128:(g + 1) * 128].unsqueeze(1)
                            .broadcast_to([128, 4, 128]), ALU.mult),
                            waits=[("oc", cs), ("ov", sct_done)])

                        yb = 4 + qb
                        ob_ = 6 + qb
                        for j in range(4):
                            h = q * 4 + j
                            lsl = s_lsq[:, lq + j * 128: lq + (j + 1) * 128]
                            xsl = s_xtm[:, h * 64:(h + 1) * 64]
                            w0 = (bank_wait(yb) + [("ov", cls)]) if j == 0 else []
                            p.emit("tensor", lambda e, lsl=lsl, xsl=xsl, yb=yb, j=j:
                                   e.matmul(BANKS[yb][:, j * 64:(j + 1) * 64],
                                            lsl, xsl, start=True, stop=True),
                                   waits=w0)
                        for j in range(4):
                            h = q * 4 + j
                            g2 = h >> 3
                            ti_c = 36 + g2 // 2
                            pr = (g2 % 2) * 64
                            csl = s_xbcr[pr:pr + 64, ti_c * 128:(ti_c + 1) * 128]
                            jj = 8 * (g2 >> 1) + (h % 8)
                            hrow = (g2 % 2) * 64
                            ssl = s_spk[hrow:hrow + 64, jj * 64:(jj + 1) * 64]
                            w1 = bank_wait(ob_) if j == 0 else []
                            cyo = p.emit("tensor", lambda e, csl=csl, ssl=ssl, ob_=ob_, j=j:
                                   e.matmul(BANKS[ob_][:, j * 64:(j + 1) * 64], csl, ssl,
                                            start=True, stop=True),
                                   waits=w1, inc=(("ot", 1) if j == 3 else None))
                        dv_ = s_dout[:, c * 64 + q * 4: c * 64 + (q + 1) * 4]
                        ddv = s_DdT[:, c * 64 + q * 4: c * 64 + (q + 1) * 4]
                        c1_ = p.emit("vector", lambda e, ob_=ob_, dv_=dv_: e.tensor_tensor(
                            s_tmp1[:, 0:256].rearrange("p (j q2) -> p j q2", j=4),
                            BANKS[ob_][:, 0:256].rearrange("p (j q2) -> p j q2", j=4),
                            dv_.unsqueeze(2).broadcast_to([128, 4, 64]), ALU.mult),
                            waits=[("ot", cyo)])
                        yq = (q % 2) * 512
                        c2_ = p.emit("vector", lambda e, yb=yb, yq=yq: e.tensor_add(
                            s_yf[:, yq: yq + 256], s_tmp1[:, 0:256], BANKS[yb][:, 0:256]))
                        bank_rec[yb] = ("ov", c2_)
                        bank_rec[ob_] = ("ov", c2_)
                        c3_ = p.emit("vector", lambda e, q=q, ddv=ddv: e.tensor_tensor(
                            s_tmp1[:, 0:256].rearrange("p (j q2) -> p j q2", j=4),
                            s_xtm[:, q * 256:(q + 1) * 256].rearrange("p (j q2) -> p j q2", j=4),
                            ddv.unsqueeze(2).broadcast_to([128, 4, 64]), ALU.mult))
                        c4_ = p.emit("vector", lambda e, yq=yq: e.tensor_add(
                            s_yf[:, yq: yq + 256], s_yf[:, yq: yq + 256], s_tmp1[:, 0:256]))
                        gb2 = 2 + qb
                        tw2 = bank_wait(gb2)
                        cTg = None
                        for j in range(2):
                            src = s_yf[:, yq + j * 128: yq + (j + 1) * 128]
                            cTg = p.emit("tensor", lambda e, src=src, gb2=gb2, j=j:
                                         e.transpose(BANKS[gb2][:, j * 128:(j + 1) * 128],
                                                     src, s_i128[:]),
                                         waits=((tw2 + [("ov", c4_)]) if j == 0 else []),
                                         inc=(("ot", 1) if j == 1 else None))
                        cvg = p.emit("vector", lambda e, q=q, gb2=gb2: e.tensor_tensor(
                            s_gat[:, q * 256:(q + 1) * 256], BANKS[gb2][:, 0:256],
                            s_zr[:, q * 256:(q + 1) * 256], ALU.mult),
                            waits=[("ot", cTg), ("zbq", czq)])
                        bank_rec[gb2] = ("ov", cvg)
                    z_rel[c] = ("ov", p.cur("ov"))
                if own and BSUB >= 5:

                    wgs = [] if gsp_cnt == 0 else [("gsq", 16 * gsp_cnt)]
                    cg1_ = p.emit("vector", lambda e: e.tensor_copy(s_ghi[:], s_gat[:]),
                                  waits=wgs)
                    cg2_ = p.emit("vector", lambda e: e.scalar_tensor_tensor(
                        s_glo[:], s_ghi[:], -1.0, s_gat[:], ALU.mult, ALU.add))
                    p.emit("sync", lambda e, c=c: e.dma_start(
                        out=gh_v[:, :, (c - 2) * 128:(c - 1) * 128],
                        in_=s_ghi[:].rearrange("p (i t) -> p i t", i=32)),
                        waits=[("ov", cg2_)], inc=("gsq", 16))
                    p.emit("sync", lambda e, c=c: e.dma_start(
                        out=gl_v[:, :, (c - 2) * 128:(c - 1) * 128],
                        in_=s_glo[:].rearrange("p (i t) -> p i t", i=32)),
                        inc=("gsq", 16))
                    gsp_cnt += 2

                for sb in range(4):
                    b = [2, 3, 4, 5][sb] if own else [2, 3, 4, 5][sb]
                    tw = bank_wait(b)
                    cst = None
                    for j8 in range(8):
                        jj = sb * 8 + j8
                        for hh in range(2):
                            h = 16 * (jj >> 3) + (jj % 8) + hh * 8
                            g = h >> 3
                            bsl = s_btm[:, g * 64:(g + 1) * 64]
                            xdsl = s_xd[:, h * 64:(h + 1) * 64]
                            pr = hh * 64
                            lastst = (j8 == 7 and hh == 1)
                            w0 = (tw + [("ov", cxd)]) if (j8 == 0 and hh == 0) else []
                            kw = {"tile_position": (0, 64)} if pr else {}
                            cst = p.emit("tensor", lambda e, bsl=bsl, xdsl=xdsl, b=b, j8=j8,
                                   pr=pr, kw=kw: e.matmul(
                                       BANKS[b][pr:pr + 64, j8 * 64:(j8 + 1) * 64],
                                       bsl, xdsl, start=True, stop=True, **kw),
                                   waits=w0, inc=(("ot", 1) if lastst else None))
                    cvu = None
                    for j8 in range(8):
                        jj = sb * 8 + j8
                        dcol = s_d32[:, c * 32 + jj: c * 32 + jj + 1]
                        cvu = p.emit("vector", lambda e, jj=jj, b=b, j8=j8, dcol=dcol:
                                     e.scalar_tensor_tensor(
                                         s_spk[:, jj * 64:(jj + 1) * 64],
                                         s_spk[:, jj * 64:(jj + 1) * 64], dcol,
                                         BANKS[b][:, j8 * 64:(j8 + 1) * 64],
                                         ALU.mult, ALU.add),
                                     waits=([("ot", cst)] if j8 == 0 else []))
                    bank_rec[b] = ("ov", cvu)
                if own and c not in z_rel:
                    z_rel[c] = ("zbq", czq)
                xb_rel[c] = ("ot", p.cur("ot"))

            p.barrier([("gsq", p.cur("gsq")),
                       ("ov", p.cur("ov")), ("oc", p.cur("oc")), ("ot", p.cur("ot"))])
            stB.close()

        if UPTO >= 3:
            stC = ExitStack()
            s_gh = sbuf("s_gh", [128, 32 * 512], BF16, st=stC)
            s_gl = sbuf("s_gl", [128, 32 * 512], BF16, st=stC)
            s_wof = sbuf("s_wof", [128, 2 * 32 * 128], st=stC)
            s_wohi = sbuf("s_wohi", [128, 32 * 128], BF16, st=stC)
            s_wolo = sbuf("s_wolo", [128, 32 * 128], BF16, st=stC)
            s_xc = sbuf("s_xc", [128, 16 * 512], st=stC)
            s_res = sbuf("s_res", [128, 16 * 512], st=stC)
            s_sq = sbuf("s_sq", [128, 512], st=stC)
            s_st = sbuf("s_st", [1, 4 * 512], st=stC)
            s_mb = sbuf("s_mb", [128, 512], st=stC)
            s_rb = sbuf("s_rb", [128, 512], st=stC)
            s_y1t = sbuf("s_y1t", [128, 512], st=stC)
            s_isq = sbuf("s_isq", [128, 2], st=stC)

            seed_c = nc.inline_tensor(np.array([[2048.0, 8192.0]], np.float32), name="seedc").ap()
            s_seed = sbuf("s_seed", [1, 2], st=stC)
            cq = p.emit("sync", lambda e: e.dma_start(out=s_seed[:], in_=seed_c[:]),
                        inc=("qCc", 16))
            csq = p.emit("scalar", lambda e: e.activation(s_seed[:], s_seed[:], AF.Sqrt),
                         waits=[("qCc", cq)])
            crc = p.emit("vector", lambda e: e.reciprocal(s_seed[:], s_seed[:]),
                         waits=[("oc", csq)])
            cT = p.emit("tensor", lambda e: e.matmul(BANKS[0][:, 0:2], s_onesr[:],
                                                     s_seed[:], start=True, stop=True),
                        waits=bank_wait(0) + [("ov", crc)], inc=("ot", 1))
            cv = p.emit("vector", lambda e: e.tensor_copy(s_isq[:], BANKS[0][:, 0:2]),
                        waits=[("ot", cT)])
            bank_rec[0] = ("ov", cv)

            ghv2 = ghi_sp.rearrange("(i p) t -> p i t", p=128)
            glv2 = glo_sp.rearrange("(i p) t -> p i t", p=128)
            wov = wt_out.rearrange("(k p) e -> p k e", p=128)
            xv2 = x_fm.rearrange("(k p) t -> p k t", p=128)
            y1v = y1_sp.rearrange("(k p) t -> p k t", p=128)
            y1fv = y1f_sc.rearrange("c p t -> p c t")

            wo_rel = {}
            gh_rel = {}
            last_bounce = None
            for tg in range(4):
                t0 = tg * 512
                wg_ = [gh_rel[tg - 1]] if tg >= 1 else []
                cgh = p.emit("sync", lambda e, t0=t0: e.dma_start(
                    out=s_gh[:].rearrange("p (i t) -> p i t", i=32),
                    in_=ghv2[:, :, t0:t0 + 512]), waits=wg_, inc=("ghq", 16))
                cgl = p.emit("sync", lambda e, t0=t0: e.dma_start(
                    out=s_gl[:].rearrange("p (i t) -> p i t", i=32),
                    in_=glv2[:, :, t0:t0 + 512]), inc=("ghq", 16))
                cxc = p.emit("sync", lambda e, t0=t0: e.dma_start(
                    out=s_xc[:].rearrange("p (k t) -> p k t", k=16),
                    in_=xv2[:, :, HALO + t0: HALO + t0 + 512]), inc=("xcq", 16))
                res_done = {}
                for oi in range(16):
                    gwo = tg * 16 + oi
                    wb = gwo % 2
                    wbase = wb * 32 * 128
                    wfw = [wo_rel[gwo - 2]] if gwo >= 2 else []
                    cwf = p.emit("sync", lambda e, oi=oi, wbase=wbase: e.dma_start(
                        out=s_wof[:, wbase:wbase + 32 * 128].rearrange("p (k e) -> p k e", k=32),
                        in_=wov[:, :, oi * 128:(oi + 1) * 128]),
                        waits=wfw, inc=(f"woq{wb}", 16))
                    whw = [("ot", p.cur("ot"))] if gwo >= 2 else []
                    cwh = p.emit("vector", lambda e, wbase=wbase: e.tensor_copy(
                        s_wohi[:], s_wof[:, wbase:wbase + 32 * 128]),
                        waits=[(f"woq{wb}", cwf)] + ([] if gwo < 2 else [wo_mm_rel]))
                    cwl = p.emit("vector", lambda e, wbase=wbase: e.scalar_tensor_tensor(
                        s_wolo[:], s_wohi[:], -1.0, s_wof[:, wbase:wbase + 32 * 128],
                        ALU.mult, ALU.add))
                    wo_rel[gwo] = ("ov", cwl)
                    b = 2 + (gwo % 2)
                    tw = bank_wait(b)
                    first = True
                    cmm = None
                    for k in range(32):
                        for (wsb, gsb) in ((s_wohi, s_gh), (s_wohi, s_gl), (s_wolo, s_gh)):
                            w_ap = wsb[:, k * 128:(k + 1) * 128]
                            g_ap = gsb[:, k * 512:(k + 1) * 512]
                            last = (k == 31 and wsb is s_wolo)
                            wts = (tw + [("ov", cwl), ("ghq", cgl)]) if first else []
                            cmm = p.emit("tensor", lambda e, w=w_ap, g=g_ap, b=b, f=first,
                                         l=last: e.matmul(BANKS[b][:, 0:512], w, g,
                                                          start=f, stop=l),
                                         waits=wts, inc=(("ot", 1) if last else None))
                            first = False
                    wo_mm_rel = ("ot", cmm)
                    cre = p.emit("vector", lambda e, oi=oi, b=b: e.tensor_add(
                        s_res[:, oi * 512:(oi + 1) * 512], BANKS[b][:, 0:512],
                        s_xc[:, oi * 512:(oi + 1) * 512]),
                        waits=[("ot", cmm), ("xcq", cxc)])
                    bank_rec[b] = ("ov", cre)
                    res_done[oi] = cre
                    csqr = p.emit("scalar", lambda e, oi=oi: e.activation(
                        s_sq[:], s_res[:, oi * 512:(oi + 1) * 512], AF.Square),
                        waits=[("ov", cre)])
                    wst = bank_wait(4) if oi == 0 else []
                    p.emit("tensor", lambda e, oi=oi: e.matmul(
                        BANKS[4][0:1, 0:512], s_onescol[:],
                        s_res[:, oi * 512:(oi + 1) * 512],
                        start=(oi == 0), stop=(oi == 15)),
                        waits=(wst + [("ov", cre)]))
                    cst2 = p.emit("tensor", lambda e, oi=oi: e.matmul(
                        BANKS[4][0:1, 512:1024] if False else BANKS[5][0:1, 0:512],
                        s_onescol[:], s_sq[:],
                        start=(oi == 0), stop=(oi == 15)),
                        waits=(bank_wait(5) if oi == 0 else []) + [("oc", csqr)],
                        inc=(("ot", 1) if oi == 15 else None))
                cm1 = p.emit("vector", lambda e: e.tensor_scalar_mul(
                    s_st[0:1, 0:512], BANKS[4][0:1, 0:512], 1.0 / 2048.0),
                    waits=[("ot", cst2)])
                cm2 = p.emit("vector", lambda e: e.tensor_scalar_mul(
                    s_st[0:1, 512:1024], BANKS[5][0:1, 0:512], 1.0 / 2048.0))
                bank_rec[4] = ("ov", cm2)
                bank_rec[5] = ("ov", cm2)
                cm3 = p.emit("vector", lambda e: e.tensor_tensor(
                    s_st[0:1, 1024:1536], s_st[0:1, 0:512], s_st[0:1, 0:512], ALU.mult))
                cm4 = p.emit("vector", lambda e: e.scalar_tensor_tensor(
                    s_st[0:1, 512:1024], s_st[0:1, 512:1024], 1e-5,
                    s_st[0:1, 1024:1536], ALU.add, ALU.subtract))
                cm5 = p.emit("scalar", lambda e: e.activation(
                    s_st[0:1, 1024:1536], s_st[0:1, 512:1024], AF.Sqrt),
                    waits=[("ov", cm4)])
                cm6 = p.emit("vector", lambda e: e.reciprocal(
                    s_st[0:1, 1536:2048], s_st[0:1, 1024:1536]), waits=[("oc", cm5)])
                cb1 = p.emit("tensor", lambda e: e.matmul(
                    BANKS[6][:, 0:512], s_onesr[:], s_st[0:1, 0:512],
                    start=True, stop=True), waits=bank_wait(6) + [("ov", cm6)],
                    inc=("ot", 1))
                cb1v = p.emit("vector", lambda e: e.tensor_copy(s_mb[:], BANKS[6][:, 0:512]),
                              waits=[("ot", cb1)])
                bank_rec[6] = ("ov", cb1v)
                cb2 = p.emit("tensor", lambda e: e.matmul(
                    BANKS[7][:, 0:512], s_onesr[:], s_st[0:1, 1536:2048],
                    start=True, stop=True), waits=bank_wait(7) + [("ov", cm6)],
                    inc=("ot", 1))
                cb2v = p.emit("vector", lambda e: e.tensor_copy(s_rb[:], BANKS[7][:, 0:512]),
                              waits=[("ot", cb2)])
                bank_rec[7] = ("ov", cb2v)
                for oi in range(16):
                    wy1 = [("ov", cb2v)] + ([("y1q", last_bounce)] if last_bounce else [])
                    cy1 = p.emit("vector", lambda e, oi=oi: e.tensor_sub(
                        s_y1t[:], s_res[:, oi * 512:(oi + 1) * 512], s_mb[:]),
                        waits=wy1)
                    cy2 = p.emit("vector", lambda e: e.tensor_mul(
                        s_y1t[:], s_y1t[:], s_rb[:]))
                    cspl = p.emit("sync", lambda e, oi=oi, t0=t0: e.dma_start(
                        out=y1v[:, oi, t0:t0 + 512], in_=s_y1t[:]),
                        waits=[("ov", cy2)], inc=("y1q", 16))
                    b2 = 2 + (oi % 2)
                    cfa = p.emit("tensor", lambda e, b2=b2: e.matmul(
                        BANKS[b2][:, 0:512], s_h128[:], s_y1t[:], start=True, stop=True),
                        waits=bank_wait(b2) + [("ov", cy2)], inc=("ot", 1))
                    cfv = p.emit("vector", lambda e, b2=b2: e.tensor_copy(
                        s_y1t[:], BANKS[b2][:, 0:512]), waits=[("ot", cfa), ("y1q", cspl)])
                    bank_rec[b2] = ("ov", cfv)
                    last_bounce = p.emit("sync", lambda e, oi=oi, t0=t0: e.dma_start(
                        out=y1fv[:, oi, t0:t0 + 512], in_=s_y1t[:]),
                        waits=[("ov", cfv)], inc=("y1q", 16))
                gh_rel[tg] = ("ot", p.cur("ot"))

            p.barrier([("y1q", p.cur("y1q")), ("ov", p.cur("ov")), ("oc", p.cur("oc")),
                       ("ot", p.cur("ot"))])
            stC.close()

        if UPTO >= 4:
            s_rw = sbuf("s_rw", [128, 4])
            s_mwc = sbuf("s_mwc", [128, 4])
            stD0 = ExitStack()
            s_wsl = sbuf("s_wsl", [128, 2 * 8192], st=stD0)
            s_wac = sbuf("s_wac", [128, 100], st=stD0)
            s_rwr = sbuf("s_rwr", [1, 8], st=stD0)
            wgv = wg_p.rearrange("(k p) e -> p k e", p=128)
            wuv = wu_p.rearrange("(k p) e -> p k e", p=128)
            wdv = wd_p.rearrange("(b p) e -> p b e", p=128)
            red_rec = {}
            gsl = 0
            for wi, (wv_, nslab, fl, acc0) in enumerate([(wgv, 16, 8192, 0),
                                                         (wuv, 16, 8192, 16),
                                                         (wdv, 64, 2048, 32)]):
                for sl_ in range(nslab):
                    bb = gsl % 2
                    wfw = [red_rec[gsl - 2]] if gsl >= 2 else []
                    cwd = p.emit("sync", lambda e, wv_=wv_, sl_=sl_, bb=bb, fl=fl:
                                 e.dma_start(out=s_wsl[:, bb * 8192:bb * 8192 + fl],
                                             in_=wv_[:, sl_, :]),
                                 waits=wfw, inc=(f"wmq{bb}", 16))
                    col = acc0 + (sl_ if wi < 2 else sl_)
                    cgr = p.emit("vector", lambda e, bb=bb, fl=fl, col=col:
                                 e.reduce_sum(s_wac[:, col:col + 1].unsqueeze(2),
                                              s_wsl[:, bb * 8192:bb * 8192 + fl].unsqueeze(1),
                                              AX.X, apply_absolute_value=True),
                                 waits=[(f"wmq{bb}", cwd)])
                    red_rec[gsl] = ("ov", cgr)
                    gsl += 1
                nin = nslab
                cfold = p.emit("vector", lambda e, acc0=acc0, nin=nin, wi=wi: e.reduce_sum(
                    s_wac[:, 96 + wi:97 + wi].unsqueeze(2),
                    s_wac[:, acc0:acc0 + nin].unsqueeze(1), AX.X),
                    waits=[red_rec[gsl - 1]])
                cT = p.emit("tensor", lambda e, wi=wi: e.matmul(
                    BANKS[0][0:1, wi:wi + 1], s_onescol[:], s_wac[:, 96 + wi:97 + wi],
                    start=True, stop=True),
                    waits=bank_wait(0) + [("ov", cfold)], inc=("ot", 1))
                cmn = p.emit("vector", lambda e, wi=wi: e.tensor_scalar(
                    s_rwr[0:1, wi:wi + 1], BANKS[0][0:1, wi:wi + 1],
                    1.0 / 16777216.0, 1e-5, ALU.mult, ALU.max),
                    waits=[("ot", cT)])
                bank_rec[0] = ("ov", cmn)
                crc = p.emit("vector", lambda e, wi=wi: e.reciprocal(
                    s_rwr[0:1, 4 + wi:5 + wi], s_rwr[0:1, wi:wi + 1]))
            cT = p.emit("tensor", lambda e: e.matmul(
                BANKS[1][:, 0:3], s_onesr[:], s_rwr[0:1, 0:3], start=True, stop=True),
                waits=bank_wait(1) + [("ov", p.cur("ov"))], inc=("ot", 1))
            cT2 = p.emit("tensor", lambda e: e.matmul(
                BANKS[1][:, 4:7], s_onesr[:], s_rwr[0:1, 4:7], start=True, stop=True),
                inc=("ot", 1))
            cv = p.emit("vector", lambda e: e.tensor_copy(s_mwc[:, 0:3], BANKS[1][:, 0:3]),
                        waits=[("ot", cT2)])
            cv2 = p.emit("vector", lambda e: e.tensor_copy(s_rw[:, 0:3], BANKS[1][:, 4:7]))
            bank_rec[1] = ("ov", cv2)
            p.barrier([("ov", p.cur("ov")), ("ot", p.cur("ot")), ("og", p.cur("og"))])
            stD0.close()

        if UPTO >= 5:
            stD1 = ExitStack()
            s_q1 = sbuf("s_q1", [128, 16 * 2048], BF16, st=stD1)
            s_mx = sbuf("s_mx", [128, 2048], st=stD1)
            s_row1 = sbuf("s_row1", [128, 2048], st=stD1)
            s_fg = sbuf("s_fg", [128, 2048], st=stD1)
            s_fu = sbuf("s_fu", [128, 2048], st=stD1)
            s_shin = sbuf("s_shin", [128, 2 * 512], st=stD1)
            s_abs1 = sbuf("s_abs1", [128, 512], st=stD1)
            s_isqd = sbuf("s_isqd", [128, 2], st=stD1)
            s_wgf = sbuf("s_wgf", [128, 16 * 512], st=stD1)
            s_wgt = sbuf("s_wgt", [128, 16 * 512], BF16, st=stD1)
            s_wut = sbuf("s_wut", [128, 16 * 512], BF16, st=stD1)
            s_gv = sbuf("s_gv", [128, 512], st=stD1)
            s_sg = sbuf("s_sg", [128, 512], st=stD1)
            s_uv = sbuf("s_uv", [128, 512], st=stD1)
            s_mid = sbuf("s_mid", [128, 512], st=stD1)
            s_sam4 = sbuf("s_sam4", [128, 4 * 512], st=stD1)


            def cross_part_max_to_rows(src_mx, dst_row, ntile, mxc_buf, mxT_buf, s_oh16):
                recs = []
                for j in range(ntile):
                    bk2 = 2 + j % 2
                    cT_ = p.emit("tensor", lambda e, j=j, bk2=bk2: e.transpose(
                        BANKS[bk2][:, 0:128], src_mx[:, j * 128:(j + 1) * 128], s_i128[:]),
                        waits=bank_wait(bk2) + ([("ov", p.cur("ov")),
                                                 ("oc", p.cur("oc"))] if j == 0 else []),
                        inc=("ot", 1))
                    cr_ = p.emit("vector", lambda e, j=j, bk2=bk2: e.reduce_max(
                        mxc_buf[:, j:j + 1].unsqueeze(2),
                        BANKS[bk2][:, 0:128].unsqueeze(1), AX.X),
                        waits=[("ot", cT_)])
                    bank_rec[bk2] = ("ov", cr_)
                cT2_ = p.emit("tensor", lambda e: e.transpose(
                    BANKS[4][0:ntile, 0:128], mxc_buf[:, 0:ntile], s_i128[:]),
                    waits=bank_wait(4) + [("ov", p.cur("ov"))], inc=("ot", 1))
                cv2_ = p.emit("vector", lambda e: e.tensor_copy(
                    mxT_buf[0:ntile, 0:128], BANKS[4][0:ntile, 0:128]),
                    waits=[("ot", cT2_)])
                bank_rec[4] = ("ov", cv2_)
                for j in range(ntile):
                    bk2 = 5 + j % 2
                    cT3_ = p.emit("tensor", lambda e, j=j, bk2=bk2: e.matmul(
                        BANKS[bk2][:, 0:128], s_oh16[0:ntile, j * 128:(j + 1) * 128],
                        mxT_buf[0:ntile, 0:128], start=True, stop=True),
                        waits=bank_wait(bk2) + ([("ov", cv2_)] if j <= 1 else []),
                        inc=("ot", 1))
                    cv3_ = p.emit("vector", lambda e, j=j, bk2=bk2: e.tensor_copy(
                        dst_row[:, j * 128:(j + 1) * 128], BANKS[bk2][:, 0:128]),
                        waits=[("ot", cT3_)])
                    bank_rec[bk2] = ("ov", cv3_)
                return p.cur("ov")

            seed2_c = nc.inline_tensor(np.array([[2048.0, 8192.0]], np.float32), name="seed2c").ap()
            s_sd2 = sbuf("s_sd2", [1, 2], st=stD1)
            cq = p.emit("sync", lambda e: e.dma_start(out=s_sd2[:], in_=seed2_c[:]),
                        inc=("qDc", 16))
            csq = p.emit("scalar", lambda e: e.activation(s_sd2[:], s_sd2[:], AF.Sqrt),
                         waits=[("qDc", cq)])
            crc = p.emit("vector", lambda e: e.reciprocal(s_sd2[:], s_sd2[:]),
                         waits=[("oc", csq)])
            cT = p.emit("tensor", lambda e: e.matmul(BANKS[0][:, 0:2], s_onesr[:],
                                                     s_sd2[:], start=True, stop=True),
                        waits=bank_wait(0) + [("ov", crc)], inc=("ot", 1))
            cv = p.emit("vector", lambda e: e.tensor_copy(s_isqd[:], BANKS[0][:, 0:2]),
                        waits=[("ot", cT)])
            bank_rec[0] = ("ov", cv)

            y1f_shv = y1f_sc.rearrange("c (a b) t -> (c a) (b t)", b=16)
            shin_carry = {}
            def sweep1_q1(tag, rnd):
                shin_rel = dict(shin_carry)
                prev_mx = None
                for m in range(64):
                    slot = m % 2
                    bk = 2 + m % 2
                    wsh = [shin_rel[m - 2]] if (m - 2) in shin_rel else []
                    cin = p.emit("sync", lambda e, m=m, slot=slot: e.dma_start(
                        out=s_shin[:, slot * 512:(slot + 1) * 512],
                        in_=y1f_shv[:, m * 512:(m + 1) * 512]),
                        waits=wsh, inc=(f"s1q{tag}{slot}", 16))
                    cmm = p.emit("tensor", lambda e, slot=slot, bk=bk: e.matmul(
                        BANKS[bk][:, 0:512], s_m1[:], s_shin[:, slot * 512:(slot + 1) * 512],
                        start=True, stop=True),
                        waits=bank_wait(bk) + [(f"s1q{tag}{slot}", cin)], inc=("ot", 1))
                    tb = m % 4
                    bblk = m // 4
                    if not rnd:
                        if bblk == 0:
                            cev = p.emit("scalar", lambda e, bk=bk, tb=tb: e.activation(
                                s_mx[:, tb * 512:(tb + 1) * 512], BANKS[bk][:, 0:512],
                                AF.Abs), waits=[("ot", cmm)])
                            cev = ("oc", cev)
                        else:
                            cab = p.emit("scalar", lambda e, bk=bk: e.activation(
                                s_abs1[:], BANKS[bk][:, 0:512], AF.Abs),
                                waits=[("ot", cmm)] + ([prev_mx] if prev_mx else []))
                            cev = p.emit("vector", lambda e, tb=tb: e.tensor_max(
                                s_mx[:, tb * 512:(tb + 1) * 512],
                                s_mx[:, tb * 512:(tb + 1) * 512], s_abs1[:]),
                                waits=[("oc", cab)])
                            cev = ("ov", cev)
                            prev_mx = cev
                    else:
                        cev = p.emit("vector", lambda e, bk=bk, tb=tb, m=m: e._custom_dve(
                            RND_TT, out=s_q1[:, m * 512:(m + 1) * 512],
                            in0=BANKS[bk][:, 0:512],
                            in1=s_row1[:, tb * 512:(tb + 1) * 512], s0=MAGIC),
                            waits=[("ot", cmm)])
                        cev = ("ov", cev)
                    bank_rec[bk] = cev
                    shin_rel[m] = ("ot", cmm)
                shin_carry.clear()
                shin_carry[-2] = shin_rel[62]
                shin_carry[-1] = shin_rel[63]
            sweep1_q1("a", False)
            s_mxc1 = sbuf("s_mxc1", [128, 16], st=stD1)
            s_mxT1 = sbuf("s_mxT1", [16, 128], st=stD1)
            s_oh16a = sbuf("s_oh16a", [16, 16 * 128], st=stD1)
            coh = p.emit("sync", lambda e: e.dma_start(out=s_oh16a[:], in_=oh16_c[:]),
                         inc=("qFc", 16))
            p.barrier([("qFc", p.cur("qFc"))])
            cpm = cross_part_max_to_rows(s_mx, s_row1, 16, s_mxc1, s_mxT1, s_oh16a)
            ca = p.emit("vector", lambda e: e.tensor_scalar(
                s_row1[:], s_row1[:], s_isqd[:, 0:1], 1e-5, ALU.mult, ALU.max),
                waits=[("ov", cpm)])
            cfg = p.emit("vector", lambda e: e.tensor_scalar(
                s_fg[:], s_row1[:], s_mwc[:, 0:1], 1.0 / 127.0, ALU.mult, ALU.mult))
            cfu = p.emit("vector", lambda e: e.tensor_scalar(
                s_fu[:], s_row1[:], s_mwc[:, 1:2], 1.0 / 127.0, ALU.mult, ALU.mult))
            crr = p.emit("vector", lambda e: e.reciprocal(s_row1[:], s_row1[:]))
            cs1 = p.emit("vector", lambda e: e.tensor_scalar(
                s_row1[:], s_row1[:], s_isqd[:, 0:1], 127.0, ALU.mult, ALU.mult))
            sweep1_q1("b", True)
            q1_done = p.cur("ov")

            samv = sam_sc.rearrange("c p t -> p c t")
            tern_rel = None
            mm_rel_gu = {}
            for os_ in range(16):
                wts = [tern_rel] if tern_rel else []
                cda = p.emit("sync", lambda e, os_=os_: e.dma_start(
                    out=s_wgf[:].rearrange("p (k e) -> p k e", k=16),
                    in_=wgv[:, :, os_ * 512:(os_ + 1) * 512]),
                    waits=wts + ([mm_rel_gu[os_ - 1]] if os_ >= 1 else []),
                    inc=("wgq", 16))
                cwgq = p.cur("wgq")
                ctg = p.emit("vector", lambda e: e._custom_dve(
                    TERN_TS, out=s_wgt[:], in0=s_wgf[:], s0=MAGIC, s1=s_rw[:, 0:1],
                    imm2=1.0), waits=[("wgq", cwgq), ("ov", q1_done)])
                cdu = p.emit("sync", lambda e, os_=os_: e.dma_start(
                    out=s_wgf[:].rearrange("p (k e) -> p k e", k=16),
                    in_=wuv[:, :, os_ * 512:(os_ + 1) * 512]),
                    waits=[("ov", ctg)], inc=("wgq", 16))
                cwuq = p.cur("wgq")
                ctu = p.emit("vector", lambda e: e._custom_dve(
                    TERN_TS, out=s_wut[:], in0=s_wgf[:], s0=MAGIC, s1=s_rw[:, 1:2],
                    imm2=1.0), waits=[("wgq", cwuq)])
                tern_rel = ("ov", ctu)
                for tb in range(4):
                    for osub in range(4):
                        it = tb * 4 + osub
                        bg = 4 + it % 2
                        bu = 6 + it % 2
                        cmg = cmu = None
                        for b16 in range(16):
                            wg_ = (bank_wait(bg) + [("ov", ctu)]) if b16 == 0 else []
                            cmg = p.emit("tensor", lambda e, b16=b16, osub=osub, tb=tb,
                                         bg=bg: e.matmul(
                                             BANKS[bg][:, 0:512],
                                             s_wgt[:, b16 * 512 + osub * 128: b16 * 512 + (osub + 1) * 128],
                                             s_q1[:, b16 * 2048 + tb * 512: b16 * 2048 + (tb + 1) * 512],
                                             start=(b16 == 0), stop=(b16 == 15)),
                                         waits=wg_, inc=(("ot", 1) if b16 == 15 else None))
                        for b16 in range(16):
                            wu_ = bank_wait(bu) if b16 == 0 else []
                            cmu = p.emit("tensor", lambda e, b16=b16, osub=osub, tb=tb,
                                         bu=bu: e.matmul(
                                             BANKS[bu][:, 0:512],
                                             s_wut[:, b16 * 512 + osub * 128: b16 * 512 + (osub + 1) * 128],
                                             s_q1[:, b16 * 2048 + tb * 512: b16 * 2048 + (tb + 1) * 512],
                                             start=(b16 == 0), stop=(b16 == 15)),
                                         waits=wu_, inc=(("ot", 1) if b16 == 15 else None))
                        cg_ = p.emit("vector", lambda e, bg=bg, tb=tb: e.tensor_tensor(
                            s_gv[:], BANKS[bg][:, 0:512], s_fg[:, tb * 512:(tb + 1) * 512],
                            ALU.mult), waits=[("ot", cmu)])
                        bank_rec[bg] = ("ov", cg_)
                        csg = p.emit("scalar", lambda e: e.activation(
                            s_sg[:], s_gv[:], AF.Silu), waits=[("ov", cg_)])
                        cu_ = p.emit("vector", lambda e, bu=bu, tb=tb: e.tensor_tensor(
                            s_uv[:], BANKS[bu][:, 0:512], s_fu[:, tb * 512:(tb + 1) * 512],
                            ALU.mult))
                        bank_rec[bu] = ("ov", cu_)
                        cmid = p.emit("vector", lambda e: e.tensor_mul(
                            s_mid[:], s_sg[:], s_uv[:]), waits=[("oc", csg)])
                        bA = 2 + it % 2
                        cfa = p.emit("tensor", lambda e, bA=bA: e.matmul(
                            BANKS[bA][:, 0:512], s_h128[:], s_mid[:], start=True, stop=True),
                            waits=bank_wait(bA) + [("ov", cmid)], inc=("ot", 1))
                        cfv = p.emit("vector", lambda e, bA=bA, osub=osub: e.tensor_copy(
                            s_sam4[:, osub * 512:(osub + 1) * 512], BANKS[bA][:, 0:512]),
                            waits=[("ot", cfa)] + ([("smq", smq_last)] if os_ + tb > 0 else []))
                        bank_rec[bA] = ("ov", cfv)
                    smq_last = p.emit("sync", lambda e, os_=os_, tb=tb: e.dma_start(
                        out=samv[:, os_ * 4:(os_ + 1) * 4, tb * 512:(tb + 1) * 512],
                        in_=s_sam4[:].rearrange("p (c t) -> p c t", c=4)),
                        waits=[("ov", p.cur("ov"))], inc=("smq", 16))
                mm_rel_gu[os_] = ("ot", p.cur("ot"))
            s_wdf = s_wgf
            s_wdt = s_wgt
            wdtv = wdt_sp.rearrange("(b p) e -> p b e", p=128)
            wd_rel = {}
            for b2 in range(64):
                slot = b2 % 2
                wts = [wd_rel[b2 - 2]] if b2 >= 2 else (
                    [tern_rel, mm_rel_gu[15]] if b2 == 0 else [])
                cdd = p.emit("sync", lambda e, b2=b2, slot=slot: e.dma_start(
                    out=s_wdf[:, slot * 2048:(slot + 1) * 2048], in_=wdv[:, b2, :]),
                    waits=wts, inc=(f"wdq{slot}", 16))
                ctd = p.emit("vector", lambda e, slot=slot: e._custom_dve(
                    TERN_TS, out=s_wdt[:, slot * 2048:(slot + 1) * 2048],
                    in0=s_wdf[:, slot * 2048:(slot + 1) * 2048], s0=MAGIC,
                    s1=s_rw[:, 2:3], imm2=1.0), waits=[(f"wdq{slot}", cdd)])
                cds = p.emit("sync", lambda e, b2=b2, slot=slot: e.dma_start(
                    out=wdtv[:, b2, :], in_=s_wdt[:, slot * 2048:(slot + 1) * 2048]),
                    waits=[("ov", ctd)], inc=(f"wdsq{slot}", 16))
                wd_rel[b2] = (f"wdsq{slot}", cds)
            p.barrier([("smq", p.cur("smq")), ("wdsq0", p.cur("wdsq0")),
                       ("wdsq1", p.cur("wdsq1")), ("ov", p.cur("ov")),
                       ("oc", p.cur("oc")), ("ot", p.cur("ot"))])
            stD1.close()

        if UPTO >= 6:
            stD2 = ExitStack()
            s_q2 = sbuf("s_q2", [128, 64 * 512], BF16, st=stD2)
            s_mx2 = sbuf("s_mx2", [128, 512], st=stD2)
            s_row2 = sbuf("s_row2", [128, 512], st=stD2)
            s_fd = sbuf("s_fd", [128, 512], st=stD2)
            s_shin2 = sbuf("s_shin2", [128, 1 * 512], st=stD2)
            s_mxc2 = sbuf("s_mxc2", [128, 16], st=stD2)
            s_mxT2 = sbuf("s_mxT2", [16, 128], st=stD2)
            s_oh16b = sbuf("s_oh16b", [16, 16 * 128], st=stD2)
            s_wdts = sbuf("s_wdts", [128, 2 * 64 * 128], BF16, st=stD2)
            s_y1c = sbuf("s_y1c", [128, 16 * 512], st=stD2)
            s_res2 = sbuf("s_res2", [128, 16 * 512], st=stD2)
            s_sq2 = sbuf("s_sq2", [128, 512], st=stD2)
            s_st2 = sbuf("s_st2", [1, 4 * 512], st=stD2)
            s_mb2 = sbuf("s_mb2", [128, 512], st=stD2)
            s_rb2 = sbuf("s_rb2", [128, 512], st=stD2)
            s_ot = sbuf("s_ot", [128, 512], st=stD2)
            s_abs2 = s_ot
            s_isq2 = sbuf("s_isq2", [128, 2], st=stD2)

            seed3_c = nc.inline_tensor(np.array([[2048.0, 8192.0]], np.float32), name="seed3c").ap()
            s_sd3 = sbuf("s_sd3", [1, 2], st=stD2)
            coh2 = p.emit("sync", lambda e: e.dma_start(out=s_oh16b[:], in_=oh16_c[:]),
                          inc=("qGc", 16))
            p.barrier([("qGc", p.cur("qGc"))])
            cq = p.emit("sync", lambda e: e.dma_start(out=s_sd3[:], in_=seed3_c[:]),
                        inc=("qEc", 16))
            csq = p.emit("scalar", lambda e: e.activation(s_sd3[:], s_sd3[:], AF.Sqrt),
                         waits=[("qEc", cq)])
            crc = p.emit("vector", lambda e: e.reciprocal(s_sd3[:], s_sd3[:]),
                         waits=[("oc", csq)])
            cT = p.emit("tensor", lambda e: e.matmul(BANKS[0][:, 0:2], s_onesr[:],
                                                     s_sd3[:], start=True, stop=True),
                        waits=bank_wait(0) + [("ov", crc)], inc=("ot", 1))
            cv = p.emit("vector", lambda e: e.tensor_copy(s_isq2[:], BANKS[0][:, 0:2]),
                        waits=[("ot", cT)])
            bank_rec[0] = ("ov", cv)

            last_out = None
            sh2_carry = {}
            sam_shv = sam_sc.rearrange("c (a b) t -> (c a) (b t)", b=64)
            y1v2 = y1_sp.rearrange("(k p) t -> p k t", p=128)
            ofv = out_fm.rearrange("(k p) t -> p k t", p=128)
            for qt in range(4):
                q0 = qt * 512
                prev_mx2 = None
                for rnd in (False, True):
                    tag = "r" if rnd else "m"
                    sh_rel = dict(sh2_carry)
                    for b2 in range(64):
                        slot = 0
                        bk = 2 + b2 % 2
                        wsh = [sh_rel[b2 - 1]] if (b2 - 1) in sh_rel else []
                        cin = p.emit("sync", lambda e, b2=b2, slot=slot, q0=q0: e.dma_start(
                            out=s_shin2[:, slot * 512:(slot + 1) * 512],
                            in_=sam_shv[:, b2 * TOWN + q0: b2 * TOWN + q0 + 512]),
                            waits=wsh, inc=(f"s2q{tag}{qt}{slot}", 16))
                        cmm = p.emit("tensor", lambda e, slot=slot, bk=bk: e.matmul(
                            BANKS[bk][:, 0:512], s_m2[:],
                            s_shin2[:, slot * 512:(slot + 1) * 512],
                            start=True, stop=True),
                            waits=bank_wait(bk) + [(f"s2q{tag}{qt}{slot}", cin)],
                            inc=("ot", 1))
                        if not rnd:
                            if b2 == 0:
                                cev = p.emit("scalar", lambda e, bk=bk: e.activation(
                                    s_mx2[:], BANKS[bk][:, 0:512], AF.Abs),
                                    waits=[("ot", cmm)])
                                cev = ("oc", cev)
                            else:
                                wab = [("ot", cmm)]
                                if prev_mx2:
                                    wab.append(prev_mx2)
                                elif last_out:
                                    wab.append(("qO", last_out))
                                cab = p.emit("scalar", lambda e, bk=bk: e.activation(
                                    s_abs2[:], BANKS[bk][:, 0:512], AF.Abs),
                                    waits=wab)
                                cev = p.emit("vector", lambda e: e.tensor_max(
                                    s_mx2[:], s_mx2[:], s_abs2[:]),
                                    waits=[("oc", cab)])
                                cev = ("ov", cev)
                                prev_mx2 = cev
                        else:
                            cev = p.emit("vector", lambda e, bk=bk, b2=b2: e._custom_dve(
                                RND_TT, out=s_q2[:, b2 * 512:(b2 + 1) * 512],
                                in0=BANKS[bk][:, 0:512], in1=s_row2[:], s0=MAGIC),
                                waits=[("ot", cmm)])
                            cev = ("ov", cev)
                        bank_rec[bk] = cev
                        sh_rel[b2] = ("ot", cmm)
                    sh2_carry.clear()
                    sh2_carry[-1] = sh_rel[63]
                    if not rnd:
                        cpm2 = cross_part_max_to_rows(s_mx2, s_row2, 4, s_mxc2, s_mxT2, s_oh16b)
                        ca = p.emit("vector", lambda e: e.tensor_scalar(
                            s_row2[:], s_row2[:], s_isq2[:, 1:2], 1e-5, ALU.mult, ALU.max),
                            waits=[("ov", cpm2)])
                        cfd = p.emit("vector", lambda e: e.tensor_scalar(
                            s_fd[:], s_row2[:], s_mwc[:, 2:3], 1.0 / 127.0, ALU.mult,
                            ALU.mult))
                        crr = p.emit("vector", lambda e: e.reciprocal(s_row2[:], s_row2[:]))
                        cs2 = p.emit("vector", lambda e: e.tensor_scalar(
                            s_row2[:], s_row2[:], s_isq2[:, 1:2], 127.0, ALU.mult, ALU.mult))
                q2_done = p.cur("ov")
                cy1q = p.emit("sync", lambda e, q0=q0: e.dma_start(
                    out=s_y1c[:].rearrange("p (k t) -> p k t", k=16),
                    in_=y1v2[:, :, q0:q0 + 512]), inc=(f"yq2{qt}", 16))
                wslab_rel = {}
                for oi in range(16):
                    slot = oi % 2
                    wts = [wslab_rel[oi - 2]] if oi >= 2 else []
                    cwd2 = p.emit("sync", lambda e, oi=oi, slot=slot: e.dma_start(
                        out=s_wdts[:, slot * 8192:(slot + 1) * 8192]
                        .rearrange("p (b e) -> p b e", b=64),
                        in_=wdtv[:, :, oi * 128:(oi + 1) * 128]),
                        waits=wts, inc=(f"wd2{qt}{slot}", 16))
                    bk = 4 + oi % 2
                    cmm = None
                    for b2 in range(64):
                        wmm = (bank_wait(bk) + [(f"wd2{qt}{slot}", cwd2),
                                                ("ov", q2_done)]) if b2 == 0 else []
                        cmm = p.emit("tensor", lambda e, b2=b2, bk=bk, slot=slot: e.matmul(
                            BANKS[bk][:, 0:512],
                            s_wdts[:, slot * 8192 + b2 * 128: slot * 8192 + (b2 + 1) * 128],
                            s_q2[:, b2 * 512:(b2 + 1) * 512],
                            start=(b2 == 0), stop=(b2 == 63)),
                            waits=wmm, inc=(("ot", 1) if b2 == 63 else None))
                    wslab_rel[oi] = ("ot", cmm)
                    cr1 = p.emit("vector", lambda e, bk=bk: e.tensor_mul(
                        s_ot[:], BANKS[bk][:, 0:512], s_fd[:]),
                        waits=[("ot", cmm), (f"yq2{qt}", cy1q)] +
                              ([("qO", last_out)] if last_out else []))
                    cr2 = p.emit("vector", lambda e, oi=oi: e.tensor_add(
                        s_res2[:, oi * 512:(oi + 1) * 512], s_ot[:],
                        s_y1c[:, oi * 512:(oi + 1) * 512]))
                    bank_rec[bk] = ("ov", cr2)
                    csqr = p.emit("scalar", lambda e, oi=oi: e.activation(
                        s_sq2[:], s_res2[:, oi * 512:(oi + 1) * 512], AF.Square),
                        waits=[("ov", cr2)])
                    p.emit("tensor", lambda e, oi=oi: e.matmul(
                        BANKS[6][0:1, 0:512], s_onescol[:],
                        s_res2[:, oi * 512:(oi + 1) * 512],
                        start=(oi == 0), stop=(oi == 15)),
                        waits=(bank_wait(6) if oi == 0 else []) + [("ov", cr2)])
                    cst2 = p.emit("tensor", lambda e, oi=oi: e.matmul(
                        BANKS[7][0:1, 0:512], s_onescol[:], s_sq2[:],
                        start=(oi == 0), stop=(oi == 15)),
                        waits=(bank_wait(7) if oi == 0 else []) + [("oc", csqr)],
                        inc=(("ot", 1) if oi == 15 else None))
                cm1 = p.emit("vector", lambda e: e.tensor_scalar_mul(
                    s_st2[0:1, 0:512], BANKS[6][0:1, 0:512], 1.0 / 2048.0),
                    waits=[("ot", cst2)])
                cm2 = p.emit("vector", lambda e: e.tensor_scalar_mul(
                    s_st2[0:1, 512:1024], BANKS[7][0:1, 0:512], 1.0 / 2048.0))
                bank_rec[6] = ("ov", cm2)
                bank_rec[7] = ("ov", cm2)
                cm3 = p.emit("vector", lambda e: e.tensor_tensor(
                    s_st2[0:1, 1024:1536], s_st2[0:1, 0:512], s_st2[0:1, 0:512], ALU.mult))
                cm4 = p.emit("vector", lambda e: e.scalar_tensor_tensor(
                    s_st2[0:1, 512:1024], s_st2[0:1, 512:1024], 1e-5,
                    s_st2[0:1, 1024:1536], ALU.add, ALU.subtract))
                cm5 = p.emit("scalar", lambda e: e.activation(
                    s_st2[0:1, 1024:1536], s_st2[0:1, 512:1024], AF.Sqrt),
                    waits=[("ov", cm4)])
                cm6 = p.emit("vector", lambda e: e.reciprocal(
                    s_st2[0:1, 1536:2048], s_st2[0:1, 1024:1536]), waits=[("oc", cm5)])
                cb1 = p.emit("tensor", lambda e: e.matmul(
                    BANKS[2][:, 0:512], s_onesr[:], s_st2[0:1, 0:512],
                    start=True, stop=True), waits=bank_wait(2) + [("ov", cm6)],
                    inc=("ot", 1))
                cb1v = p.emit("vector", lambda e: e.tensor_copy(s_mb2[:], BANKS[2][:, 0:512]),
                              waits=[("ot", cb1)])
                bank_rec[2] = ("ov", cb1v)
                cb2 = p.emit("tensor", lambda e: e.matmul(
                    BANKS[3][:, 0:512], s_onesr[:], s_st2[0:1, 1536:2048],
                    start=True, stop=True), waits=bank_wait(3) + [("ov", cm6)],
                    inc=("ot", 1))
                cb2v = p.emit("vector", lambda e: e.tensor_copy(s_rb2[:], BANKS[3][:, 0:512]),
                              waits=[("ot", cb2)])
                bank_rec[3] = ("ov", cb2v)
                for oi in range(16):
                    co1 = p.emit("vector", lambda e, oi=oi: e.tensor_sub(
                        s_ot[:], s_res2[:, oi * 512:(oi + 1) * 512], s_mb2[:]),
                        waits=[("ov", cb2v)] + ([("qO", last_out)] if last_out else []))
                    co2 = p.emit("vector", lambda e: e.tensor_mul(s_ot[:], s_ot[:], s_rb2[:]))
                    last_out = p.emit("sync", lambda e, oi=oi, q0=q0: e.dma_start(
                        out=ofv[:, oi, q0:q0 + 512], in_=s_ot[:]),
                        waits=[("ov", co2)], inc=("qO", 16))
            stD2.close()

        s_dbg = sbuf("s_dbg", [128, 2 * TT])
        cdbg = p.emit("sync", lambda e: e.dma_start(
            out=s_dbg[:, 0:TOWN], in_=y1_sp[0:128, :]), inc=("qD1", 16))
        cdbg2 = p.emit("sync", lambda e: e.dma_start(
            out=s_dbg[:, TT:TT + TOWN], in_=y1f_sc[0, :, :]), inc=("qD2", 16))
        p.emit("sync", lambda e: e.dma_start(out=dbg[:, 0:TOWN],
               in_=s_dbg[:, 0:TOWN]),
               waits=[("qD1", cdbg), ("qD2", cdbg2)], inc=("qD3", 16))
        p.emit("sync", lambda e: e.dma_start(out=dbg[:, TT:TT + TOWN],
               in_=s_dbg[:, TT:TT + TOWN]), inc=("qD3", 16))
        p.build_block(block, final_waits=[(n, p.cur(n)) for n in list(p.counts.keys())])
    return nc


def _shard_inputs(L0_attn_input_, L0_attn_in_proj_w, L0_attn_conv1d_weight,
                  L0_attn_conv1d_bias, L0_attn_z_bias, L0_attn_D, L0_attn_out_proj_w,
                  L0_ffn_w_gate, L0_ffn_w_up, L0_ffn_w_down):
    X = np.ascontiguousarray(np.asarray(L0_attn_input_, dtype=np.float32)[0])
    wt_in = np.ascontiguousarray(np.asarray(L0_attn_in_proj_w, np.float32).T)
    convw = np.ascontiguousarray(
        np.asarray(L0_attn_conv1d_weight, np.float32).reshape(CONVD, 4)
        .reshape(40, 128, 4).transpose(1, 0, 2).reshape(128, 160))
    convb = np.ascontiguousarray(
        np.asarray(L0_attn_conv1d_bias, np.float32).reshape(40, 128).T)
    zbias = np.ascontiguousarray(np.asarray(L0_attn_z_bias, np.float32).reshape(32, 128).T)
    dv = np.asarray(L0_attn_D, np.float32).reshape(1, 64)
    wt_out = np.ascontiguousarray(np.asarray(L0_attn_out_proj_w, np.float32).T)

    perm1 = np.empty(2048, np.int64)
    for b in range(16):
        for c in range(16):
            for a in range(8):
                perm1[b * 128 + c * 8 + a] = c * 128 + a * 16 + b
    perm2 = np.empty(8192, np.int64)
    for b2 in range(64):
        for c in range(64):
            for a2 in range(2):
                perm2[b2 * 128 + c * 2 + a2] = c * 128 + a2 * 64 + b2
    wg = np.ascontiguousarray(np.asarray(L0_ffn_w_gate, np.float32).T[perm1, :])
    wu = np.ascontiguousarray(np.asarray(L0_ffn_w_up, np.float32).T[perm1, :])
    wd = np.ascontiguousarray(np.asarray(L0_ffn_w_down, np.float32).T[perm2, :])

    in_maps = []
    for k in range(NCORE):
        lo = k * TOWN - HALO
        sl = np.zeros((TT, DM), np.float32)
        src0 = max(lo, 0)
        sl[TT - (k * TOWN + TOWN - src0):] = X[src0: (k + 1) * TOWN]
        x_fmk = np.ascontiguousarray(sl.T)
        hm = np.full((128, 1), 0.0 if k == 0 else 1.0, np.float32)
        in_maps.append({"x_fm": x_fmk, "wt_in": wt_in, "convw": convw, "convb": convb,
                        "zb": zbias, "dvec": dv, "wt_out": wt_out,
                        "wg_p": wg, "wu_p": wu, "wd_p": wd, "hm_fm": hm})
    return in_maps


_NC_CACHE = {}

def _get_nc():
    if "nc" not in _NC_CACHE:
        nc = build_program()
        lower_extended_insts(nc)
        _NC_CACHE["nc"] = nc
    return _NC_CACHE["nc"]


def kernel(**inputs):
    in_maps = _shard_inputs(**inputs)
    nc = _get_nc()
    res = run_bass_kernel_spmd(nc, in_maps, core_ids=list(range(NCORE)))
    out = np.empty((16384, 2048), np.float32)
    for k in range(NCORE):
        out[k * TOWN:(k + 1) * TOWN, :] = np.asarray(res.results[k]["out_fm"]).T
    return out.reshape(1, 16384, 2048)


# version 06
# speedup vs baseline: 1.0655x; 1.0655x over previous
import numpy as np
from contextlib import ExitStack

import concourse.bass as bass
import concourse.mybir as mybir
from concourse.bass_utils import run_bass_kernel_spmd
from concourse.library_overlay import lower_extended_insts
from concourse.bass_isa import ReduceOp
from concourse.dve_spec import Spec, Src0, Src1, C0, C1, C2, Zero, maxx, minn
from concourse.dve_op import DveOp
from concourse import dve_ops as _dvo
import concourse.dve_spec as _dsp
from concourse.library_config import all_libraries

F32 = mybir.dt.float32
BF16 = mybir.dt.bfloat16
AF = mybir.ActivationFunctionType
ALU = mybir.AluOpType
AX = mybir.AxisListType

MAGIC = 12582912.0

def _register_op(name, spec, ref):
    if name in _dvo._SUB_OPCODE_FOR_NAME:
        return next(o for o in _dvo.OPS if o.name == name)
    lowered = _dsp.lower(spec, ver="v3", perf_en=False, name=name)
    op = DveOp(name, spec, subdim=False, uops_sha={"v3": lowered.sha("v3")},
               reference=ref)
    row = max(_dvo._SUB_OPCODE_FOR_NAME.values()) + 1
    assert row < 0x20
    _dvo.OPS.append(op)
    _dvo._SUB_OPCODE_FOR_NAME[op.name] = row
    _dvo.CUSTOM_DVE_SPECS[op.name] = spec
    _dvo.CUSTOM_DVE_REFS[op.name] = ref
    return op

RND_TT = _register_op(
    "RND_TT_ANT", Spec(body=(Src0 * Src1 + C0) - C0),
    lambda in0, in1, s0, s1, imm2: np.rint(in0 * in1))
TERN_TS = _register_op(
    "TERN_TS_ANT", Spec(body=minn(maxx(((Src0 * C1) + C0) - C0, Zero - C2), C2)),
    lambda in0, in1, s0, s1, imm2: np.minimum(np.maximum(np.rint(in0 * s1), -imm2), imm2))

def _hadamard(n):
    H = np.ones((1, 1), np.float32)
    while H.shape[0] < n:
        H = np.block([[H, H], [H, -H]])
    return H

H128 = _hadamard(128)
H16 = _hadamard(16)
H64 = _hadamard(64)
M1 = np.zeros((128, 128), np.float32)
for c in range(16):
    for a in range(8):
        for cp in range(16):
            M1[c * 8 + a, cp * 8 + a] = H16[c, cp]
M2 = np.zeros((128, 128), np.float32)
for c in range(64):
    for a in range(2):
        for cp in range(64):
            M2[c * 2 + a, cp * 2 + a] = H64[c, cp]
L1K = np.triu(np.ones((128, 128), np.float32), 0)
PEN = np.tril(np.ones((128, 128), np.float32), -1) * (-10000.0)
I128 = np.eye(128, dtype=np.float32)
OH16 = np.zeros((16, 16 * 128), np.float32)
for j in range(16):
    OH16[j, j * 128:(j + 1) * 128] = 1.0
OH64 = np.zeros((64, 64 * 128), np.float32)
for h in range(64):
    OH64[h, h * 128:(h + 1) * 128] = 1.0

NCORE = 8
TOWN = 2048
HALO = 256
TT = TOWN + HALO
NCH = TT // 128
NOWNCH = TOWN // 128
DM = 2048
CONVD = 5120
DI = 4096
DS = 64
NG = 8
NH = 64
PH = 64
FFI = 8192


class P:
    ENGINES = ("sync", "gpsimd", "tensor", "vector", "scalar")
    CHAIN = {"vector": "ov", "scalar": "oc"}

    def __init__(self, nc, stack):
        self.nc = nc
        self.plans = {e: [] for e in self.ENGINES}
        self.sems = {}
        self.counts = {}
        self.stack = stack

    def sem(self, name):
        if name not in self.sems:
            self.sems[name] = self.stack.enter_context(self.nc.semaphore(name))
            self.counts[name] = 0
        return self.sems[name]

    def emit(self, eng, fn, waits=(), inc=None):
        w = list(waits)
        if eng in self.CHAIN:
            assert inc is None
            sname = self.CHAIN[eng]
            self.sem(sname)
            if self.counts[sname] > 0:
                w.append((sname, self.counts[sname]))
            self.counts[sname] += 1
            inc = (sname, 1)
            out_count = self.counts[sname]
        elif inc is not None:
            sname, amt = inc
            self.sem(sname)
            self.counts[sname] += amt
            out_count = self.counts[sname]
        else:
            out_count = None
        for sname2, _v in w:
            self.sem(sname2)
        self.plans[eng].append((w, fn, inc))
        return out_count

    def cur(self, sname):
        self.sem(sname)
        return self.counts[sname]

    def barrier(self, waits):
        for eng in self.ENGINES:
            if eng in self.CHAIN:
                self.emit(eng, lambda e: e.nop(),
                          waits=[x for x in waits if x[0] != self.CHAIN[eng]])
            else:
                self.emit(eng, lambda e: e.nop(), waits=list(waits))

    def build_block(self, block, final_waits):
        for eng_name in self.ENGINES:
            plan = self.plans[eng_name]
            dec = getattr(block, eng_name)

            def mk(eng_name, plan):
                def body(eng):
                    for waits, fn, inc in plan:
                        for sname, v in waits:
                            eng.wait_ge(self.sems[sname], v)
                        r = fn(eng)
                        if inc is not None:
                            r.then_inc(self.sems[inc[0]], inc[1])
                    if eng_name == "sync":
                        for sname, v in final_waits:
                            eng.wait_ge(self.sems[sname], v)
                return body
            dec(mk(eng_name, plan))


def build_program(upto_phase=None):
    import os
    UPTO = int(os.environ.get("K_UPTO", "9")) if upto_phase is None else upto_phase
    BSUB = int(os.environ.get("K_BSUB", "9"))
    nc = bass.Bass("TRN2", target_bir_lowering=False, debug=False)

    x_fm = nc.dram_tensor("x_fm", [DM, TT], F32, kind="ExternalInput").ap()
    wt_in = nc.dram_tensor("wt_in", [DM, 9280], F32, kind="ExternalInput").ap()
    convw = nc.dram_tensor("convw", [128, 40 * 4], F32, kind="ExternalInput").ap()
    convb = nc.dram_tensor("convb", [128, 40], F32, kind="ExternalInput").ap()
    zb = nc.dram_tensor("zb", [128, 32], F32, kind="ExternalInput").ap()
    dvec = nc.dram_tensor("dvec", [1, 64], F32, kind="ExternalInput").ap()
    wt_out = nc.dram_tensor("wt_out", [DI, DM], F32, kind="ExternalInput").ap()
    wg_p = nc.dram_tensor("wg_p", [DM, FFI], F32, kind="ExternalInput").ap()
    wu_p = nc.dram_tensor("wu_p", [DM, FFI], F32, kind="ExternalInput").ap()
    wd_p = nc.dram_tensor("wd_p", [FFI, DM], F32, kind="ExternalInput").ap()
    hm_fm = nc.dram_tensor("hm_fm", [128, 1], F32, kind="ExternalInput").ap()
    out_fm = nc.dram_tensor("out_fm", [DM, TOWN], F32, kind="ExternalOutput").ap()

    xbc_sp = nc.dram_tensor("xbc_sp", [CONVD, TT], F32, kind="Internal").ap()
    z_sp = nc.dram_tensor("z_sp", [DI, TT], F32, kind="Internal").ap()
    ghi_sp = nc.dram_tensor("ghi_sp", [DI, TOWN], BF16, kind="Internal").ap()
    glo_sp = nc.dram_tensor("glo_sp", [DI, TOWN], BF16, kind="Internal").ap()
    y1_sp = nc.dram_tensor("y1_sp", [DM, TOWN], F32, kind="Internal").ap()
    y1f_sc = nc.dram_tensor("y1f_sc", [16, 128, TOWN], F32, kind="Internal").ap()
    sam_sc = nc.dram_tensor("sam_sc", [64, 128, TOWN], F32, kind="Internal").ap()
    wdt_sp = nc.dram_tensor("wdt_sp", [FFI, DM], BF16, kind="Internal").ap()
    dbg = nc.dram_tensor("dbg", [128, 2 * TT], F32, kind="ExternalOutput").ap()

    h128_c = nc.inline_tensor(H128, name="h128c").ap()
    m1_c = nc.inline_tensor(M1, name="m1c").ap()
    m2_c = nc.inline_tensor(M2, name="m2c").ap()
    l1_c = nc.inline_tensor(L1K, name="l1c").ap()
    pen_c = nc.inline_tensor(PEN, name="penc").ap()
    i128_c = nc.inline_tensor(I128, name="i128c").ap()
    oh_c = nc.inline_tensor(OH64, name="ohc").ap()
    noh_c = nc.inline_tensor(-OH64, name="nohc").ap()
    oh16_c = nc.inline_tensor(OH16, name="oh16c").ap()
    onesr_c = nc.inline_tensor(np.ones((1, 128), np.float32), name="onesrc").ap()
    onescol_c = nc.inline_tensor(np.ones((128, 1), np.float32), name="onescolc").ap()

    with ExitStack() as stack:
        def sbuf(name, shape, dtype=F32, st=None):
            return (st or stack).enter_context(nc.sbuf_tensor(name, shape, dtype))
        def psum(name, shape, dtype=F32, st=None):
            return (st or stack).enter_context(nc.psum_tensor(name, shape, dtype))

        p = P(nc, stack)
        block = stack.enter_context(nc.Block())

        s_h128 = sbuf("s_h128", [128, 128])
        s_m1 = sbuf("s_m1", [128, 128])
        s_m2 = sbuf("s_m2", [128, 128])
        s_l1 = sbuf("s_l1", [128, 128])
        s_pen_bf = sbuf("s_pen_bf", [128, 128], BF16)
        s_i128 = sbuf("s_i128", [128, 128])
        s_i128bf = sbuf("s_i128bf", [128, 128], BF16)
        s_onesr = sbuf("s_onesr", [1, 128])
        s_onesr_bf = sbuf("s_onesr_bf", [1, 128], BF16)
        s_onescol = sbuf("s_onescol", [128, 1])
        s_convw = sbuf("s_convw", [128, 40 * 4])
        s_convb = sbuf("s_convb", [128, 40])
        s_zb = sbuf("s_zb", [128, 32])
        s_dvec = sbuf("s_dvec", [1, 64])
        s_alog = sbuf("s_alog", [64, TT])
        BANKS = [psum(f"pbk{i}", [128, 512]) for i in range(8)]
        pA = BANKS[:4]

        for src, dst in [(h128_c, s_h128), (m1_c, s_m1), (m2_c, s_m2), (l1_c, s_l1),
                         (i128_c, s_i128), (onesr_c, s_onesr), (onescol_c, s_onescol),
                         (convw, s_convw), (convb, s_convb), (zb, s_zb), (dvec, s_dvec)]:
            p.emit("sync", lambda e, s=src, d=dst: e.dma_start(out=d[:], in_=s[:]),
                   inc=("qC", 16))
        for src, dst in [(pen_c, s_pen_bf), (i128_c, s_i128bf), (onesr_c, s_onesr_bf)]:
            p.emit("gpsimd", lambda e, s=src, d=dst: e.dma_start(out=d[:], in_=s[:]),
                   inc=("qCg", 16))
        consts_qS = p.cur("qC")
        consts_qG = p.cur("qCg")
        p.barrier([("qC", consts_qS), ("qCg", consts_qG)])

        stA = ExitStack()
        EW = 128
        HLEN0, HOFF1, HLEN1 = 1152, 1149, 1155
        HL = max(HLEN0, HLEN1)
        s_xhi = sbuf("s_xhi", [128, 16 * HL], BF16, st=stA)
        s_xlo = sbuf("s_xlo", [128, 16 * HL], BF16, st=stA)
        s_xf = sbuf("s_xf", [128, 2 * HL], st=stA)
        s_wf = sbuf("s_wf", [128, 2 * 16 * EW], st=stA)
        s_whi = sbuf("s_whi", [128, 2 * 16 * EW], BF16, st=stA)
        s_wlo = sbuf("s_wlo", [128, 2 * 16 * EW], BF16, st=stA)
        s_acc = sbuf("s_acc", [128, 2 * HL], st=stA)
        s_cv = sbuf("s_cv", [128, HL], st=stA)
        s_out = sbuf("s_out", [128, 2 * HL], st=stA)

        xv = x_fm.rearrange("(k p) t -> p k t", p=128)
        wv = wt_in.rearrange("(k p) e -> p k e", p=128)
        ET = [("xbc", i) for i in range(40)] + [("z", i) for i in range(32)] + [("a", 0)]
        grp_seq = 0
        spq_count = {0: 0, 1: 0}

        last_half_ov = 0
        evac_rec = {}
        wsplit_done = {}
        for th, (toff, tlen) in enumerate([(0, HLEN0), (HOFF1, HLEN1)]):
            xsplit_done = {}
            for k in range(16):
                buf = k % 2
                waits = [("ov", xsplit_done[k - 2])] if k >= 2 else (
                    [("ov", last_half_ov)] if th else [])
                c_q = p.emit("sync", lambda e, k=k, b=buf, toff=toff, tlen=tlen: e.dma_start(
                    out=s_xf[:, b * HL:b * HL + tlen], in_=xv[:, k, toff:toff + tlen]),
                    waits=waits, inc=(f"xq{buf}", 16))
                p.emit("vector", lambda e, k=k, b=buf, tlen=tlen: e.tensor_copy(
                    s_xhi[:, k * HL:k * HL + tlen], s_xf[:, b * HL:b * HL + tlen]),
                    waits=[(f"xq{buf}", c_q)])
                xsplit_done[k] = p.emit("vector", lambda e, k=k, b=buf, tlen=tlen:
                    e.scalar_tensor_tensor(
                        s_xlo[:, k * HL:k * HL + tlen], s_xhi[:, k * HL:k * HL + tlen],
                        -1.0, s_xf[:, b * HL:b * HL + tlen], ALU.mult, ALU.add))
            ngw = [512] * (tlen // 512) + ([tlen % 512] if tlen % 512 else [])
            wsplit_done = wsplit_done if th else {}

            for ei, (kind, idx) in enumerate(ET):
                gei = th * len(ET) + ei
                if kind == "xbc":
                    e0, ewid = idx * 128, 128
                elif kind == "z":
                    e0, ewid = 5120 + idx * 128, 128
                else:
                    e0, ewid = 9216, 64
                wb = gei % 2
                wbase = wb * 16 * EW
                waits_wf = [("ov", wsplit_done[gei - 2])] if gei >= 2 else []
                waits_wb = [("ot", 3 * (gei - 1))] if gei >= 2 else []
                cS = p.emit("sync", lambda e, e0=e0, ewid=ewid, wbase=wbase: e.dma_start(
                    out=s_wf[:, wbase:wbase + 16 * ewid].rearrange("p (k e) -> p k e", k=16),
                    in_=wv[:, :, e0:e0 + ewid]), waits=waits_wf, inc=(f"wfq{wb}", 16))
                cG = p.emit("gpsimd", lambda e, e0=e0, ewid=ewid, wbase=wbase: e.dma_start(
                    out=s_whi[:, wbase:wbase + 16 * ewid].rearrange("p (k e) -> p k e", k=16),
                    in_=wv[:, :, e0:e0 + ewid]), waits=waits_wb, inc=(f"whq{wb}", 16))
                cWrd = p.emit("vector", lambda e, ewid=ewid, wbase=wbase: e.scalar_tensor_tensor(
                    s_wlo[:, wbase:wbase + 16 * ewid], s_whi[:, wbase:wbase + 16 * ewid],
                    -1.0, s_wf[:, wbase:wbase + 16 * ewid], ALU.mult, ALU.add),
                    waits=[(f"wfq{wb}", cS), (f"whq{wb}", cG)])
                wsplit_done[gei] = cWrd

                t0 = 0
                for tg, tw in enumerate(ngw):
                    bank = grp_seq % 4
                    waits_mm = [("ov", cWrd)]
                    if grp_seq >= 4:
                        waits_mm.append(evac_rec[grp_seq - 4])
                    first = True
                    for k in range(16):
                        for pi, (wsb, xsb) in enumerate(((s_whi, s_xhi), (s_whi, s_xlo),
                                                         (s_wlo, s_xhi))):
                            w_ap = wsb[:, wbase + k * ewid: wbase + (k + 1) * ewid]
                            x_ap = xsb[:, k * HL + t0: k * HL + t0 + tw]
                            last = (k == 15 and pi == 2)
                            teA_c = p.emit("tensor", lambda e, w=w_ap, x=x_ap, b=bank,
                                   ewid=ewid, tw=tw, f=first, l=last:
                                   e.matmul(pA[b][0:ewid, 0:tw], w, x, start=f, stop=l),
                                   waits=(waits_mm if first else []),
                                   inc=(("ot", 1) if last else None))
                            first = False
                    ob = (gei % 2) * HL
                    if kind == "a":
                        cv_ = p.emit("vector", lambda e, b=bank, t0=t0, tw=tw, toff=toff:
                               e.tensor_copy(s_alog[:, toff + t0: toff + t0 + tw],
                                             pA[b][0:64, 0:tw]),
                               waits=[("ot", teA_c)])
                        evac_rec[grp_seq] = ("ov", cv_)
                    elif kind == "z":
                        wz = [("ot", teA_c)]
                        if spq_count[gei % 2] > 0:
                            wz.append((f"spq{gei % 2}", 16 * spq_count[gei % 2]))
                        cs_ = p.emit("scalar", lambda e, b=bank, t0=t0, tw=tw, ob=ob, idx=idx:
                               e.activation(s_out[:, ob + t0: ob + t0 + tw], pA[b][:, 0:tw],
                                            AF.Silu, bias=s_zb[:, idx:idx + 1]),
                               waits=wz)
                        evac_rec[grp_seq] = ("oc", cs_)
                    else:
                        cv_ = p.emit("vector", lambda e, b=bank, t0=t0, tw=tw, ob=ob:
                               e.tensor_copy(s_acc[:, ob + t0: ob + t0 + tw],
                                             pA[b][:, 0:tw]),
                               waits=[("ot", teA_c)])
                        evac_rec[grp_seq] = ("ov", cv_)
                    t0 += tw
                    grp_seq += 1

                evA_after = [evac_rec[g] for g in range(grp_seq - len(ngw), grp_seq)]
                ob = (gei % 2) * HL
                sp0 = 0 if th == 0 else HLEN0
                lcl0 = 0 if th == 0 else 3
                splen = tlen - lcl0
                if kind == "xbc":
                    w3 = s_convw[:, idx * 4 + 3: idx * 4 + 4]
                    p.emit("vector", lambda e, ob=ob, w3=w3, tlen=tlen: e.tensor_scalar(
                        s_cv[:, 0:tlen], s_acc[:, ob:ob + tlen], w3, None, ALU.mult),
                        waits=list(evA_after))
                    cv_last = None
                    for kk, sh in ((2, 1), (1, 2), (0, 3)):
                        wk = s_convw[:, idx * 4 + kk: idx * 4 + kk + 1]
                        cv_last = p.emit("vector", lambda e, ob=ob, wk=wk, sh=sh, tlen=tlen:
                               e.scalar_tensor_tensor(
                                   s_cv[:, sh:tlen], s_acc[:, ob: ob + tlen - sh], wk,
                                   s_cv[:, sh:tlen], ALU.mult, ALU.add))
                    wsl = [("ov", cv_last)]
                    if spq_count[gei % 2] > 0:
                        wsl.append((f"spq{gei % 2}", 16 * spq_count[gei % 2]))
                    cs = p.emit("scalar", lambda e, ob=ob, idx=idx, tlen=tlen: e.activation(
                        s_out[:, ob:ob + tlen], s_cv[:, 0:tlen], AF.Silu,
                        bias=s_convb[:, idx:idx + 1]),
                        waits=wsl)
                    p.emit("sync", lambda e, ob=ob, idx=idx, sp0=sp0, lcl0=lcl0,
                           splen=splen: e.dma_start(
                        out=xbc_sp[idx * 128:(idx + 1) * 128, sp0:sp0 + splen],
                        in_=s_out[:, ob + lcl0: ob + lcl0 + splen]),
                        waits=[("oc", cs)], inc=(f"spq{gei % 2}", 16))
                    spq_count[gei % 2] += 1
                elif kind == "z":
                    p.emit("sync", lambda e, ob=ob, idx=idx, sp0=sp0, lcl0=lcl0,
                           splen=splen: e.dma_start(
                        out=z_sp[idx * 128:(idx + 1) * 128, sp0:sp0 + splen],
                        in_=s_out[:, ob + lcl0: ob + lcl0 + splen]),
                        waits=list(evA_after), inc=(f"spq{gei % 2}", 16))
                    spq_count[gei % 2] += 1

            last_half_ov = p.cur("ov")
        p.barrier([("spq0", p.cur("spq0")), ("spq1", p.cur("spq1")),
                   ("ov", p.cur("ov")), ("oc", p.cur("oc")), ("ot", p.cur("ot"))])
        stA.close()

        if UPTO >= 2:
            stB = ExitStack()
            s_dtT = sbuf("s_dtT", [128, NCH * 64], st=stB)
            s_DdT = sbuf("s_DdT", [128, NCH * 64], st=stB)
            s_dstT = sbuf("s_dstT", [128, NCH * 64], st=stB)
            s_ahi = sbuf("s_ahi", [64, NCH * 128], BF16, st=stB)
            s_alo = sbuf("s_alo", [64, NCH * 128], BF16, st=stB)
            s_nahi = sbuf("s_nahi", [64, NCH * 128], BF16, st=stB)
            s_nalo = sbuf("s_nalo", [64, NCH * 128], BF16, st=stB)
            s_ash = sbuf("s_ash", [128, NCH * 64], st=stB)
            s_dout = sbuf("s_dout", [128, NCH * 64], st=stB)
            s_d32 = sbuf("s_d32", [128, NCH * 32], st=stB)
            s_dbc = sbuf("s_dbc", [128, 64], st=stB)
            s_ohbf = sbuf("s_ohbf", [64, 64 * 128], BF16, st=stB)
            s_spk = sbuf("s_spk", [128, 32 * 64], st=stB)
            s_u1 = sbuf("s_u1", [128, 128], st=stB)
            s_hm = sbuf("s_hm", [128, 1], st=stB)

            stPre = ExitStack()
            s_prb = sbuf("s_prb", [128, NCH * 64], st=stPre)
            s_dtr = sbuf("s_dtr", [1, NCH * 64], st=stPre)
            s_ahsf = sbuf("s_ahsf", [64, NCH * 128], st=stPre)
            bank_rec = {}

            def bank_wait(i):
                return [bank_rec[i]] if i in bank_rec else []

            cg1 = p.emit("gpsimd", lambda e: e.dma_start(out=s_ohbf[:], in_=oh_c[:]),
                         inc=("qBg", 16))
            p.emit("gpsimd", lambda e: e.dma_start(out=s_hm[:], in_=hm_fm[:]),
                   inc=("qBg", 16))
            p.emit("vector", lambda e: e.tensor_scalar(s_u1[:], s_l1[:], -1.0, 1.0,
                                                       ALU.mult, ALU.add))
            p.emit("vector", lambda e: e.memset(s_spk[:], 0.0))
            p.barrier([("qBg", p.cur("qBg"))])

            for c in range(NCH):
                b = c % 2
                tw = bank_wait(b)
                cT = p.emit("tensor", lambda e, c=c, b=b: e.transpose(
                    BANKS[b][:, 0:64], s_alog[:, c * 128:(c + 1) * 128],
                    s_i128[0:64, 0:64]), waits=tw, inc=("ot", 1))
                cv = p.emit("vector", lambda e, c=c, b=b: e.tensor_copy(
                    s_prb[:, c * 64:(c + 1) * 64], BANKS[b][:, 0:64]),
                    waits=[("ot", cT)])
                bank_rec[b] = ("ov", cv)
            c1 = p.emit("scalar", lambda e: e.activation(s_dtT[:], s_prb[:], AF.Abs),
                        waits=[("ov", p.cur("ov"))])
            c2 = p.emit("scalar", lambda e: e.activation(s_dtT[:], s_dtT[:], AF.Exp, scale=-1.0))
            c3 = p.emit("vector", lambda e: e.tensor_scalar_add(s_dtT[:], s_dtT[:], 1.0),
                        waits=[("oc", c2)])
            c4 = p.emit("scalar", lambda e: e.activation(s_dtT[:], s_dtT[:], AF.Ln),
                        waits=[("ov", c3)])
            c5 = p.emit("scalar", lambda e: e.activation(s_DdT[:], s_prb[:], AF.Relu))
            c6 = p.emit("vector", lambda e: e.tensor_add(s_dtT[:], s_dtT[:], s_DdT[:]),
                        waits=[("oc", c5), ("oc", c4)])
            c7 = p.emit("vector", lambda e: e.reciprocal(s_DdT[:], s_dtT[:]))
            cT = p.emit("tensor", lambda e: e.matmul(BANKS[0][:, 0:64], s_onesr[:],
                                                     s_dvec[:], start=True, stop=True),
                        waits=bank_wait(0), inc=("ot", 1))
            cv = p.emit("vector", lambda e: e.tensor_copy(s_dbc[:], BANKS[0][:, 0:64]),
                        waits=[("ot", cT)])
            bank_rec[0] = ("ov", cv)
            c8 = p.emit("vector", lambda e: e.tensor_tensor(
                s_DdT[:].rearrange("p (c h) -> p c h", c=NCH),
                s_DdT[:].rearrange("p (c h) -> p c h", c=NCH),
                s_dbc[:].unsqueeze(1).broadcast_to([128, NCH, 64]), ALU.mult))
            c9 = p.emit("vector", lambda e: e.tensor_scalar_mul(s_prb[:], s_dtT[:], -1.0))
            for c in range(NCH):
                b0, b1 = c % 2, 2 + (c % 2)
                ndt = s_prb[:, c * 64:(c + 1) * 64]
                w0 = bank_wait(b0) + ([("ov", c9)] if c <= 1 else [])
                cT0 = p.emit("tensor", lambda e, ndt=ndt, b0=b0: e.matmul(
                    BANKS[b0][0:64, 0:128], ndt, s_l1[:], start=True, stop=True),
                    waits=w0, inc=("ot", 1))
                cv0 = p.emit("vector", lambda e, c=c, b0=b0: e.tensor_copy(
                    s_ahsf[:, c * 128:(c + 1) * 128], BANKS[b0][0:64, 0:128]),
                    waits=[("ot", cT0)])
                bank_rec[b0] = ("ov", cv0)
                w1 = bank_wait(b1) + ([("ov", c9)] if c <= 1 else [])
                cT1 = p.emit("tensor", lambda e, ndt=ndt, b1=b1: e.matmul(
                    BANKS[b1][:, 0:64], s_l1[:], ndt, start=True, stop=True),
                    waits=w1, inc=("ot", 1))
                cT2 = p.emit("tensor", lambda e, ndt=ndt, b1=b1: e.matmul(
                    BANKS[b1][:, 64:128], s_u1[:], ndt, start=True, stop=True),
                    inc=("ot", 1))
                cv1 = p.emit("vector", lambda e, c=c, b1=b1: e.tensor_copy(
                    s_ash[:, c * 64:(c + 1) * 64], BANKS[b1][:, 0:64]),
                    waits=[("ot", cT2)])
                cv2 = p.emit("vector", lambda e, c=c, b1=b1: e.tensor_copy(
                    s_dstT[:, c * 64:(c + 1) * 64], BANKS[b1][:, 64:128]))
                bank_rec[b1] = ("ov", cv2)
            cdd = p.emit("scalar", lambda e: e.activation(s_dstT[:], s_dstT[:], AF.Exp),
                         waits=[("ov", p.cur("ov"))])
            cd1 = p.emit("scalar", lambda e: e.activation(s_dout[:], s_ash[:], AF.Exp))
            for j in range(3):
                b = j % 2
                tw = bank_wait(b)
                cT = p.emit("tensor", lambda e, j=j, b=b: e.matmul(
                    BANKS[b][0:1, 0:384], s_onescol[:], s_prb[:, j * 384:(j + 1) * 384],
                    start=True, stop=True), waits=tw, inc=("ot", 1))
                cv = p.emit("vector", lambda e, j=j, b=b: e.tensor_copy(
                    s_dtr[:, j * 384:(j + 1) * 384], BANKS[b][0:1, 0:384]),
                    waits=[("ot", cT)])
                bank_rec[b] = ("ov", cv)
            cd2 = p.emit("scalar", lambda e: e.activation(s_dtr[:], s_dtr[:], AF.Exp),
                         waits=[("ov", p.cur("ov"))])
            ch1 = p.emit("vector", lambda e: e.tensor_copy(s_ahi[:], s_ahsf[:]))
            ch2 = p.emit("vector", lambda e: e.scalar_tensor_tensor(
                s_alo[:], s_ahi[:], -1.0, s_ahsf[:], ALU.mult, ALU.add))
            ch3 = p.emit("vector", lambda e: e.tensor_scalar_mul(s_nahi[:], s_ahi[:], -1.0))
            ch4 = p.emit("vector", lambda e: e.tensor_scalar_mul(s_nalo[:], s_alo[:], -1.0))
            for c in range(NCH):
                b = c % 2
                tw = bank_wait(b) + ([("oc", cd2)] if c <= 1 else [])
                r0 = s_dtr[:, c * 64:(c + 1) * 64].rearrange("o (g m) -> o g m", g=8)
                cT0 = p.emit("tensor", lambda e, r0=r0, b=b: e.matmul(
                    BANKS[b][0:64, 0:32], s_onesr[:, 0:64], r0[:, 0::2, :],
                    start=True, stop=True), waits=tw, inc=("ot", 1))
                cT1 = p.emit("tensor", lambda e, r0=r0, b=b: e.matmul(
                    BANKS[b][64:128, 0:32], s_onesr[:, 0:64], r0[:, 1::2, :], start=True,
                    stop=True, tile_position=(0, 64)), inc=("ot", 1))
                cv = p.emit("vector", lambda e, c=c, b=b: e.tensor_copy(
                    s_d32[:, c * 32:(c + 1) * 32], BANKS[b][:, 0:32]), waits=[("ot", cT1)])
                bank_rec[b] = ("ov", cv)
            p.barrier([("ov", p.cur("ov")), ("oc", p.cur("oc")), ("ot", p.cur("ot"))])
            stPre.close()

            s_xbcr = sbuf("s_xbcr", [128, 2 * 40 * 128], st=stB)
            s_zr = sbuf("s_zr", [128, 32 * 128], st=stB)
            s_xtm = sbuf("s_xtm", [128, 32 * 128], st=stB)
            s_btm = sbuf("s_btm", [128, 4 * 128], st=stB)
            s_xd = sbuf("s_xd", [128, 32 * 128], st=stB)
            s_sct = sbuf("s_sct", [128, 8 * 128], st=stB)
            s_lmq = sbuf("s_lmq", [128, 2 * 512], st=stB)
            s_lsq = sbuf("s_lsq", [128, 2 * 512], st=stB)
            s_tmp1 = sbuf("s_tmp1", [128, 512], st=stB)
            s_yf = sbuf("s_yf", [128, 2 * 512], st=stB)
            s_ghi = sbuf("s_ghi", [128, 32 * 128], BF16, st=stB)
            s_glo = sbuf("s_glo", [128, 32 * 128], BF16, st=stB)

            xbv = xbc_sp.rearrange("(i p) t -> p i t", p=128)
            zv = z_sp.rearrange("(i p) t -> p i t", p=128)
            gh_v = ghi_sp.rearrange("(i p) t -> p i t", p=128)
            gl_v = glo_sp.rearrange("(i p) t -> p i t", p=128)
            xb_rel = {}
            z_rel = {}
            gat_rel = {}
            gsp_cnt = 0
            for c in range(NCH):
                own = c >= 2
                xsl_ = (c % 2) * 5120
                xb = s_xbcr[:, xsl_:xsl_ + 5120]
                zr = s_zr[:, 0:4096]
                wx = [xb_rel[c - 2]] if c >= 2 else []
                xbs = f"xbq{c % 2}"
                cxq = p.emit("sync", lambda e, c=c, xb=xb: e.dma_start(
                    out=xb.rearrange("p (i t) -> p i t", i=40),
                    in_=xbv[:, :, c * 128:(c + 1) * 128]),
                    waits=wx, inc=(xbs, 16))
                if own:
                    wz = [z_rel[c - 1]] if (c - 1) in z_rel else []
                    czq = p.emit("sync", lambda e, c=c, zr=zr: e.dma_start(
                        out=zr.rearrange("p (i t) -> p i t", i=32),
                        in_=zv[:, :, c * 128:(c + 1) * 128]),
                        waits=wz, inc=("zbq", 16))

                if c < 2:
                    chm = p.emit("vector", lambda e, xb=xb: e.tensor_scalar_mul(
                        xb, xb, s_hm[:, 0:1]), waits=[(xbs, cxq)])
                    xwait = [("ov", chm)]
                else:
                    xwait = [(xbs, cxq)]

                for grp in range(9 if BSUB >= 2 else 0):
                    b = grp % 2
                    tw = bank_wait(b)
                    cT = None
                    for j in range(4):
                        i = grp * 4 + j
                        src = xb[:, i * 128:(i + 1) * 128]
                        cT = p.emit("tensor", lambda e, src=src, b=b, j=j: e.transpose(
                            BANKS[b][:, j * 128:(j + 1) * 128], src, s_i128[:]),
                            waits=((tw + xwait) if j == 0 else []),
                            inc=(("ot", 1) if j == 3 else None))
                    dst = (s_xtm[:, grp * 512:(grp + 1) * 512] if grp < 8
                           else s_btm[:, 0:512])
                    cv = p.emit("vector", lambda e, dst=dst, b=b: e.tensor_copy(
                        dst, BANKS[b][:, 0:512]), waits=[("ot", cT)])
                    bank_rec[b] = ("ov", cv)
                xtm_done = p.cur("ov")

                if BSUB < 2:
                    xb_rel[c] = (xbs, cxq)
                    if own:
                        z_rel[c] = ("zbq", czq)
                    continue
                cxd = p.emit("vector", lambda e, c=c: e.tensor_tensor(
                    s_xd[:].rearrange("p (h q) -> p h q", h=64),
                    s_xtm[:].rearrange("p (h q) -> p h q", h=64),
                    s_dstT[:, c * 64:(c + 1) * 64].unsqueeze(2).broadcast_to([128, 64, 64]),
                    ALU.mult))

                if own and BSUB >= 3:
                    for par in range(2):
                        b = 2 + par
                        tw = bank_wait(b)
                        cT = None
                        for j in range(4):
                            ti_b = 32 + j
                            ti_c = 36 + j
                            pr = par * 64
                            lb = xb[pr:pr + 64, ti_b * 128:(ti_b + 1) * 128]
                            rc_ = xb[pr:pr + 64, ti_c * 128:(ti_c + 1) * 128]
                            cT = p.emit("tensor", lambda e, lb=lb, rc_=rc_, b=b, j=j:
                                        e.matmul(BANKS[b][:, j * 128:(j + 1) * 128], lb, rc_,
                                                 start=True, stop=True),
                                        waits=(tw if j == 0 else []),
                                        inc=(("ot", 1) if j == 3 else None))
                        dst = s_sct[:].rearrange("p (j q) -> p j q", j=4)[
                            :, :, par * 128:(par + 1) * 128]
                        cv = p.emit("vector", lambda e, dst=dst, b=b: e.tensor_copy(
                            dst, BANKS[b][:, 0:512].rearrange("p (j s) -> p j s", j=4)),
                            waits=[("ot", cT)])
                        bank_rec[b] = ("ov", cv)
                    sct_done = p.cur("ov")

                    for q in range(16 if BSUB >= 4 else 0):
                        qb = q % 2
                        b = qb
                        tw = bank_wait(b)
                        cT = None
                        for j in range(4):
                            h = q * 4 + j
                            sl = slice(j * 128, (j + 1) * 128)
                            ohs = s_ohbf[:, h * 128:(h + 1) * 128]
                            ahi = s_ahi[:, c * 128:(c + 1) * 128]
                            alo = s_alo[:, c * 128:(c + 1) * 128]
                            nahi = s_nahi[:, c * 128:(c + 1) * 128]
                            nalo = s_nalo[:, c * 128:(c + 1) * 128]
                            p.emit("tensor", lambda e, ohs=ohs, ahi=ahi, b=b, sl=sl:
                                   e.matmul(BANKS[b][:, sl], ohs, ahi, start=True, stop=False),
                                   waits=(tw if j == 0 else []))
                            p.emit("tensor", lambda e, ohs=ohs, alo=alo, b=b, sl=sl:
                                   e.matmul(BANKS[b][:, sl], ohs, alo, start=False, stop=False))
                            p.emit("tensor", lambda e, ohs=ohs, nahi=nahi, b=b, sl=sl:
                                   e.matmul(BANKS[b][:, sl], nahi, ohs, start=False, stop=False))
                            p.emit("tensor", lambda e, ohs=ohs, nalo=nalo, b=b, sl=sl:
                                   e.matmul(BANKS[b][:, sl], nalo, ohs, start=False, stop=False))
                            cT = p.emit("tensor", lambda e, b=b, sl=sl:
                                        e.matmul(BANKS[b][:, sl], s_i128bf[:], s_pen_bf[:],
                                                 start=False, stop=True),
                                        inc=(("ot", 1) if j == 3 else None))
                        lq = (q % 2) * 512
                        cs = p.emit("scalar", lambda e, b=b, lq=lq: e.activation(
                            s_lmq[:, lq:lq + 512], BANKS[b][:, 0:512], AF.Exp),
                            waits=[("ot", cT)])
                        bank_rec[b] = ("oc", cs)
                        g = q >> 1
                        cls = p.emit("vector", lambda e, g=g, lq=lq: e.tensor_tensor(
                            s_lsq[:, lq:lq + 512].rearrange("p (j s) -> p j s", j=4),
                            s_lmq[:, lq:lq + 512].rearrange("p (j s) -> p j s", j=4),
                            s_sct[:, g * 128:(g + 1) * 128].unsqueeze(1)
                            .broadcast_to([128, 4, 128]), ALU.mult),
                            waits=[("oc", cs), ("ov", sct_done)])

                        yb = 4 + qb
                        ob_ = 6 + qb
                        for j in range(4):
                            h = q * 4 + j
                            lsl = s_lsq[:, lq + j * 128: lq + (j + 1) * 128]
                            xsl = s_xtm[:, h * 64:(h + 1) * 64]
                            w0 = (bank_wait(yb) + [("ov", cls)]) if j == 0 else []
                            p.emit("tensor", lambda e, lsl=lsl, xsl=xsl, yb=yb, j=j:
                                   e.matmul(BANKS[yb][:, j * 64:(j + 1) * 64],
                                            lsl, xsl, start=True, stop=True),
                                   waits=w0)
                        for j in range(4):
                            h = q * 4 + j
                            g2 = h >> 3
                            ti_c = 36 + g2 // 2
                            pr = (g2 % 2) * 64
                            csl = xb[pr:pr + 64, ti_c * 128:(ti_c + 1) * 128]
                            jj = 8 * (g2 >> 1) + (h % 8)
                            hrow = (g2 % 2) * 64
                            ssl = s_spk[hrow:hrow + 64, jj * 64:(jj + 1) * 64]
                            w1 = bank_wait(ob_) if j == 0 else []
                            cyo = p.emit("tensor", lambda e, csl=csl, ssl=ssl, ob_=ob_, j=j:
                                   e.matmul(BANKS[ob_][:, j * 64:(j + 1) * 64], csl, ssl,
                                            start=True, stop=True),
                                   waits=w1, inc=(("ot", 1) if j == 3 else None))
                        dv_ = s_dout[:, c * 64 + q * 4: c * 64 + (q + 1) * 4]
                        ddv = s_DdT[:, c * 64 + q * 4: c * 64 + (q + 1) * 4]
                        c1_ = p.emit("vector", lambda e, ob_=ob_, dv_=dv_: e.tensor_tensor(
                            s_tmp1[:, 0:256].rearrange("p (j q2) -> p j q2", j=4),
                            BANKS[ob_][:, 0:256].rearrange("p (j q2) -> p j q2", j=4),
                            dv_.unsqueeze(2).broadcast_to([128, 4, 64]), ALU.mult),
                            waits=[("ot", cyo)])
                        yq = (q % 2) * 512
                        c2_ = p.emit("vector", lambda e, yb=yb, yq=yq: e.tensor_add(
                            s_yf[:, yq: yq + 256], s_tmp1[:, 0:256], BANKS[yb][:, 0:256]))
                        bank_rec[yb] = ("ov", c2_)
                        bank_rec[ob_] = ("ov", c2_)
                        c3_ = p.emit("vector", lambda e, q=q, ddv=ddv: e.tensor_tensor(
                            s_tmp1[:, 0:256].rearrange("p (j q2) -> p j q2", j=4),
                            s_xtm[:, q * 256:(q + 1) * 256].rearrange("p (j q2) -> p j q2", j=4),
                            ddv.unsqueeze(2).broadcast_to([128, 4, 64]), ALU.mult))
                        c4_ = p.emit("vector", lambda e, yq=yq: e.tensor_add(
                            s_yf[:, yq: yq + 256], s_yf[:, yq: yq + 256], s_tmp1[:, 0:256]))
                        gb2 = 2 + qb
                        tw2 = bank_wait(gb2)
                        cTg = None
                        for j in range(2):
                            src = s_yf[:, yq + j * 128: yq + (j + 1) * 128]
                            cTg = p.emit("tensor", lambda e, src=src, gb2=gb2, j=j:
                                         e.transpose(BANKS[gb2][:, j * 128:(j + 1) * 128],
                                                     src, s_i128[:]),
                                         waits=((tw2 + [("ov", c4_)]) if j == 0 else []),
                                         inc=(("ot", 1) if j == 1 else None))
                        cvg = p.emit("vector", lambda e, q=q, gb2=gb2, zr=zr: e.tensor_tensor(
                            zr[:, q * 256:(q + 1) * 256], BANKS[gb2][:, 0:256],
                            zr[:, q * 256:(q + 1) * 256], ALU.mult),
                            waits=[("ot", cTg), ("zbq", czq)])
                        bank_rec[gb2] = ("ov", cvg)
                    z_rel[c] = ("ov", p.cur("ov"))
                if own and BSUB >= 5:

                    wgs = [] if gsp_cnt == 0 else [("gsq", 16 * gsp_cnt)]
                    cg1_ = p.emit("vector", lambda e, zr=zr: e.tensor_copy(s_ghi[:], zr),
                                  waits=wgs)
                    cg2_ = p.emit("vector", lambda e, zr=zr: e.scalar_tensor_tensor(
                        s_glo[:], s_ghi[:], -1.0, zr, ALU.mult, ALU.add))
                    p.emit("sync", lambda e, c=c: e.dma_start(
                        out=gh_v[:, :, (c - 2) * 128:(c - 1) * 128],
                        in_=s_ghi[:].rearrange("p (i t) -> p i t", i=32)),
                        waits=[("ov", cg2_)], inc=("gsq", 16))
                    p.emit("sync", lambda e, c=c: e.dma_start(
                        out=gl_v[:, :, (c - 2) * 128:(c - 1) * 128],
                        in_=s_glo[:].rearrange("p (i t) -> p i t", i=32)),
                        inc=("gsq", 16))
                    gsp_cnt += 2
                    z_rel[c] = ("ov", cg2_)

                for sb in range(4):
                    b = [2, 3, 4, 5][sb] if own else [2, 3, 4, 5][sb]
                    tw = bank_wait(b)
                    cst = None
                    for j8 in range(8):
                        jj = sb * 8 + j8
                        for hh in range(2):
                            h = 16 * (jj >> 3) + (jj % 8) + hh * 8
                            g = h >> 3
                            bsl = s_btm[:, g * 64:(g + 1) * 64]
                            xdsl = s_xd[:, h * 64:(h + 1) * 64]
                            pr = hh * 64
                            lastst = (j8 == 7 and hh == 1)
                            w0 = (tw + [("ov", cxd)]) if (j8 == 0 and hh == 0) else []
                            kw = {"tile_position": (0, 64)} if pr else {}
                            cst = p.emit("tensor", lambda e, bsl=bsl, xdsl=xdsl, b=b, j8=j8,
                                   pr=pr, kw=kw: e.matmul(
                                       BANKS[b][pr:pr + 64, j8 * 64:(j8 + 1) * 64],
                                       bsl, xdsl, start=True, stop=True, **kw),
                                   waits=w0, inc=(("ot", 1) if lastst else None))
                    cvu = None
                    for j8 in range(8):
                        jj = sb * 8 + j8
                        dcol = s_d32[:, c * 32 + jj: c * 32 + jj + 1]
                        cvu = p.emit("vector", lambda e, jj=jj, b=b, j8=j8, dcol=dcol:
                                     e.scalar_tensor_tensor(
                                         s_spk[:, jj * 64:(jj + 1) * 64],
                                         s_spk[:, jj * 64:(jj + 1) * 64], dcol,
                                         BANKS[b][:, j8 * 64:(j8 + 1) * 64],
                                         ALU.mult, ALU.add),
                                     waits=([("ot", cst)] if j8 == 0 else []))
                    bank_rec[b] = ("ov", cvu)
                if own and c not in z_rel:
                    z_rel[c] = ("zbq", czq)
                xb_rel[c] = ("ot", p.cur("ot"))

            p.barrier([("gsq", p.cur("gsq")),
                       ("ov", p.cur("ov")), ("oc", p.cur("oc")), ("ot", p.cur("ot"))])
            stB.close()

        if UPTO >= 3:
            stC = ExitStack()
            s_gh = sbuf("s_gh", [128, 32 * 512], BF16, st=stC)
            s_gl = sbuf("s_gl", [128, 32 * 512], BF16, st=stC)
            s_wof = sbuf("s_wof", [128, 2 * 32 * 128], st=stC)
            s_wohi = sbuf("s_wohi", [128, 32 * 128], BF16, st=stC)
            s_wolo = sbuf("s_wolo", [128, 32 * 128], BF16, st=stC)
            s_xc = sbuf("s_xc", [128, 16 * 512], st=stC)
            s_res = sbuf("s_res", [128, 16 * 512], st=stC)
            s_sq = sbuf("s_sq", [128, 512], st=stC)
            s_st = sbuf("s_st", [1, 4 * 512], st=stC)
            s_mb = sbuf("s_mb", [128, 512], st=stC)
            s_rb = sbuf("s_rb", [128, 512], st=stC)
            s_y1t = sbuf("s_y1t", [128, 512], st=stC)
            s_isq = sbuf("s_isq", [128, 2], st=stC)

            seed_c = nc.inline_tensor(np.array([[2048.0, 8192.0]], np.float32), name="seedc").ap()
            s_seed = sbuf("s_seed", [1, 2], st=stC)
            cq = p.emit("sync", lambda e: e.dma_start(out=s_seed[:], in_=seed_c[:]),
                        inc=("qCc", 16))
            csq = p.emit("scalar", lambda e: e.activation(s_seed[:], s_seed[:], AF.Sqrt),
                         waits=[("qCc", cq)])
            crc = p.emit("vector", lambda e: e.reciprocal(s_seed[:], s_seed[:]),
                         waits=[("oc", csq)])
            cT = p.emit("tensor", lambda e: e.matmul(BANKS[0][:, 0:2], s_onesr[:],
                                                     s_seed[:], start=True, stop=True),
                        waits=bank_wait(0) + [("ov", crc)], inc=("ot", 1))
            cv = p.emit("vector", lambda e: e.tensor_copy(s_isq[:], BANKS[0][:, 0:2]),
                        waits=[("ot", cT)])
            bank_rec[0] = ("ov", cv)

            ghv2 = ghi_sp.rearrange("(i p) t -> p i t", p=128)
            glv2 = glo_sp.rearrange("(i p) t -> p i t", p=128)
            wov = wt_out.rearrange("(k p) e -> p k e", p=128)
            xv2 = x_fm.rearrange("(k p) t -> p k t", p=128)
            y1v = y1_sp.rearrange("(k p) t -> p k t", p=128)
            y1fv = y1f_sc.rearrange("c p t -> p c t")

            wo_rel = {}
            gh_rel = {}
            last_bounce = None
            for tg in range(4):
                t0 = tg * 512
                wg_ = [gh_rel[tg - 1]] if tg >= 1 else []
                cgh = p.emit("sync", lambda e, t0=t0: e.dma_start(
                    out=s_gh[:].rearrange("p (i t) -> p i t", i=32),
                    in_=ghv2[:, :, t0:t0 + 512]), waits=wg_, inc=("ghq", 16))
                cgl = p.emit("sync", lambda e, t0=t0: e.dma_start(
                    out=s_gl[:].rearrange("p (i t) -> p i t", i=32),
                    in_=glv2[:, :, t0:t0 + 512]), inc=("ghq", 16))
                cxc = p.emit("sync", lambda e, t0=t0: e.dma_start(
                    out=s_xc[:].rearrange("p (k t) -> p k t", k=16),
                    in_=xv2[:, :, HALO + t0: HALO + t0 + 512]), inc=("xcq", 16))
                res_done = {}
                for oi in range(16):
                    gwo = tg * 16 + oi
                    wb = gwo % 2
                    wbase = wb * 32 * 128
                    wfw = [wo_rel[gwo - 2]] if gwo >= 2 else []
                    cwf = p.emit("sync", lambda e, oi=oi, wbase=wbase: e.dma_start(
                        out=s_wof[:, wbase:wbase + 32 * 128].rearrange("p (k e) -> p k e", k=32),
                        in_=wov[:, :, oi * 128:(oi + 1) * 128]),
                        waits=wfw, inc=(f"woq{wb}", 16))
                    whw = [("ot", p.cur("ot"))] if gwo >= 2 else []
                    cwh = p.emit("vector", lambda e, wbase=wbase: e.tensor_copy(
                        s_wohi[:], s_wof[:, wbase:wbase + 32 * 128]),
                        waits=[(f"woq{wb}", cwf)] + ([] if gwo < 2 else [wo_mm_rel]))
                    cwl = p.emit("vector", lambda e, wbase=wbase: e.scalar_tensor_tensor(
                        s_wolo[:], s_wohi[:], -1.0, s_wof[:, wbase:wbase + 32 * 128],
                        ALU.mult, ALU.add))
                    wo_rel[gwo] = ("ov", cwl)
                    b = 2 + (gwo % 2)
                    tw = bank_wait(b)
                    first = True
                    cmm = None
                    for k in range(32):
                        for (wsb, gsb) in ((s_wohi, s_gh), (s_wohi, s_gl), (s_wolo, s_gh)):
                            w_ap = wsb[:, k * 128:(k + 1) * 128]
                            g_ap = gsb[:, k * 512:(k + 1) * 512]
                            last = (k == 31 and wsb is s_wolo)
                            wts = (tw + [("ov", cwl), ("ghq", cgl)]) if first else []
                            cmm = p.emit("tensor", lambda e, w=w_ap, g=g_ap, b=b, f=first,
                                         l=last: e.matmul(BANKS[b][:, 0:512], w, g,
                                                          start=f, stop=l),
                                         waits=wts, inc=(("ot", 1) if last else None))
                            first = False
                    wo_mm_rel = ("ot", cmm)
                    cre = p.emit("vector", lambda e, oi=oi, b=b: e.tensor_add(
                        s_res[:, oi * 512:(oi + 1) * 512], BANKS[b][:, 0:512],
                        s_xc[:, oi * 512:(oi + 1) * 512]),
                        waits=[("ot", cmm), ("xcq", cxc)])
                    bank_rec[b] = ("ov", cre)
                    res_done[oi] = cre
                    csqr = p.emit("scalar", lambda e, oi=oi: e.activation(
                        s_sq[:], s_res[:, oi * 512:(oi + 1) * 512], AF.Square),
                        waits=[("ov", cre)])
                    wst = bank_wait(4) if oi == 0 else []
                    p.emit("tensor", lambda e, oi=oi: e.matmul(
                        BANKS[4][0:1, 0:512], s_onescol[:],
                        s_res[:, oi * 512:(oi + 1) * 512],
                        start=(oi == 0), stop=(oi == 15)),
                        waits=(wst + [("ov", cre)]))
                    cst2 = p.emit("tensor", lambda e, oi=oi: e.matmul(
                        BANKS[4][0:1, 512:1024] if False else BANKS[5][0:1, 0:512],
                        s_onescol[:], s_sq[:],
                        start=(oi == 0), stop=(oi == 15)),
                        waits=(bank_wait(5) if oi == 0 else []) + [("oc", csqr)],
                        inc=(("ot", 1) if oi == 15 else None))
                cm1 = p.emit("vector", lambda e: e.tensor_scalar_mul(
                    s_st[0:1, 0:512], BANKS[4][0:1, 0:512], 1.0 / 2048.0),
                    waits=[("ot", cst2)])
                cm2 = p.emit("vector", lambda e: e.tensor_scalar_mul(
                    s_st[0:1, 512:1024], BANKS[5][0:1, 0:512], 1.0 / 2048.0))
                bank_rec[4] = ("ov", cm2)
                bank_rec[5] = ("ov", cm2)
                cm3 = p.emit("vector", lambda e: e.tensor_tensor(
                    s_st[0:1, 1024:1536], s_st[0:1, 0:512], s_st[0:1, 0:512], ALU.mult))
                cm4 = p.emit("vector", lambda e: e.scalar_tensor_tensor(
                    s_st[0:1, 512:1024], s_st[0:1, 512:1024], 1e-5,
                    s_st[0:1, 1024:1536], ALU.add, ALU.subtract))
                cm5 = p.emit("scalar", lambda e: e.activation(
                    s_st[0:1, 1024:1536], s_st[0:1, 512:1024], AF.Sqrt),
                    waits=[("ov", cm4)])
                cm6 = p.emit("vector", lambda e: e.reciprocal(
                    s_st[0:1, 1536:2048], s_st[0:1, 1024:1536]), waits=[("oc", cm5)])
                cb1 = p.emit("tensor", lambda e: e.matmul(
                    BANKS[6][:, 0:512], s_onesr[:], s_st[0:1, 0:512],
                    start=True, stop=True), waits=bank_wait(6) + [("ov", cm6)],
                    inc=("ot", 1))
                cb1v = p.emit("vector", lambda e: e.tensor_copy(s_mb[:], BANKS[6][:, 0:512]),
                              waits=[("ot", cb1)])
                bank_rec[6] = ("ov", cb1v)
                cb2 = p.emit("tensor", lambda e: e.matmul(
                    BANKS[7][:, 0:512], s_onesr[:], s_st[0:1, 1536:2048],
                    start=True, stop=True), waits=bank_wait(7) + [("ov", cm6)],
                    inc=("ot", 1))
                cb2v = p.emit("vector", lambda e: e.tensor_copy(s_rb[:], BANKS[7][:, 0:512]),
                              waits=[("ot", cb2)])
                bank_rec[7] = ("ov", cb2v)
                for oi in range(16):
                    wy1 = [("ov", cb2v)] + ([("y1q", last_bounce)] if last_bounce else [])
                    cy1 = p.emit("vector", lambda e, oi=oi: e.tensor_sub(
                        s_y1t[:], s_res[:, oi * 512:(oi + 1) * 512], s_mb[:]),
                        waits=wy1)
                    cy2 = p.emit("vector", lambda e: e.tensor_mul(
                        s_y1t[:], s_y1t[:], s_rb[:]))
                    cspl = p.emit("sync", lambda e, oi=oi, t0=t0: e.dma_start(
                        out=y1v[:, oi, t0:t0 + 512], in_=s_y1t[:]),
                        waits=[("ov", cy2)], inc=("y1q", 16))
                    b2 = 2 + (oi % 2)
                    cfa = p.emit("tensor", lambda e, b2=b2: e.matmul(
                        BANKS[b2][:, 0:512], s_h128[:], s_y1t[:], start=True, stop=True),
                        waits=bank_wait(b2) + [("ov", cy2)], inc=("ot", 1))
                    cfv = p.emit("vector", lambda e, b2=b2: e.tensor_copy(
                        s_y1t[:], BANKS[b2][:, 0:512]), waits=[("ot", cfa), ("y1q", cspl)])
                    bank_rec[b2] = ("ov", cfv)
                    last_bounce = p.emit("sync", lambda e, oi=oi, t0=t0: e.dma_start(
                        out=y1fv[:, oi, t0:t0 + 512], in_=s_y1t[:]),
                        waits=[("ov", cfv)], inc=("y1q", 16))
                gh_rel[tg] = ("ot", p.cur("ot"))

            p.barrier([("y1q", p.cur("y1q")), ("ov", p.cur("ov")), ("oc", p.cur("oc")),
                       ("ot", p.cur("ot"))])
            stC.close()

        if UPTO >= 4:
            s_rw = sbuf("s_rw", [128, 4])
            s_mwc = sbuf("s_mwc", [128, 4])
            stD0 = ExitStack()
            s_wsl = sbuf("s_wsl", [128, 2 * 8192], st=stD0)
            s_wac = sbuf("s_wac", [128, 100], st=stD0)
            s_rwr = sbuf("s_rwr", [1, 8], st=stD0)
            wgv = wg_p.rearrange("(k p) e -> p k e", p=128)
            wuv = wu_p.rearrange("(k p) e -> p k e", p=128)
            wdv = wd_p.rearrange("(b p) e -> p b e", p=128)
            red_rec = {}
            gsl = 0
            for wi, (wv_, nslab, fl, acc0) in enumerate([(wgv, 16, 8192, 0),
                                                         (wuv, 16, 8192, 16),
                                                         (wdv, 64, 2048, 32)]):
                for sl_ in range(nslab):
                    bb = gsl % 2
                    wfw = [red_rec[gsl - 2]] if gsl >= 2 else []
                    cwd = p.emit("sync", lambda e, wv_=wv_, sl_=sl_, bb=bb, fl=fl:
                                 e.dma_start(out=s_wsl[:, bb * 8192:bb * 8192 + fl],
                                             in_=wv_[:, sl_, :]),
                                 waits=wfw, inc=(f"wmq{bb}", 16))
                    col = acc0 + (sl_ if wi < 2 else sl_)
                    cgr = p.emit("vector", lambda e, bb=bb, fl=fl, col=col:
                                 e.reduce_sum(s_wac[:, col:col + 1].unsqueeze(2),
                                              s_wsl[:, bb * 8192:bb * 8192 + fl].unsqueeze(1),
                                              AX.X, apply_absolute_value=True),
                                 waits=[(f"wmq{bb}", cwd)])
                    red_rec[gsl] = ("ov", cgr)
                    gsl += 1
                nin = nslab
                cfold = p.emit("vector", lambda e, acc0=acc0, nin=nin, wi=wi: e.reduce_sum(
                    s_wac[:, 96 + wi:97 + wi].unsqueeze(2),
                    s_wac[:, acc0:acc0 + nin].unsqueeze(1), AX.X),
                    waits=[red_rec[gsl - 1]])
                cT = p.emit("tensor", lambda e, wi=wi: e.matmul(
                    BANKS[0][0:1, wi:wi + 1], s_onescol[:], s_wac[:, 96 + wi:97 + wi],
                    start=True, stop=True),
                    waits=bank_wait(0) + [("ov", cfold)], inc=("ot", 1))
                cmn = p.emit("vector", lambda e, wi=wi: e.tensor_scalar(
                    s_rwr[0:1, wi:wi + 1], BANKS[0][0:1, wi:wi + 1],
                    1.0 / 16777216.0, 1e-5, ALU.mult, ALU.max),
                    waits=[("ot", cT)])
                bank_rec[0] = ("ov", cmn)
                crc = p.emit("vector", lambda e, wi=wi: e.reciprocal(
                    s_rwr[0:1, 4 + wi:5 + wi], s_rwr[0:1, wi:wi + 1]))
            cT = p.emit("tensor", lambda e: e.matmul(
                BANKS[1][:, 0:3], s_onesr[:], s_rwr[0:1, 0:3], start=True, stop=True),
                waits=bank_wait(1) + [("ov", p.cur("ov"))], inc=("ot", 1))
            cT2 = p.emit("tensor", lambda e: e.matmul(
                BANKS[1][:, 4:7], s_onesr[:], s_rwr[0:1, 4:7], start=True, stop=True),
                inc=("ot", 1))
            cv = p.emit("vector", lambda e: e.tensor_copy(s_mwc[:, 0:3], BANKS[1][:, 0:3]),
                        waits=[("ot", cT2)])
            cv2 = p.emit("vector", lambda e: e.tensor_copy(s_rw[:, 0:3], BANKS[1][:, 4:7]))
            bank_rec[1] = ("ov", cv2)
            p.barrier([("ov", p.cur("ov")), ("ot", p.cur("ot")), ("og", p.cur("og"))])
            stD0.close()

        if UPTO >= 5:
            stD1 = ExitStack()
            s_q1 = sbuf("s_q1", [128, 16 * 2048], BF16, st=stD1)
            s_mx = sbuf("s_mx", [128, 2048], st=stD1)
            s_row1 = sbuf("s_row1", [128, 2048], st=stD1)
            s_fg = sbuf("s_fg", [128, 2048], st=stD1)
            s_fu = sbuf("s_fu", [128, 2048], st=stD1)
            s_shin = sbuf("s_shin", [128, 2 * 512], st=stD1)
            s_abs1 = sbuf("s_abs1", [128, 512], st=stD1)
            s_isqd = sbuf("s_isqd", [128, 2], st=stD1)
            s_wgf = sbuf("s_wgf", [128, 16 * 512], st=stD1)
            s_wgt = sbuf("s_wgt", [128, 16 * 512], BF16, st=stD1)
            s_wut = sbuf("s_wut", [128, 16 * 512], BF16, st=stD1)
            s_gv = sbuf("s_gv", [128, 512], st=stD1)
            s_sg = sbuf("s_sg", [128, 512], st=stD1)
            s_uv = sbuf("s_uv", [128, 512], st=stD1)
            s_mid = sbuf("s_mid", [128, 512], st=stD1)
            s_sam4 = sbuf("s_sam4", [128, 4 * 512], st=stD1)


            def cross_part_max_to_rows(src_mx, dst_row, ntile, mxc_buf, mxT_buf, s_oh16):
                recs = []
                for j in range(ntile):
                    bk2 = 2 + j % 2
                    cT_ = p.emit("tensor", lambda e, j=j, bk2=bk2: e.transpose(
                        BANKS[bk2][:, 0:128], src_mx[:, j * 128:(j + 1) * 128], s_i128[:]),
                        waits=bank_wait(bk2) + ([("ov", p.cur("ov")),
                                                 ("oc", p.cur("oc"))] if j == 0 else []),
                        inc=("ot", 1))
                    cr_ = p.emit("vector", lambda e, j=j, bk2=bk2: e.reduce_max(
                        mxc_buf[:, j:j + 1].unsqueeze(2),
                        BANKS[bk2][:, 0:128].unsqueeze(1), AX.X),
                        waits=[("ot", cT_)])
                    bank_rec[bk2] = ("ov", cr_)
                cT2_ = p.emit("tensor", lambda e: e.transpose(
                    BANKS[4][0:ntile, 0:128], mxc_buf[:, 0:ntile], s_i128[:]),
                    waits=bank_wait(4) + [("ov", p.cur("ov"))], inc=("ot", 1))
                cv2_ = p.emit("vector", lambda e: e.tensor_copy(
                    mxT_buf[0:ntile, 0:128], BANKS[4][0:ntile, 0:128]),
                    waits=[("ot", cT2_)])
                bank_rec[4] = ("ov", cv2_)
                for j in range(ntile):
                    bk2 = 5 + j % 2
                    cT3_ = p.emit("tensor", lambda e, j=j, bk2=bk2: e.matmul(
                        BANKS[bk2][:, 0:128], s_oh16[0:ntile, j * 128:(j + 1) * 128],
                        mxT_buf[0:ntile, 0:128], start=True, stop=True),
                        waits=bank_wait(bk2) + ([("ov", cv2_)] if j <= 1 else []),
                        inc=("ot", 1))
                    cv3_ = p.emit("vector", lambda e, j=j, bk2=bk2: e.tensor_copy(
                        dst_row[:, j * 128:(j + 1) * 128], BANKS[bk2][:, 0:128]),
                        waits=[("ot", cT3_)])
                    bank_rec[bk2] = ("ov", cv3_)
                return p.cur("ov")

            seed2_c = nc.inline_tensor(np.array([[2048.0, 8192.0]], np.float32), name="seed2c").ap()
            s_sd2 = sbuf("s_sd2", [1, 2], st=stD1)
            cq = p.emit("sync", lambda e: e.dma_start(out=s_sd2[:], in_=seed2_c[:]),
                        inc=("qDc", 16))
            csq = p.emit("scalar", lambda e: e.activation(s_sd2[:], s_sd2[:], AF.Sqrt),
                         waits=[("qDc", cq)])
            crc = p.emit("vector", lambda e: e.reciprocal(s_sd2[:], s_sd2[:]),
                         waits=[("oc", csq)])
            cT = p.emit("tensor", lambda e: e.matmul(BANKS[0][:, 0:2], s_onesr[:],
                                                     s_sd2[:], start=True, stop=True),
                        waits=bank_wait(0) + [("ov", crc)], inc=("ot", 1))
            cv = p.emit("vector", lambda e: e.tensor_copy(s_isqd[:], BANKS[0][:, 0:2]),
                        waits=[("ot", cT)])
            bank_rec[0] = ("ov", cv)

            y1f_shv = y1f_sc.rearrange("c (a b) t -> (c a) (b t)", b=16)
            shin_carry = {}
            def sweep1_q1(tag, rnd):
                shin_rel = dict(shin_carry)
                prev_mx = None
                for m in range(64):
                    slot = m % 2
                    bk = 2 + m % 2
                    wsh = [shin_rel[m - 2]] if (m - 2) in shin_rel else []
                    cin = p.emit("sync", lambda e, m=m, slot=slot: e.dma_start(
                        out=s_shin[:, slot * 512:(slot + 1) * 512],
                        in_=y1f_shv[:, m * 512:(m + 1) * 512]),
                        waits=wsh, inc=(f"s1q{tag}{slot}", 16))
                    cmm = p.emit("tensor", lambda e, slot=slot, bk=bk: e.matmul(
                        BANKS[bk][:, 0:512], s_m1[:], s_shin[:, slot * 512:(slot + 1) * 512],
                        start=True, stop=True),
                        waits=bank_wait(bk) + [(f"s1q{tag}{slot}", cin)], inc=("ot", 1))
                    tb = m % 4
                    bblk = m // 4
                    if not rnd:
                        if bblk == 0:
                            cev = p.emit("scalar", lambda e, bk=bk, tb=tb: e.activation(
                                s_mx[:, tb * 512:(tb + 1) * 512], BANKS[bk][:, 0:512],
                                AF.Abs), waits=[("ot", cmm)])
                            cev = ("oc", cev)
                        else:
                            cab = p.emit("scalar", lambda e, bk=bk: e.activation(
                                s_abs1[:], BANKS[bk][:, 0:512], AF.Abs),
                                waits=[("ot", cmm)] + ([prev_mx] if prev_mx else []))
                            cev = p.emit("vector", lambda e, tb=tb: e.tensor_max(
                                s_mx[:, tb * 512:(tb + 1) * 512],
                                s_mx[:, tb * 512:(tb + 1) * 512], s_abs1[:]),
                                waits=[("oc", cab)])
                            cev = ("ov", cev)
                            prev_mx = cev
                    else:
                        cev = p.emit("vector", lambda e, bk=bk, tb=tb, m=m: e._custom_dve(
                            RND_TT, out=s_q1[:, m * 512:(m + 1) * 512],
                            in0=BANKS[bk][:, 0:512],
                            in1=s_row1[:, tb * 512:(tb + 1) * 512], s0=MAGIC),
                            waits=[("ot", cmm)])
                        cev = ("ov", cev)
                    bank_rec[bk] = cev
                    shin_rel[m] = ("ot", cmm)
                shin_carry.clear()
                shin_carry[-2] = shin_rel[62]
                shin_carry[-1] = shin_rel[63]
            sweep1_q1("a", False)
            s_mxc1 = sbuf("s_mxc1", [128, 16], st=stD1)
            s_mxT1 = sbuf("s_mxT1", [16, 128], st=stD1)
            s_oh16a = sbuf("s_oh16a", [16, 16 * 128], st=stD1)
            coh = p.emit("sync", lambda e: e.dma_start(out=s_oh16a[:], in_=oh16_c[:]),
                         inc=("qFc", 16))
            p.barrier([("qFc", p.cur("qFc"))])
            cpm = cross_part_max_to_rows(s_mx, s_row1, 16, s_mxc1, s_mxT1, s_oh16a)
            ca = p.emit("vector", lambda e: e.tensor_scalar(
                s_row1[:], s_row1[:], s_isqd[:, 0:1], 1e-5, ALU.mult, ALU.max),
                waits=[("ov", cpm)])
            cfg = p.emit("vector", lambda e: e.tensor_scalar(
                s_fg[:], s_row1[:], s_mwc[:, 0:1], 1.0 / 127.0, ALU.mult, ALU.mult))
            cfu = p.emit("vector", lambda e: e.tensor_scalar(
                s_fu[:], s_row1[:], s_mwc[:, 1:2], 1.0 / 127.0, ALU.mult, ALU.mult))
            crr = p.emit("vector", lambda e: e.reciprocal(s_row1[:], s_row1[:]))
            cs1 = p.emit("vector", lambda e: e.tensor_scalar(
                s_row1[:], s_row1[:], s_isqd[:, 0:1], 127.0, ALU.mult, ALU.mult))
            sweep1_q1("b", True)
            q1_done = p.cur("ov")

            samv = sam_sc.rearrange("c p t -> p c t")
            tern_rel = None
            mm_rel_gu = {}
            for os_ in range(16):
                wts = [tern_rel] if tern_rel else []
                cda = p.emit("sync", lambda e, os_=os_: e.dma_start(
                    out=s_wgf[:].rearrange("p (k e) -> p k e", k=16),
                    in_=wgv[:, :, os_ * 512:(os_ + 1) * 512]),
                    waits=wts + ([mm_rel_gu[os_ - 1]] if os_ >= 1 else []),
                    inc=("wgq", 16))
                cwgq = p.cur("wgq")
                ctg = p.emit("vector", lambda e: e._custom_dve(
                    TERN_TS, out=s_wgt[:], in0=s_wgf[:], s0=MAGIC, s1=s_rw[:, 0:1],
                    imm2=1.0), waits=[("wgq", cwgq), ("ov", q1_done)])
                cdu = p.emit("sync", lambda e, os_=os_: e.dma_start(
                    out=s_wgf[:].rearrange("p (k e) -> p k e", k=16),
                    in_=wuv[:, :, os_ * 512:(os_ + 1) * 512]),
                    waits=[("ov", ctg)], inc=("wgq", 16))
                cwuq = p.cur("wgq")
                ctu = p.emit("vector", lambda e: e._custom_dve(
                    TERN_TS, out=s_wut[:], in0=s_wgf[:], s0=MAGIC, s1=s_rw[:, 1:2],
                    imm2=1.0), waits=[("wgq", cwuq)])
                tern_rel = ("ov", ctu)
                for tb in range(4):
                    for osub in range(4):
                        it = tb * 4 + osub
                        bg = 4 + it % 2
                        bu = 6 + it % 2
                        cmg = cmu = None
                        for b16 in range(16):
                            wg_ = (bank_wait(bg) + [("ov", ctu)]) if b16 == 0 else []
                            cmg = p.emit("tensor", lambda e, b16=b16, osub=osub, tb=tb,
                                         bg=bg: e.matmul(
                                             BANKS[bg][:, 0:512],
                                             s_wgt[:, b16 * 512 + osub * 128: b16 * 512 + (osub + 1) * 128],
                                             s_q1[:, b16 * 2048 + tb * 512: b16 * 2048 + (tb + 1) * 512],
                                             start=(b16 == 0), stop=(b16 == 15)),
                                         waits=wg_, inc=(("ot", 1) if b16 == 15 else None))
                        for b16 in range(16):
                            wu_ = bank_wait(bu) if b16 == 0 else []
                            cmu = p.emit("tensor", lambda e, b16=b16, osub=osub, tb=tb,
                                         bu=bu: e.matmul(
                                             BANKS[bu][:, 0:512],
                                             s_wut[:, b16 * 512 + osub * 128: b16 * 512 + (osub + 1) * 128],
                                             s_q1[:, b16 * 2048 + tb * 512: b16 * 2048 + (tb + 1) * 512],
                                             start=(b16 == 0), stop=(b16 == 15)),
                                         waits=wu_, inc=(("ot", 1) if b16 == 15 else None))
                        cg_ = p.emit("vector", lambda e, bg=bg, tb=tb: e.tensor_tensor(
                            s_gv[:], BANKS[bg][:, 0:512], s_fg[:, tb * 512:(tb + 1) * 512],
                            ALU.mult), waits=[("ot", cmu)])
                        bank_rec[bg] = ("ov", cg_)
                        csg = p.emit("scalar", lambda e: e.activation(
                            s_sg[:], s_gv[:], AF.Silu), waits=[("ov", cg_)])
                        cu_ = p.emit("vector", lambda e, bu=bu, tb=tb: e.tensor_tensor(
                            s_uv[:], BANKS[bu][:, 0:512], s_fu[:, tb * 512:(tb + 1) * 512],
                            ALU.mult))
                        bank_rec[bu] = ("ov", cu_)
                        cmid = p.emit("vector", lambda e: e.tensor_mul(
                            s_mid[:], s_sg[:], s_uv[:]), waits=[("oc", csg)])
                        bA = 2 + it % 2
                        cfa = p.emit("tensor", lambda e, bA=bA: e.matmul(
                            BANKS[bA][:, 0:512], s_h128[:], s_mid[:], start=True, stop=True),
                            waits=bank_wait(bA) + [("ov", cmid)], inc=("ot", 1))
                        cfv = p.emit("vector", lambda e, bA=bA, osub=osub: e.tensor_copy(
                            s_sam4[:, osub * 512:(osub + 1) * 512], BANKS[bA][:, 0:512]),
                            waits=[("ot", cfa)] + ([("smq", smq_last)] if os_ + tb > 0 else []))
                        bank_rec[bA] = ("ov", cfv)
                    smq_last = p.emit("sync", lambda e, os_=os_, tb=tb: e.dma_start(
                        out=samv[:, os_ * 4:(os_ + 1) * 4, tb * 512:(tb + 1) * 512],
                        in_=s_sam4[:].rearrange("p (c t) -> p c t", c=4)),
                        waits=[("ov", p.cur("ov"))], inc=("smq", 16))
                mm_rel_gu[os_] = ("ot", p.cur("ot"))
            s_wdf = s_wgf
            s_wdt = s_wgt
            wdtv = wdt_sp.rearrange("(b p) e -> p b e", p=128)
            wd_rel = {}
            for b2 in range(64):
                slot = b2 % 2
                wts = [wd_rel[b2 - 2]] if b2 >= 2 else (
                    [tern_rel, mm_rel_gu[15]] if b2 == 0 else [])
                cdd = p.emit("sync", lambda e, b2=b2, slot=slot: e.dma_start(
                    out=s_wdf[:, slot * 2048:(slot + 1) * 2048], in_=wdv[:, b2, :]),
                    waits=wts, inc=(f"wdq{slot}", 16))
                ctd = p.emit("vector", lambda e, slot=slot: e._custom_dve(
                    TERN_TS, out=s_wdt[:, slot * 2048:(slot + 1) * 2048],
                    in0=s_wdf[:, slot * 2048:(slot + 1) * 2048], s0=MAGIC,
                    s1=s_rw[:, 2:3], imm2=1.0), waits=[(f"wdq{slot}", cdd)])
                cds = p.emit("sync", lambda e, b2=b2, slot=slot: e.dma_start(
                    out=wdtv[:, b2, :], in_=s_wdt[:, slot * 2048:(slot + 1) * 2048]),
                    waits=[("ov", ctd)], inc=(f"wdsq{slot}", 16))
                wd_rel[b2] = (f"wdsq{slot}", cds)
            p.barrier([("smq", p.cur("smq")), ("wdsq0", p.cur("wdsq0")),
                       ("wdsq1", p.cur("wdsq1")), ("ov", p.cur("ov")),
                       ("oc", p.cur("oc")), ("ot", p.cur("ot"))])
            stD1.close()

        if UPTO >= 6:
            stD2 = ExitStack()
            s_q2 = sbuf("s_q2", [128, 64 * 512], BF16, st=stD2)
            s_mx2 = sbuf("s_mx2", [128, 512], st=stD2)
            s_row2 = sbuf("s_row2", [128, 512], st=stD2)
            s_fd = sbuf("s_fd", [128, 512], st=stD2)
            s_shin2 = sbuf("s_shin2", [128, 2 * 2 * 512], st=stD2)
            s_mxc2 = sbuf("s_mxc2", [128, 16], st=stD2)
            s_mxT2 = sbuf("s_mxT2", [16, 128], st=stD2)
            s_oh16b = sbuf("s_oh16b", [16, 4 * 128], st=stD2)
            s_wdts = sbuf("s_wdts", [128, 2 * 64 * 128], BF16, st=stD2)
            s_y1c = sbuf("s_y1c", [128, 16 * 512], st=stD2)
            s_res2 = sbuf("s_res2", [128, 16 * 512], st=stD2)
            s_sq2 = sbuf("s_sq2", [128, 512], st=stD2)
            s_st2 = sbuf("s_st2", [1, 3 * 512], st=stD2)
            s_mb2 = sbuf("s_mb2", [128, 512], st=stD2)
            s_rb2 = sbuf("s_rb2", [128, 512], st=stD2)
            s_ot = sbuf("s_ot", [128, 512], st=stD2)
            s_abs2 = s_ot
            s_isq2 = sbuf("s_isq2", [128, 2], st=stD2)

            seed3_c = nc.inline_tensor(np.array([[2048.0, 8192.0]], np.float32), name="seed3c").ap()
            s_sd3 = sbuf("s_sd3", [1, 2], st=stD2)
            coh2 = p.emit("sync", lambda e: e.dma_start(out=s_oh16b[:], in_=oh16_c[:, 0:512]),
                          inc=("qGc", 16))
            p.barrier([("qGc", p.cur("qGc"))])
            cq = p.emit("sync", lambda e: e.dma_start(out=s_sd3[:], in_=seed3_c[:]),
                        inc=("qEc", 16))
            csq = p.emit("scalar", lambda e: e.activation(s_sd3[:], s_sd3[:], AF.Sqrt),
                         waits=[("qEc", cq)])
            crc = p.emit("vector", lambda e: e.reciprocal(s_sd3[:], s_sd3[:]),
                         waits=[("oc", csq)])
            cT = p.emit("tensor", lambda e: e.matmul(BANKS[0][:, 0:2], s_onesr[:],
                                                     s_sd3[:], start=True, stop=True),
                        waits=bank_wait(0) + [("ov", crc)], inc=("ot", 1))
            cv = p.emit("vector", lambda e: e.tensor_copy(s_isq2[:], BANKS[0][:, 0:2]),
                        waits=[("ot", cT)])
            bank_rec[0] = ("ov", cv)

            last_out = None
            sh2_carry = {}
            sam_shv = sam_sc.rearrange("c (a b) t -> (c a) (b t)", b=64)
            y1v2 = y1_sp.rearrange("(k p) t -> p k t", p=128)
            ofv = out_fm.rearrange("(k p) t -> p k t", p=128)
            for qt in range(4):
                q0 = qt * 512
                prev_mx2 = None
                sam_3d = sam_shv[:].rearrange("p (b t) -> p b t", b=64)
                for rnd in (False, True):
                    tag = "r" if rnd else "m"
                    sh_rel = dict(sh2_carry)
                    for g in range(32):
                        slot = g % 2
                        wsh = [sh_rel[g - 2]] if (g - 2) in sh_rel else []
                        cin = p.emit("sync", lambda e, g=g, slot=slot, q0=q0: e.dma_start(
                            out=s_shin2[:, slot * 1024:(slot + 1) * 1024]
                            .rearrange("p (j t) -> p j t", j=2),
                            in_=sam_3d[:, g * 2:(g + 1) * 2, q0:q0 + 512]),
                            waits=wsh, inc=(f"s2q{tag}{qt}{slot}", 16))
                        cmm_g = None
                        for j in range(2):
                            b2 = g * 2 + j
                            bk = 2 + b2 % 2
                            cmm = p.emit("tensor", lambda e, slot=slot, j=j, bk=bk: e.matmul(
                                BANKS[bk][:, 0:512], s_m2[:],
                                s_shin2[:, slot * 1024 + j * 512: slot * 1024 + (j + 1) * 512],
                                start=True, stop=True),
                                waits=bank_wait(bk) +
                                      ([(f"s2q{tag}{qt}{slot}", cin)] if j == 0 else []),
                                inc=("ot", 1))
                            cmm_g = cmm
                            if not rnd:
                                if b2 == 0:
                                    cev = p.emit("scalar", lambda e, bk=bk: e.activation(
                                        s_mx2[:], BANKS[bk][:, 0:512], AF.Abs),
                                        waits=[("ot", cmm)])
                                    cev = ("oc", cev)
                                else:
                                    wab = [("ot", cmm)]
                                    if prev_mx2:
                                        wab.append(prev_mx2)
                                    elif last_out:
                                        wab.append(("qO", last_out))
                                    cab = p.emit("scalar", lambda e, bk=bk: e.activation(
                                        s_abs2[:], BANKS[bk][:, 0:512], AF.Abs),
                                        waits=wab)
                                    cev = p.emit("vector", lambda e: e.tensor_max(
                                        s_mx2[:], s_mx2[:], s_abs2[:]),
                                        waits=[("oc", cab)])
                                    cev = ("ov", cev)
                                    prev_mx2 = cev
                            else:
                                cev = p.emit("vector", lambda e, bk=bk, b2=b2: e._custom_dve(
                                    RND_TT, out=s_q2[:, b2 * 512:(b2 + 1) * 512],
                                    in0=BANKS[bk][:, 0:512], in1=s_row2[:], s0=MAGIC),
                                    waits=[("ot", cmm)])
                                cev = ("ov", cev)
                            bank_rec[bk] = cev
                        sh_rel[g] = ("ot", cmm_g)
                    sh2_carry.clear()
                    sh2_carry[-2] = sh_rel[30]
                    sh2_carry[-1] = sh_rel[31]
                    if not rnd:
                        cpm2 = cross_part_max_to_rows(s_mx2, s_row2, 4, s_mxc2, s_mxT2, s_oh16b)
                        ca = p.emit("vector", lambda e: e.tensor_scalar(
                            s_row2[:], s_row2[:], s_isq2[:, 1:2], 1e-5, ALU.mult, ALU.max),
                            waits=[("ov", cpm2)])
                        cfd = p.emit("vector", lambda e: e.tensor_scalar(
                            s_fd[:], s_row2[:], s_mwc[:, 2:3], 1.0 / 127.0, ALU.mult,
                            ALU.mult))
                        crr = p.emit("vector", lambda e: e.reciprocal(s_row2[:], s_row2[:]))
                        cs2 = p.emit("vector", lambda e: e.tensor_scalar(
                            s_row2[:], s_row2[:], s_isq2[:, 1:2], 127.0, ALU.mult, ALU.mult))
                q2_done = p.cur("ov")
                cy1q = p.emit("sync", lambda e, q0=q0: e.dma_start(
                    out=s_y1c[:].rearrange("p (k t) -> p k t", k=16),
                    in_=y1v2[:, :, q0:q0 + 512]), inc=(f"yq2{qt}", 16))
                wslab_rel = {}
                for oi in range(16):
                    slot = oi % 2
                    wts = [wslab_rel[oi - 2]] if oi >= 2 else []
                    cwd2 = p.emit("sync", lambda e, oi=oi, slot=slot: e.dma_start(
                        out=s_wdts[:, slot * 8192:(slot + 1) * 8192]
                        .rearrange("p (b e) -> p b e", b=64),
                        in_=wdtv[:, :, oi * 128:(oi + 1) * 128]),
                        waits=wts, inc=(f"wd2{qt}{slot}", 16))
                    bk = 4 + oi % 2
                    cmm = None
                    for b2 in range(64):
                        wmm = (bank_wait(bk) + [(f"wd2{qt}{slot}", cwd2),
                                                ("ov", q2_done)]) if b2 == 0 else []
                        cmm = p.emit("tensor", lambda e, b2=b2, bk=bk, slot=slot: e.matmul(
                            BANKS[bk][:, 0:512],
                            s_wdts[:, slot * 8192 + b2 * 128: slot * 8192 + (b2 + 1) * 128],
                            s_q2[:, b2 * 512:(b2 + 1) * 512],
                            start=(b2 == 0), stop=(b2 == 63)),
                            waits=wmm, inc=(("ot", 1) if b2 == 63 else None))
                    wslab_rel[oi] = ("ot", cmm)
                    cr1 = p.emit("vector", lambda e, bk=bk: e.tensor_mul(
                        s_ot[:], BANKS[bk][:, 0:512], s_fd[:]),
                        waits=[("ot", cmm), (f"yq2{qt}", cy1q)] +
                              ([("qO", last_out)] if last_out else []))
                    cr2 = p.emit("vector", lambda e, oi=oi: e.tensor_add(
                        s_res2[:, oi * 512:(oi + 1) * 512], s_ot[:],
                        s_y1c[:, oi * 512:(oi + 1) * 512]))
                    bank_rec[bk] = ("ov", cr2)
                    csqr = p.emit("scalar", lambda e, oi=oi: e.activation(
                        s_sq2[:], s_res2[:, oi * 512:(oi + 1) * 512], AF.Square),
                        waits=[("ov", cr2)])
                    p.emit("tensor", lambda e, oi=oi: e.matmul(
                        BANKS[6][0:1, 0:512], s_onescol[:],
                        s_res2[:, oi * 512:(oi + 1) * 512],
                        start=(oi == 0), stop=(oi == 15)),
                        waits=(bank_wait(6) if oi == 0 else []) + [("ov", cr2)])
                    cst2 = p.emit("tensor", lambda e, oi=oi: e.matmul(
                        BANKS[7][0:1, 0:512], s_onescol[:], s_sq2[:],
                        start=(oi == 0), stop=(oi == 15)),
                        waits=(bank_wait(7) if oi == 0 else []) + [("oc", csqr)],
                        inc=(("ot", 1) if oi == 15 else None))
                cm1 = p.emit("vector", lambda e: e.tensor_scalar_mul(
                    s_st2[0:1, 0:512], BANKS[6][0:1, 0:512], 1.0 / 2048.0),
                    waits=[("ot", cst2)])
                cm2 = p.emit("vector", lambda e: e.tensor_scalar_mul(
                    s_st2[0:1, 512:1024], BANKS[7][0:1, 0:512], 1.0 / 2048.0))
                bank_rec[6] = ("ov", cm2)
                bank_rec[7] = ("ov", cm2)
                cm3 = p.emit("vector", lambda e: e.tensor_tensor(
                    s_st2[0:1, 1024:1536], s_st2[0:1, 0:512], s_st2[0:1, 0:512], ALU.mult))
                cm4 = p.emit("vector", lambda e: e.scalar_tensor_tensor(
                    s_st2[0:1, 512:1024], s_st2[0:1, 512:1024], 1e-5,
                    s_st2[0:1, 1024:1536], ALU.add, ALU.subtract))
                cm5 = p.emit("scalar", lambda e: e.activation(
                    s_st2[0:1, 1024:1536], s_st2[0:1, 512:1024], AF.Sqrt),
                    waits=[("ov", cm4)])
                cm6 = p.emit("vector", lambda e: e.reciprocal(
                    s_st2[0:1, 512:1024], s_st2[0:1, 1024:1536]), waits=[("oc", cm5)])
                cb1 = p.emit("tensor", lambda e: e.matmul(
                    BANKS[2][:, 0:512], s_onesr[:], s_st2[0:1, 0:512],
                    start=True, stop=True), waits=bank_wait(2) + [("ov", cm6)],
                    inc=("ot", 1))
                cb1v = p.emit("vector", lambda e: e.tensor_copy(s_mb2[:], BANKS[2][:, 0:512]),
                              waits=[("ot", cb1)])
                bank_rec[2] = ("ov", cb1v)
                cb2 = p.emit("tensor", lambda e: e.matmul(
                    BANKS[3][:, 0:512], s_onesr[:], s_st2[0:1, 512:1024],
                    start=True, stop=True), waits=bank_wait(3) + [("ov", cm6)],
                    inc=("ot", 1))
                cb2v = p.emit("vector", lambda e: e.tensor_copy(s_rb2[:], BANKS[3][:, 0:512]),
                              waits=[("ot", cb2)])
                bank_rec[3] = ("ov", cb2v)
                for oi in range(16):
                    co1 = p.emit("vector", lambda e, oi=oi: e.tensor_sub(
                        s_ot[:], s_res2[:, oi * 512:(oi + 1) * 512], s_mb2[:]),
                        waits=[("ov", cb2v)] + ([("qO", last_out)] if last_out else []))
                    co2 = p.emit("vector", lambda e: e.tensor_mul(s_ot[:], s_ot[:], s_rb2[:]))
                    last_out = p.emit("sync", lambda e, oi=oi, q0=q0: e.dma_start(
                        out=ofv[:, oi, q0:q0 + 512], in_=s_ot[:]),
                        waits=[("ov", co2)], inc=("qO", 16))
            stD2.close()

        s_dbg = sbuf("s_dbg", [128, 2 * TT])
        cdbg = p.emit("sync", lambda e: e.dma_start(
            out=s_dbg[:, 0:TOWN], in_=y1_sp[0:128, :]), inc=("qD1", 16))
        cdbg2 = p.emit("sync", lambda e: e.dma_start(
            out=s_dbg[:, TT:TT + TOWN], in_=y1f_sc[0, :, :]), inc=("qD2", 16))
        p.emit("sync", lambda e: e.dma_start(out=dbg[:, 0:TOWN],
               in_=s_dbg[:, 0:TOWN]),
               waits=[("qD1", cdbg), ("qD2", cdbg2)], inc=("qD3", 16))
        p.emit("sync", lambda e: e.dma_start(out=dbg[:, TT:TT + TOWN],
               in_=s_dbg[:, TT:TT + TOWN]), inc=("qD3", 16))
        p.build_block(block, final_waits=[(n, p.cur(n)) for n in list(p.counts.keys())])
    return nc


def _shard_inputs(L0_attn_input_, L0_attn_in_proj_w, L0_attn_conv1d_weight,
                  L0_attn_conv1d_bias, L0_attn_z_bias, L0_attn_D, L0_attn_out_proj_w,
                  L0_ffn_w_gate, L0_ffn_w_up, L0_ffn_w_down):
    X = np.ascontiguousarray(np.asarray(L0_attn_input_, dtype=np.float32)[0])
    wt_in = np.ascontiguousarray(np.asarray(L0_attn_in_proj_w, np.float32).T)
    convw = np.ascontiguousarray(
        np.asarray(L0_attn_conv1d_weight, np.float32).reshape(CONVD, 4)
        .reshape(40, 128, 4).transpose(1, 0, 2).reshape(128, 160))
    convb = np.ascontiguousarray(
        np.asarray(L0_attn_conv1d_bias, np.float32).reshape(40, 128).T)
    zbias = np.ascontiguousarray(np.asarray(L0_attn_z_bias, np.float32).reshape(32, 128).T)
    dv = np.asarray(L0_attn_D, np.float32).reshape(1, 64)
    wt_out = np.ascontiguousarray(np.asarray(L0_attn_out_proj_w, np.float32).T)

    perm1 = np.empty(2048, np.int64)
    for b in range(16):
        for c in range(16):
            for a in range(8):
                perm1[b * 128 + c * 8 + a] = c * 128 + a * 16 + b
    perm2 = np.empty(8192, np.int64)
    for b2 in range(64):
        for c in range(64):
            for a2 in range(2):
                perm2[b2 * 128 + c * 2 + a2] = c * 128 + a2 * 64 + b2
    wg = np.ascontiguousarray(np.asarray(L0_ffn_w_gate, np.float32).T[perm1, :])
    wu = np.ascontiguousarray(np.asarray(L0_ffn_w_up, np.float32).T[perm1, :])
    wd = np.ascontiguousarray(np.asarray(L0_ffn_w_down, np.float32).T[perm2, :])

    in_maps = []
    for k in range(NCORE):
        lo = k * TOWN - HALO
        sl = np.zeros((TT, DM), np.float32)
        src0 = max(lo, 0)
        sl[TT - (k * TOWN + TOWN - src0):] = X[src0: (k + 1) * TOWN]
        x_fmk = np.ascontiguousarray(sl.T)
        hm = np.full((128, 1), 0.0 if k == 0 else 1.0, np.float32)
        in_maps.append({"x_fm": x_fmk, "wt_in": wt_in, "convw": convw, "convb": convb,
                        "zb": zbias, "dvec": dv, "wt_out": wt_out,
                        "wg_p": wg, "wu_p": wu, "wd_p": wd, "hm_fm": hm})
    return in_maps


_NC_CACHE = {}

def _get_nc():
    if "nc" not in _NC_CACHE:
        nc = build_program()
        lower_extended_insts(nc)
        _NC_CACHE["nc"] = nc
    return _NC_CACHE["nc"]


def kernel(**inputs):
    in_maps = _shard_inputs(**inputs)
    nc = _get_nc()
    res = run_bass_kernel_spmd(nc, in_maps, core_ids=list(range(NCORE)))
    out = np.empty((16384, 2048), np.float32)
    for k in range(NCORE):
        out[k * TOWN:(k + 1) * TOWN, :] = np.asarray(res.results[k]["out_fm"]).T
    return out.reshape(1, 16384, 2048)


# version 07
# speedup vs baseline: 1.0956x; 1.0283x over previous
import numpy as np
from contextlib import ExitStack

import concourse.bass as bass
import concourse.mybir as mybir
from concourse.bass_utils import run_bass_kernel_spmd
from concourse.library_overlay import lower_extended_insts
from concourse.bass_isa import ReduceOp
from concourse.dve_spec import Spec, Src0, Src1, C0, C1, C2, Zero, maxx, minn
from concourse.dve_op import DveOp
from concourse import dve_ops as _dvo
import concourse.dve_spec as _dsp
from concourse.library_config import all_libraries

F32 = mybir.dt.float32
BF16 = mybir.dt.bfloat16
AF = mybir.ActivationFunctionType
ALU = mybir.AluOpType
AX = mybir.AxisListType

MAGIC = 12582912.0

def _register_op(name, spec, ref):
    if name in _dvo._SUB_OPCODE_FOR_NAME:
        return next(o for o in _dvo.OPS if o.name == name)
    lowered = _dsp.lower(spec, ver="v3", perf_en=False, name=name)
    op = DveOp(name, spec, subdim=False, uops_sha={"v3": lowered.sha("v3")},
               reference=ref)
    row = max(_dvo._SUB_OPCODE_FOR_NAME.values()) + 1
    assert row < 0x20
    _dvo.OPS.append(op)
    _dvo._SUB_OPCODE_FOR_NAME[op.name] = row
    _dvo.CUSTOM_DVE_SPECS[op.name] = spec
    _dvo.CUSTOM_DVE_REFS[op.name] = ref
    return op

RND_TT = _register_op(
    "RND_TT_ANT", Spec(body=(Src0 * Src1 + C0) - C0),
    lambda in0, in1, s0, s1, imm2: np.rint(in0 * in1))
TERN_TS = _register_op(
    "TERN_TS_ANT", Spec(body=minn(maxx(((Src0 * C1) + C0) - C0, Zero - C2), C2)),
    lambda in0, in1, s0, s1, imm2: np.minimum(np.maximum(np.rint(in0 * s1), -imm2), imm2))

def _hadamard(n):
    H = np.ones((1, 1), np.float32)
    while H.shape[0] < n:
        H = np.block([[H, H], [H, -H]])
    return H

H128 = _hadamard(128)
H16 = _hadamard(16)
H64 = _hadamard(64)
M1 = np.zeros((128, 128), np.float32)
for c in range(16):
    for a in range(8):
        for cp in range(16):
            M1[c * 8 + a, cp * 8 + a] = H16[c, cp]
M2 = np.zeros((128, 128), np.float32)
for c in range(64):
    for a in range(2):
        for cp in range(64):
            M2[c * 2 + a, cp * 2 + a] = H64[c, cp]
L1K = np.triu(np.ones((128, 128), np.float32), 0)
PEN = np.tril(np.ones((128, 128), np.float32), -1) * (-10000.0)
I128 = np.eye(128, dtype=np.float32)
OH16 = np.zeros((16, 16 * 128), np.float32)
for j in range(16):
    OH16[j, j * 128:(j + 1) * 128] = 1.0
OH64 = np.zeros((64, 64 * 128), np.float32)
for h in range(64):
    OH64[h, h * 128:(h + 1) * 128] = 1.0

NCORE = 8
TOWN = 2048
HALO = 256
TT = TOWN + HALO
NCH = TT // 128
NOWNCH = TOWN // 128
DM = 2048
CONVD = 5120
DI = 4096
DS = 64
NG = 8
NH = 64
PH = 64
FFI = 8192


class P:
    ENGINES = ("sync", "gpsimd", "tensor", "vector", "scalar")
    CHAIN = {"vector": "ov", "scalar": "oc"}

    def __init__(self, nc, stack):
        self.nc = nc
        self.plans = {e: [] for e in self.ENGINES}
        self.sems = {}
        self.counts = {}
        self.stack = stack

    def sem(self, name):
        if name not in self.sems:
            self.sems[name] = self.stack.enter_context(self.nc.semaphore(name))
            self.counts[name] = 0
        return self.sems[name]

    def emit(self, eng, fn, waits=(), inc=None):
        w = list(waits)
        if eng in self.CHAIN:
            assert inc is None
            sname = self.CHAIN[eng]
            self.sem(sname)
            if self.counts[sname] > 0:
                w.append((sname, self.counts[sname]))
            self.counts[sname] += 1
            inc = (sname, 1)
            out_count = self.counts[sname]
        elif inc is not None:
            sname, amt = inc
            self.sem(sname)
            self.counts[sname] += amt
            out_count = self.counts[sname]
        else:
            out_count = None
        for sname2, _v in w:
            self.sem(sname2)
        self.plans[eng].append((w, fn, inc))
        return out_count

    def cur(self, sname):
        self.sem(sname)
        return self.counts[sname]

    def barrier(self, waits):
        for eng in self.ENGINES:
            if eng in self.CHAIN:
                self.emit(eng, lambda e: e.nop(),
                          waits=[x for x in waits if x[0] != self.CHAIN[eng]])
            else:
                self.emit(eng, lambda e: e.nop(), waits=list(waits))

    def build_block(self, block, final_waits):
        for eng_name in self.ENGINES:
            plan = self.plans[eng_name]
            dec = getattr(block, eng_name)

            def mk(eng_name, plan):
                def body(eng):
                    for waits, fn, inc in plan:
                        for sname, v in waits:
                            eng.wait_ge(self.sems[sname], v)
                        r = fn(eng)
                        if inc is not None:
                            r.then_inc(self.sems[inc[0]], inc[1])
                    if eng_name == "sync":
                        for sname, v in final_waits:
                            eng.wait_ge(self.sems[sname], v)
                return body
            dec(mk(eng_name, plan))


def build_program(upto_phase=None):
    import os
    UPTO = int(os.environ.get("K_UPTO", "9")) if upto_phase is None else upto_phase
    BSUB = int(os.environ.get("K_BSUB", "9"))
    nc = bass.Bass("TRN2", target_bir_lowering=False, debug=False)

    x_fm = nc.dram_tensor("x_fm", [DM, TT], F32, kind="ExternalInput").ap()
    wt_in = nc.dram_tensor("wt_in", [DM, 9280], F32, kind="ExternalInput").ap()
    convw = nc.dram_tensor("convw", [128, 40 * 4], F32, kind="ExternalInput").ap()
    convb = nc.dram_tensor("convb", [128, 40], F32, kind="ExternalInput").ap()
    zb = nc.dram_tensor("zb", [128, 32], F32, kind="ExternalInput").ap()
    dvec = nc.dram_tensor("dvec", [1, 64], F32, kind="ExternalInput").ap()
    wt_out = nc.dram_tensor("wt_out", [DI, DM], F32, kind="ExternalInput").ap()
    wg_p = nc.dram_tensor("wg_p", [DM, FFI], F32, kind="ExternalInput").ap()
    wu_p = nc.dram_tensor("wu_p", [DM, FFI], F32, kind="ExternalInput").ap()
    wd_p = nc.dram_tensor("wd_p", [FFI, DM], F32, kind="ExternalInput").ap()
    hm_fm = nc.dram_tensor("hm_fm", [128, 1], F32, kind="ExternalInput").ap()
    out_fm = nc.dram_tensor("out_fm", [DM, TOWN], F32, kind="ExternalOutput").ap()

    xbc_sp = nc.dram_tensor("xbc_sp", [CONVD, TT], F32, kind="Internal").ap()
    z_sp = nc.dram_tensor("z_sp", [DI, TT], F32, kind="Internal").ap()
    ghi_sp = nc.dram_tensor("ghi_sp", [DI, TOWN], BF16, kind="Internal").ap()
    glo_sp = nc.dram_tensor("glo_sp", [DI, TOWN], BF16, kind="Internal").ap()
    y1_sp = nc.dram_tensor("y1_sp", [DM, TOWN], F32, kind="Internal").ap()
    y1f_sc = nc.dram_tensor("y1f_sc", [16, 128, TOWN], F32, kind="Internal").ap()
    sam_sc = nc.dram_tensor("sam_sc", [64, 128, TOWN], F32, kind="Internal").ap()
    wdt_sp = nc.dram_tensor("wdt_sp", [FFI, DM], BF16, kind="Internal").ap()
    dbg = nc.dram_tensor("dbg", [128, 2 * TT], F32, kind="ExternalOutput").ap()

    h128_c = nc.inline_tensor(H128, name="h128c").ap()
    m1_c = nc.inline_tensor(M1, name="m1c").ap()
    m2_c = nc.inline_tensor(M2, name="m2c").ap()
    l1_c = nc.inline_tensor(L1K, name="l1c").ap()
    pen_c = nc.inline_tensor(PEN, name="penc").ap()
    i128_c = nc.inline_tensor(I128, name="i128c").ap()
    oh_c = nc.inline_tensor(OH64, name="ohc").ap()
    noh_c = nc.inline_tensor(-OH64, name="nohc").ap()
    oh16_c = nc.inline_tensor(OH16, name="oh16c").ap()
    onesr_c = nc.inline_tensor(np.ones((1, 128), np.float32), name="onesrc").ap()
    onescol_c = nc.inline_tensor(np.ones((128, 1), np.float32), name="onescolc").ap()

    with ExitStack() as stack:
        def sbuf(name, shape, dtype=F32, st=None):
            return (st or stack).enter_context(nc.sbuf_tensor(name, shape, dtype))
        def psum(name, shape, dtype=F32, st=None):
            return (st or stack).enter_context(nc.psum_tensor(name, shape, dtype))

        p = P(nc, stack)
        block = stack.enter_context(nc.Block())

        s_h128 = sbuf("s_h128", [128, 128])
        s_m1 = sbuf("s_m1", [128, 128])
        s_m2 = sbuf("s_m2", [128, 128])
        s_l1 = sbuf("s_l1", [128, 128])
        s_pen_bf = sbuf("s_pen_bf", [128, 128], BF16)
        s_i128 = sbuf("s_i128", [128, 128])
        s_i128bf = sbuf("s_i128bf", [128, 128], BF16)
        s_onesr = sbuf("s_onesr", [1, 128])
        s_onesr_bf = sbuf("s_onesr_bf", [1, 128], BF16)
        s_onescol = sbuf("s_onescol", [128, 1])
        s_convw = sbuf("s_convw", [128, 40 * 4])
        s_convb = sbuf("s_convb", [128, 40])
        s_zb = sbuf("s_zb", [128, 32])
        s_dvec = sbuf("s_dvec", [1, 64])
        s_alog = sbuf("s_alog", [64, TT])
        BANKS = [psum(f"pbk{i}", [128, 512]) for i in range(8)]
        pA = BANKS[:4]

        for src, dst in [(h128_c, s_h128), (m1_c, s_m1), (m2_c, s_m2), (l1_c, s_l1),
                         (i128_c, s_i128), (onesr_c, s_onesr), (onescol_c, s_onescol),
                         (convw, s_convw), (convb, s_convb), (zb, s_zb), (dvec, s_dvec)]:
            p.emit("sync", lambda e, s=src, d=dst: e.dma_start(out=d[:], in_=s[:]),
                   inc=("qC", 16))
        for src, dst in [(pen_c, s_pen_bf), (i128_c, s_i128bf), (onesr_c, s_onesr_bf)]:
            p.emit("gpsimd", lambda e, s=src, d=dst: e.dma_start(out=d[:], in_=s[:]),
                   inc=("qCg", 16))
        consts_qS = p.cur("qC")
        consts_qG = p.cur("qCg")
        p.barrier([("qC", consts_qS), ("qCg", consts_qG)])

        stA = ExitStack()
        EW = 128
        HLEN0, HOFF1, HLEN1 = 1152, 1149, 1155
        HL = max(HLEN0, HLEN1)
        s_xhi = sbuf("s_xhi", [128, 16 * HL], BF16, st=stA)
        s_xlo = sbuf("s_xlo", [128, 16 * HL], BF16, st=stA)
        s_xf = sbuf("s_xf", [128, 2 * HL], st=stA)
        s_wf = sbuf("s_wf", [128, 4 * 16 * EW], st=stA)
        s_whi = sbuf("s_whi", [128, 4 * 16 * EW], BF16, st=stA)
        s_wlo = sbuf("s_wlo", [128, 4 * 16 * EW], BF16, st=stA)
        s_acc = sbuf("s_acc", [128, 2 * HL], st=stA)
        s_cv = sbuf("s_cv", [128, HL], st=stA)
        s_out = sbuf("s_out", [128, 2 * HL], st=stA)

        xv = x_fm.rearrange("(k p) t -> p k t", p=128)
        wv = wt_in.rearrange("(k p) e -> p k e", p=128)
        ET = [("xbc", i) for i in range(40)] + [("z", i) for i in range(32)] + [("a", 0)]
        grp_seq = 0
        spq_count = {0: 0, 1: 0}

        last_half_ov = 0
        evac_rec = {}
        wsplit_done = {}
        for th, (toff, tlen) in enumerate([(0, HLEN0), (HOFF1, HLEN1)]):
            xsplit_done = {}
            for k in range(16):
                buf = k % 2
                waits = [("ov", xsplit_done[k - 2])] if k >= 2 else (
                    [("ov", last_half_ov)] if th else [])
                c_q = p.emit("sync", lambda e, k=k, b=buf, toff=toff, tlen=tlen: e.dma_start(
                    out=s_xf[:, b * HL:b * HL + tlen], in_=xv[:, k, toff:toff + tlen]),
                    waits=waits, inc=(f"xq{buf}", 16))
                p.emit("vector", lambda e, k=k, b=buf, tlen=tlen: e.tensor_copy(
                    s_xhi[:, k * HL:k * HL + tlen], s_xf[:, b * HL:b * HL + tlen]),
                    waits=[(f"xq{buf}", c_q)])
                xsplit_done[k] = p.emit("vector", lambda e, k=k, b=buf, tlen=tlen:
                    e.scalar_tensor_tensor(
                        s_xlo[:, k * HL:k * HL + tlen], s_xhi[:, k * HL:k * HL + tlen],
                        -1.0, s_xf[:, b * HL:b * HL + tlen], ALU.mult, ALU.add))
            ngw = [512] * (tlen // 512) + ([tlen % 512] if tlen % 512 else [])
            wsplit_done = wsplit_done if th else {}

            for ei, (kind, idx) in enumerate(ET):
                gei = th * len(ET) + ei
                if kind == "xbc":
                    e0, ewid = idx * 128, 128
                elif kind == "z":
                    e0, ewid = 5120 + idx * 128, 128
                else:
                    e0, ewid = 9216, 64
                wb = gei % 4
                wbase = wb * 16 * EW
                waits_wf = [("ov", wsplit_done[gei - 4])] if gei >= 4 else []
                waits_wb = [("ot", 3 * (gei - 3))] if gei >= 4 else []
                cS = p.emit("sync", lambda e, e0=e0, ewid=ewid, wbase=wbase: e.dma_start(
                    out=s_wf[:, wbase:wbase + 16 * ewid].rearrange("p (k e) -> p k e", k=16),
                    in_=wv[:, :, e0:e0 + ewid]), waits=waits_wf, inc=(f"wfq{wb}", 16))
                cG = p.emit("gpsimd", lambda e, e0=e0, ewid=ewid, wbase=wbase: e.dma_start(
                    out=s_whi[:, wbase:wbase + 16 * ewid].rearrange("p (k e) -> p k e", k=16),
                    in_=wv[:, :, e0:e0 + ewid]), waits=waits_wb, inc=(f"whq{wb}", 16))
                cWrd = p.emit("vector", lambda e, ewid=ewid, wbase=wbase: e.scalar_tensor_tensor(
                    s_wlo[:, wbase:wbase + 16 * ewid], s_whi[:, wbase:wbase + 16 * ewid],
                    -1.0, s_wf[:, wbase:wbase + 16 * ewid], ALU.mult, ALU.add),
                    waits=[(f"wfq{wb}", cS), (f"whq{wb}", cG)])
                wsplit_done[gei] = cWrd

                t0 = 0
                for tg, tw in enumerate(ngw):
                    bank = grp_seq % 4
                    waits_mm = [("ov", cWrd)]
                    if grp_seq >= 4:
                        waits_mm.append(evac_rec[grp_seq - 4])
                    first = True
                    for k in range(16):
                        for pi, (wsb, xsb) in enumerate(((s_whi, s_xhi), (s_whi, s_xlo),
                                                         (s_wlo, s_xhi))):
                            w_ap = wsb[:, wbase + k * ewid: wbase + (k + 1) * ewid]
                            x_ap = xsb[:, k * HL + t0: k * HL + t0 + tw]
                            last = (k == 15 and pi == 2)
                            teA_c = p.emit("tensor", lambda e, w=w_ap, x=x_ap, b=bank,
                                   ewid=ewid, tw=tw, f=first, l=last:
                                   e.matmul(pA[b][0:ewid, 0:tw], w, x, start=f, stop=l),
                                   waits=(waits_mm if first else []),
                                   inc=(("ot", 1) if last else None))
                            first = False
                    ob = (gei % 2) * HL
                    if kind == "a":
                        cv_ = p.emit("vector", lambda e, b=bank, t0=t0, tw=tw, toff=toff:
                               e.tensor_copy(s_alog[:, toff + t0: toff + t0 + tw],
                                             pA[b][0:64, 0:tw]),
                               waits=[("ot", teA_c)])
                        evac_rec[grp_seq] = ("ov", cv_)
                    elif kind == "z":
                        wz = [("ot", teA_c)]
                        if spq_count[gei % 2] > 0:
                            wz.append((f"spq{gei % 2}", 16 * spq_count[gei % 2]))
                        cs_ = p.emit("scalar", lambda e, b=bank, t0=t0, tw=tw, ob=ob, idx=idx:
                               e.activation(s_out[:, ob + t0: ob + t0 + tw], pA[b][:, 0:tw],
                                            AF.Silu, bias=s_zb[:, idx:idx + 1]),
                               waits=wz)
                        evac_rec[grp_seq] = ("oc", cs_)
                    else:
                        cv_ = p.emit("vector", lambda e, b=bank, t0=t0, tw=tw, ob=ob:
                               e.tensor_copy(s_acc[:, ob + t0: ob + t0 + tw],
                                             pA[b][:, 0:tw]),
                               waits=[("ot", teA_c)])
                        evac_rec[grp_seq] = ("ov", cv_)
                    t0 += tw
                    grp_seq += 1

                evA_after = [evac_rec[g] for g in range(grp_seq - len(ngw), grp_seq)]
                ob = (gei % 2) * HL
                sp0 = 0 if th == 0 else HLEN0
                lcl0 = 0 if th == 0 else 3
                splen = tlen - lcl0
                if kind == "xbc":
                    w3 = s_convw[:, idx * 4 + 3: idx * 4 + 4]
                    p.emit("vector", lambda e, ob=ob, w3=w3, tlen=tlen: e.tensor_scalar(
                        s_cv[:, 0:tlen], s_acc[:, ob:ob + tlen], w3, None, ALU.mult),
                        waits=list(evA_after))
                    cv_last = None
                    for kk, sh in ((2, 1), (1, 2), (0, 3)):
                        wk = s_convw[:, idx * 4 + kk: idx * 4 + kk + 1]
                        cv_last = p.emit("vector", lambda e, ob=ob, wk=wk, sh=sh, tlen=tlen:
                               e.scalar_tensor_tensor(
                                   s_cv[:, sh:tlen], s_acc[:, ob: ob + tlen - sh], wk,
                                   s_cv[:, sh:tlen], ALU.mult, ALU.add))
                    wsl = [("ov", cv_last)]
                    if spq_count[gei % 2] > 0:
                        wsl.append((f"spq{gei % 2}", 16 * spq_count[gei % 2]))
                    cs = p.emit("scalar", lambda e, ob=ob, idx=idx, tlen=tlen: e.activation(
                        s_out[:, ob:ob + tlen], s_cv[:, 0:tlen], AF.Silu,
                        bias=s_convb[:, idx:idx + 1]),
                        waits=wsl)
                    p.emit("sync", lambda e, ob=ob, idx=idx, sp0=sp0, lcl0=lcl0,
                           splen=splen: e.dma_start(
                        out=xbc_sp[idx * 128:(idx + 1) * 128, sp0:sp0 + splen],
                        in_=s_out[:, ob + lcl0: ob + lcl0 + splen]),
                        waits=[("oc", cs)], inc=(f"spq{gei % 2}", 16))
                    spq_count[gei % 2] += 1
                elif kind == "z":
                    p.emit("sync", lambda e, ob=ob, idx=idx, sp0=sp0, lcl0=lcl0,
                           splen=splen: e.dma_start(
                        out=z_sp[idx * 128:(idx + 1) * 128, sp0:sp0 + splen],
                        in_=s_out[:, ob + lcl0: ob + lcl0 + splen]),
                        waits=list(evA_after), inc=(f"spq{gei % 2}", 16))
                    spq_count[gei % 2] += 1

            last_half_ov = p.cur("ov")
        p.barrier([("spq0", p.cur("spq0")), ("spq1", p.cur("spq1")),
                   ("ov", p.cur("ov")), ("oc", p.cur("oc")), ("ot", p.cur("ot"))])
        stA.close()

        if UPTO >= 2:
            stB = ExitStack()
            s_dtT = sbuf("s_dtT", [128, NCH * 64], st=stB)
            s_DdT = sbuf("s_DdT", [128, NCH * 64], st=stB)
            s_dstT = sbuf("s_dstT", [128, NCH * 64], st=stB)
            s_ahi = sbuf("s_ahi", [64, NCH * 128], BF16, st=stB)
            s_alo = sbuf("s_alo", [64, NCH * 128], BF16, st=stB)
            s_nahi = sbuf("s_nahi", [64, NCH * 128], BF16, st=stB)
            s_nalo = sbuf("s_nalo", [64, NCH * 128], BF16, st=stB)
            s_ash = sbuf("s_ash", [128, NCH * 64], st=stB)
            s_dout = sbuf("s_dout", [128, NCH * 64], st=stB)
            s_d32 = sbuf("s_d32", [128, NCH * 32], st=stB)
            s_dbc = sbuf("s_dbc", [128, 64], st=stB)
            s_ohbf = sbuf("s_ohbf", [64, 64 * 128], BF16, st=stB)
            s_spk = sbuf("s_spk", [128, 32 * 64], st=stB)
            s_u1 = sbuf("s_u1", [128, 128], st=stB)
            s_hm = sbuf("s_hm", [128, 1], st=stB)

            stPre = ExitStack()
            s_prb = sbuf("s_prb", [128, NCH * 64], st=stPre)
            s_dtr = sbuf("s_dtr", [1, NCH * 64], st=stPre)
            s_ahsf = sbuf("s_ahsf", [64, NCH * 128], st=stPre)
            bank_rec = {}

            def bank_wait(i):
                return [bank_rec[i]] if i in bank_rec else []

            cg1 = p.emit("gpsimd", lambda e: e.dma_start(out=s_ohbf[:], in_=oh_c[:]),
                         inc=("qBg", 16))
            p.emit("gpsimd", lambda e: e.dma_start(out=s_hm[:], in_=hm_fm[:]),
                   inc=("qBg", 16))
            p.emit("vector", lambda e: e.tensor_scalar(s_u1[:], s_l1[:], -1.0, 1.0,
                                                       ALU.mult, ALU.add))
            p.emit("vector", lambda e: e.memset(s_spk[:], 0.0))
            p.barrier([("qBg", p.cur("qBg"))])

            for c in range(NCH):
                b = c % 2
                tw = bank_wait(b)
                cT = p.emit("tensor", lambda e, c=c, b=b: e.transpose(
                    BANKS[b][:, 0:64], s_alog[:, c * 128:(c + 1) * 128],
                    s_i128[0:64, 0:64]), waits=tw, inc=("ot", 1))
                cv = p.emit("vector", lambda e, c=c, b=b: e.tensor_copy(
                    s_prb[:, c * 64:(c + 1) * 64], BANKS[b][:, 0:64]),
                    waits=[("ot", cT)])
                bank_rec[b] = ("ov", cv)
            c1 = p.emit("scalar", lambda e: e.activation(s_dtT[:], s_prb[:], AF.Abs),
                        waits=[("ov", p.cur("ov"))])
            c2 = p.emit("scalar", lambda e: e.activation(s_dtT[:], s_dtT[:], AF.Exp, scale=-1.0))
            c3 = p.emit("vector", lambda e: e.tensor_scalar_add(s_dtT[:], s_dtT[:], 1.0),
                        waits=[("oc", c2)])
            c4 = p.emit("scalar", lambda e: e.activation(s_dtT[:], s_dtT[:], AF.Ln),
                        waits=[("ov", c3)])
            c5 = p.emit("scalar", lambda e: e.activation(s_DdT[:], s_prb[:], AF.Relu))
            c6 = p.emit("vector", lambda e: e.tensor_add(s_dtT[:], s_dtT[:], s_DdT[:]),
                        waits=[("oc", c5), ("oc", c4)])
            c7 = p.emit("vector", lambda e: e.reciprocal(s_DdT[:], s_dtT[:]))
            cT = p.emit("tensor", lambda e: e.matmul(BANKS[0][:, 0:64], s_onesr[:],
                                                     s_dvec[:], start=True, stop=True),
                        waits=bank_wait(0), inc=("ot", 1))
            cv = p.emit("vector", lambda e: e.tensor_copy(s_dbc[:], BANKS[0][:, 0:64]),
                        waits=[("ot", cT)])
            bank_rec[0] = ("ov", cv)
            c8 = p.emit("vector", lambda e: e.tensor_tensor(
                s_DdT[:].rearrange("p (c h) -> p c h", c=NCH),
                s_DdT[:].rearrange("p (c h) -> p c h", c=NCH),
                s_dbc[:].unsqueeze(1).broadcast_to([128, NCH, 64]), ALU.mult))
            c9 = p.emit("vector", lambda e: e.tensor_scalar_mul(s_prb[:], s_dtT[:], -1.0))
            for c in range(NCH):
                b0, b1 = c % 2, 2 + (c % 2)
                ndt = s_prb[:, c * 64:(c + 1) * 64]
                w0 = bank_wait(b0) + ([("ov", c9)] if c <= 1 else [])
                cT0 = p.emit("tensor", lambda e, ndt=ndt, b0=b0: e.matmul(
                    BANKS[b0][0:64, 0:128], ndt, s_l1[:], start=True, stop=True),
                    waits=w0, inc=("ot", 1))
                cv0 = p.emit("vector", lambda e, c=c, b0=b0: e.tensor_copy(
                    s_ahsf[:, c * 128:(c + 1) * 128], BANKS[b0][0:64, 0:128]),
                    waits=[("ot", cT0)])
                bank_rec[b0] = ("ov", cv0)
                w1 = bank_wait(b1) + ([("ov", c9)] if c <= 1 else [])
                cT1 = p.emit("tensor", lambda e, ndt=ndt, b1=b1: e.matmul(
                    BANKS[b1][:, 0:64], s_l1[:], ndt, start=True, stop=True),
                    waits=w1, inc=("ot", 1))
                cT2 = p.emit("tensor", lambda e, ndt=ndt, b1=b1: e.matmul(
                    BANKS[b1][:, 64:128], s_u1[:], ndt, start=True, stop=True),
                    inc=("ot", 1))
                cv1 = p.emit("vector", lambda e, c=c, b1=b1: e.tensor_copy(
                    s_ash[:, c * 64:(c + 1) * 64], BANKS[b1][:, 0:64]),
                    waits=[("ot", cT2)])
                cv2 = p.emit("vector", lambda e, c=c, b1=b1: e.tensor_copy(
                    s_dstT[:, c * 64:(c + 1) * 64], BANKS[b1][:, 64:128]))
                bank_rec[b1] = ("ov", cv2)
            cdd = p.emit("scalar", lambda e: e.activation(s_dstT[:], s_dstT[:], AF.Exp),
                         waits=[("ov", p.cur("ov"))])
            cd1 = p.emit("scalar", lambda e: e.activation(s_dout[:], s_ash[:], AF.Exp))
            for j in range(3):
                b = j % 2
                tw = bank_wait(b)
                cT = p.emit("tensor", lambda e, j=j, b=b: e.matmul(
                    BANKS[b][0:1, 0:384], s_onescol[:], s_prb[:, j * 384:(j + 1) * 384],
                    start=True, stop=True), waits=tw, inc=("ot", 1))
                cv = p.emit("vector", lambda e, j=j, b=b: e.tensor_copy(
                    s_dtr[:, j * 384:(j + 1) * 384], BANKS[b][0:1, 0:384]),
                    waits=[("ot", cT)])
                bank_rec[b] = ("ov", cv)
            cd2 = p.emit("scalar", lambda e: e.activation(s_dtr[:], s_dtr[:], AF.Exp),
                         waits=[("ov", p.cur("ov"))])
            ch1 = p.emit("vector", lambda e: e.tensor_copy(s_ahi[:], s_ahsf[:]))
            ch2 = p.emit("vector", lambda e: e.scalar_tensor_tensor(
                s_alo[:], s_ahi[:], -1.0, s_ahsf[:], ALU.mult, ALU.add))
            ch3 = p.emit("vector", lambda e: e.tensor_scalar_mul(s_nahi[:], s_ahi[:], -1.0))
            ch4 = p.emit("vector", lambda e: e.tensor_scalar_mul(s_nalo[:], s_alo[:], -1.0))
            for c in range(NCH):
                b = c % 2
                tw = bank_wait(b) + ([("oc", cd2)] if c <= 1 else [])
                r0 = s_dtr[:, c * 64:(c + 1) * 64].rearrange("o (g m) -> o g m", g=8)
                cT0 = p.emit("tensor", lambda e, r0=r0, b=b: e.matmul(
                    BANKS[b][0:64, 0:32], s_onesr[:, 0:64], r0[:, 0::2, :],
                    start=True, stop=True), waits=tw, inc=("ot", 1))
                cT1 = p.emit("tensor", lambda e, r0=r0, b=b: e.matmul(
                    BANKS[b][64:128, 0:32], s_onesr[:, 0:64], r0[:, 1::2, :], start=True,
                    stop=True, tile_position=(0, 64)), inc=("ot", 1))
                cv = p.emit("vector", lambda e, c=c, b=b: e.tensor_copy(
                    s_d32[:, c * 32:(c + 1) * 32], BANKS[b][:, 0:32]), waits=[("ot", cT1)])
                bank_rec[b] = ("ov", cv)
            p.barrier([("ov", p.cur("ov")), ("oc", p.cur("oc")), ("ot", p.cur("ot"))])
            stPre.close()

            s_xbcr = sbuf("s_xbcr", [128, 2 * 40 * 128], st=stB)
            s_zr = sbuf("s_zr", [128, 32 * 128], st=stB)
            s_xtm = sbuf("s_xtm", [128, 32 * 128], st=stB)
            s_btm = sbuf("s_btm", [128, 4 * 128], st=stB)
            s_xd = sbuf("s_xd", [128, 32 * 128], st=stB)
            s_sct = sbuf("s_sct", [128, 8 * 128], st=stB)
            s_lmq = sbuf("s_lmq", [128, 2 * 512], st=stB)
            s_lsq = sbuf("s_lsq", [128, 2 * 512], st=stB)
            s_tmp1 = sbuf("s_tmp1", [128, 512], st=stB)
            s_yf = sbuf("s_yf", [128, 2 * 512], st=stB)
            s_ghi = sbuf("s_ghi", [128, 32 * 128], BF16, st=stB)
            s_glo = sbuf("s_glo", [128, 32 * 128], BF16, st=stB)

            xbv = xbc_sp.rearrange("(i p) t -> p i t", p=128)
            zv = z_sp.rearrange("(i p) t -> p i t", p=128)
            gh_v = ghi_sp.rearrange("(i p) t -> p i t", p=128)
            gl_v = glo_sp.rearrange("(i p) t -> p i t", p=128)
            xb_rel = {}
            z_rel = {}
            gat_rel = {}
            gsp_cnt = 0
            for c in range(NCH):
                own = c >= 2
                xsl_ = (c % 2) * 5120
                xb = s_xbcr[:, xsl_:xsl_ + 5120]
                zr = s_zr[:, 0:4096]
                wx = [xb_rel[c - 2]] if c >= 2 else []
                xbs = f"xbq{c % 2}"
                cxq = p.emit("sync", lambda e, c=c, xb=xb: e.dma_start(
                    out=xb.rearrange("p (i t) -> p i t", i=40),
                    in_=xbv[:, :, c * 128:(c + 1) * 128]),
                    waits=wx, inc=(xbs, 16))
                if own:
                    wz = [z_rel[c - 1]] if (c - 1) in z_rel else []
                    czq = p.emit("sync", lambda e, c=c, zr=zr: e.dma_start(
                        out=zr.rearrange("p (i t) -> p i t", i=32),
                        in_=zv[:, :, c * 128:(c + 1) * 128]),
                        waits=wz, inc=("zbq", 16))

                if c < 2:
                    chm = p.emit("vector", lambda e, xb=xb: e.tensor_scalar_mul(
                        xb, xb, s_hm[:, 0:1]), waits=[(xbs, cxq)])
                    xwait = [("ov", chm)]
                else:
                    xwait = [(xbs, cxq)]

                for grp in range(9 if BSUB >= 2 else 0):
                    b = grp % 2
                    tw = bank_wait(b)
                    cT = None
                    for j in range(4):
                        i = grp * 4 + j
                        src = xb[:, i * 128:(i + 1) * 128]
                        cT = p.emit("tensor", lambda e, src=src, b=b, j=j: e.transpose(
                            BANKS[b][:, j * 128:(j + 1) * 128], src, s_i128[:]),
                            waits=((tw + xwait) if j == 0 else []),
                            inc=(("ot", 1) if j == 3 else None))
                    dst = (s_xtm[:, grp * 512:(grp + 1) * 512] if grp < 8
                           else s_btm[:, 0:512])
                    cv = p.emit("vector", lambda e, dst=dst, b=b: e.tensor_copy(
                        dst, BANKS[b][:, 0:512]), waits=[("ot", cT)])
                    bank_rec[b] = ("ov", cv)
                xtm_done = p.cur("ov")

                if BSUB < 2:
                    xb_rel[c] = (xbs, cxq)
                    if own:
                        z_rel[c] = ("zbq", czq)
                    continue
                cxd = p.emit("vector", lambda e, c=c: e.tensor_tensor(
                    s_xd[:].rearrange("p (h q) -> p h q", h=64),
                    s_xtm[:].rearrange("p (h q) -> p h q", h=64),
                    s_dstT[:, c * 64:(c + 1) * 64].unsqueeze(2).broadcast_to([128, 64, 64]),
                    ALU.mult))

                if own and BSUB >= 3:
                    for par in range(2):
                        b = 2 + par
                        tw = bank_wait(b)
                        cT = None
                        for j in range(4):
                            ti_b = 32 + j
                            ti_c = 36 + j
                            pr = par * 64
                            lb = xb[pr:pr + 64, ti_b * 128:(ti_b + 1) * 128]
                            rc_ = xb[pr:pr + 64, ti_c * 128:(ti_c + 1) * 128]
                            cT = p.emit("tensor", lambda e, lb=lb, rc_=rc_, b=b, j=j:
                                        e.matmul(BANKS[b][:, j * 128:(j + 1) * 128], lb, rc_,
                                                 start=True, stop=True),
                                        waits=(tw if j == 0 else []),
                                        inc=(("ot", 1) if j == 3 else None))
                        dst = s_sct[:].rearrange("p (j q) -> p j q", j=4)[
                            :, :, par * 128:(par + 1) * 128]
                        cv = p.emit("vector", lambda e, dst=dst, b=b: e.tensor_copy(
                            dst, BANKS[b][:, 0:512].rearrange("p (j s) -> p j s", j=4)),
                            waits=[("ot", cT)])
                        bank_rec[b] = ("ov", cv)
                    sct_done = p.cur("ov")

                    for q in range(16 if BSUB >= 4 else 0):
                        qb = q % 2
                        b = qb
                        tw = bank_wait(b)
                        cT = None
                        for j in range(4):
                            h = q * 4 + j
                            sl = slice(j * 128, (j + 1) * 128)
                            ohs = s_ohbf[:, h * 128:(h + 1) * 128]
                            ahi = s_ahi[:, c * 128:(c + 1) * 128]
                            alo = s_alo[:, c * 128:(c + 1) * 128]
                            nahi = s_nahi[:, c * 128:(c + 1) * 128]
                            nalo = s_nalo[:, c * 128:(c + 1) * 128]
                            p.emit("tensor", lambda e, ohs=ohs, ahi=ahi, b=b, sl=sl:
                                   e.matmul(BANKS[b][:, sl], ohs, ahi, start=True, stop=False),
                                   waits=(tw if j == 0 else []))
                            p.emit("tensor", lambda e, ohs=ohs, alo=alo, b=b, sl=sl:
                                   e.matmul(BANKS[b][:, sl], ohs, alo, start=False, stop=False))
                            p.emit("tensor", lambda e, ohs=ohs, nahi=nahi, b=b, sl=sl:
                                   e.matmul(BANKS[b][:, sl], nahi, ohs, start=False, stop=False))
                            p.emit("tensor", lambda e, ohs=ohs, nalo=nalo, b=b, sl=sl:
                                   e.matmul(BANKS[b][:, sl], nalo, ohs, start=False, stop=False))
                            cT = p.emit("tensor", lambda e, b=b, sl=sl:
                                        e.matmul(BANKS[b][:, sl], s_i128bf[:], s_pen_bf[:],
                                                 start=False, stop=True),
                                        inc=(("ot", 1) if j == 3 else None))
                        lq = (q % 2) * 512
                        cs = p.emit("scalar", lambda e, b=b, lq=lq: e.activation(
                            s_lmq[:, lq:lq + 512], BANKS[b][:, 0:512], AF.Exp),
                            waits=[("ot", cT)])
                        bank_rec[b] = ("oc", cs)
                        g = q >> 1
                        cls = p.emit("vector", lambda e, g=g, lq=lq: e.tensor_tensor(
                            s_lsq[:, lq:lq + 512].rearrange("p (j s) -> p j s", j=4),
                            s_lmq[:, lq:lq + 512].rearrange("p (j s) -> p j s", j=4),
                            s_sct[:, g * 128:(g + 1) * 128].unsqueeze(1)
                            .broadcast_to([128, 4, 128]), ALU.mult),
                            waits=[("oc", cs), ("ov", sct_done)])

                        yb = 4 + qb
                        ob_ = 6 + qb
                        for j in range(4):
                            h = q * 4 + j
                            lsl = s_lsq[:, lq + j * 128: lq + (j + 1) * 128]
                            xsl = s_xtm[:, h * 64:(h + 1) * 64]
                            w0 = (bank_wait(yb) + [("ov", cls)]) if j == 0 else []
                            p.emit("tensor", lambda e, lsl=lsl, xsl=xsl, yb=yb, j=j:
                                   e.matmul(BANKS[yb][:, j * 64:(j + 1) * 64],
                                            lsl, xsl, start=True, stop=True),
                                   waits=w0)
                        for j in range(4):
                            h = q * 4 + j
                            g2 = h >> 3
                            ti_c = 36 + g2 // 2
                            pr = (g2 % 2) * 64
                            csl = xb[pr:pr + 64, ti_c * 128:(ti_c + 1) * 128]
                            jj = 8 * (g2 >> 1) + (h % 8)
                            hrow = (g2 % 2) * 64
                            ssl = s_spk[hrow:hrow + 64, jj * 64:(jj + 1) * 64]
                            w1 = bank_wait(ob_) if j == 0 else []
                            cyo = p.emit("tensor", lambda e, csl=csl, ssl=ssl, ob_=ob_, j=j:
                                   e.matmul(BANKS[ob_][:, j * 64:(j + 1) * 64], csl, ssl,
                                            start=True, stop=True),
                                   waits=w1, inc=(("ot", 1) if j == 3 else None))
                        dv_ = s_dout[:, c * 64 + q * 4: c * 64 + (q + 1) * 4]
                        ddv = s_DdT[:, c * 64 + q * 4: c * 64 + (q + 1) * 4]
                        c1_ = p.emit("vector", lambda e, ob_=ob_, dv_=dv_: e.tensor_tensor(
                            s_tmp1[:, 0:256].rearrange("p (j q2) -> p j q2", j=4),
                            BANKS[ob_][:, 0:256].rearrange("p (j q2) -> p j q2", j=4),
                            dv_.unsqueeze(2).broadcast_to([128, 4, 64]), ALU.mult),
                            waits=[("ot", cyo)])
                        yq = (q % 2) * 512
                        c2_ = p.emit("vector", lambda e, yb=yb, yq=yq: e.tensor_add(
                            s_yf[:, yq: yq + 256], s_tmp1[:, 0:256], BANKS[yb][:, 0:256]))
                        bank_rec[yb] = ("ov", c2_)
                        bank_rec[ob_] = ("ov", c2_)
                        c3_ = p.emit("vector", lambda e, q=q, ddv=ddv: e.tensor_tensor(
                            s_tmp1[:, 0:256].rearrange("p (j q2) -> p j q2", j=4),
                            s_xtm[:, q * 256:(q + 1) * 256].rearrange("p (j q2) -> p j q2", j=4),
                            ddv.unsqueeze(2).broadcast_to([128, 4, 64]), ALU.mult))
                        c4_ = p.emit("vector", lambda e, yq=yq: e.tensor_add(
                            s_yf[:, yq: yq + 256], s_yf[:, yq: yq + 256], s_tmp1[:, 0:256]))
                        gb2 = 2 + qb
                        tw2 = bank_wait(gb2)
                        cTg = None
                        for j in range(2):
                            src = s_yf[:, yq + j * 128: yq + (j + 1) * 128]
                            cTg = p.emit("tensor", lambda e, src=src, gb2=gb2, j=j:
                                         e.transpose(BANKS[gb2][:, j * 128:(j + 1) * 128],
                                                     src, s_i128[:]),
                                         waits=((tw2 + [("ov", c4_)]) if j == 0 else []),
                                         inc=(("ot", 1) if j == 1 else None))
                        cvg = p.emit("vector", lambda e, q=q, gb2=gb2, zr=zr: e.tensor_tensor(
                            zr[:, q * 256:(q + 1) * 256], BANKS[gb2][:, 0:256],
                            zr[:, q * 256:(q + 1) * 256], ALU.mult),
                            waits=[("ot", cTg), ("zbq", czq)])
                        bank_rec[gb2] = ("ov", cvg)
                    z_rel[c] = ("ov", p.cur("ov"))
                if own and BSUB >= 5:

                    wgs = [] if gsp_cnt == 0 else [("gsq", 16 * gsp_cnt)]
                    cg1_ = p.emit("vector", lambda e, zr=zr: e.tensor_copy(s_ghi[:], zr),
                                  waits=wgs)
                    cg2_ = p.emit("vector", lambda e, zr=zr: e.scalar_tensor_tensor(
                        s_glo[:], s_ghi[:], -1.0, zr, ALU.mult, ALU.add))
                    p.emit("sync", lambda e, c=c: e.dma_start(
                        out=gh_v[:, :, (c - 2) * 128:(c - 1) * 128],
                        in_=s_ghi[:].rearrange("p (i t) -> p i t", i=32)),
                        waits=[("ov", cg2_)], inc=("gsq", 16))
                    p.emit("sync", lambda e, c=c: e.dma_start(
                        out=gl_v[:, :, (c - 2) * 128:(c - 1) * 128],
                        in_=s_glo[:].rearrange("p (i t) -> p i t", i=32)),
                        inc=("gsq", 16))
                    gsp_cnt += 2
                    z_rel[c] = ("ov", cg2_)

                for sb in range(4):
                    b = [2, 3, 4, 5][sb] if own else [2, 3, 4, 5][sb]
                    tw = bank_wait(b)
                    cst = None
                    for j8 in range(8):
                        jj = sb * 8 + j8
                        for hh in range(2):
                            h = 16 * (jj >> 3) + (jj % 8) + hh * 8
                            g = h >> 3
                            bsl = s_btm[:, g * 64:(g + 1) * 64]
                            xdsl = s_xd[:, h * 64:(h + 1) * 64]
                            pr = hh * 64
                            lastst = (j8 == 7 and hh == 1)
                            w0 = (tw + [("ov", cxd)]) if (j8 == 0 and hh == 0) else []
                            kw = {"tile_position": (0, 64)} if pr else {}
                            cst = p.emit("tensor", lambda e, bsl=bsl, xdsl=xdsl, b=b, j8=j8,
                                   pr=pr, kw=kw: e.matmul(
                                       BANKS[b][pr:pr + 64, j8 * 64:(j8 + 1) * 64],
                                       bsl, xdsl, start=True, stop=True, **kw),
                                   waits=w0, inc=(("ot", 1) if lastst else None))
                    cvu = None
                    for j8 in range(8):
                        jj = sb * 8 + j8
                        dcol = s_d32[:, c * 32 + jj: c * 32 + jj + 1]
                        cvu = p.emit("vector", lambda e, jj=jj, b=b, j8=j8, dcol=dcol:
                                     e.scalar_tensor_tensor(
                                         s_spk[:, jj * 64:(jj + 1) * 64],
                                         s_spk[:, jj * 64:(jj + 1) * 64], dcol,
                                         BANKS[b][:, j8 * 64:(j8 + 1) * 64],
                                         ALU.mult, ALU.add),
                                     waits=([("ot", cst)] if j8 == 0 else []))
                    bank_rec[b] = ("ov", cvu)
                if own and c not in z_rel:
                    z_rel[c] = ("zbq", czq)
                xb_rel[c] = ("ot", p.cur("ot"))

            p.barrier([("gsq", p.cur("gsq")),
                       ("ov", p.cur("ov")), ("oc", p.cur("oc")), ("ot", p.cur("ot"))])
            stB.close()

        if UPTO >= 3:
            stC = ExitStack()
            s_gh = sbuf("s_gh", [128, 32 * 512], BF16, st=stC)
            s_gl = sbuf("s_gl", [128, 32 * 512], BF16, st=stC)
            s_wof = sbuf("s_wof", [128, 2 * 32 * 128], st=stC)
            s_wohi = sbuf("s_wohi", [128, 32 * 128], BF16, st=stC)
            s_wolo = sbuf("s_wolo", [128, 32 * 128], BF16, st=stC)
            s_xc = sbuf("s_xc", [128, 16 * 512], st=stC)
            s_res = sbuf("s_res", [128, 16 * 512], st=stC)
            s_sq = sbuf("s_sq", [128, 512], st=stC)
            s_st = sbuf("s_st", [1, 4 * 512], st=stC)
            s_mb = sbuf("s_mb", [128, 512], st=stC)
            s_rb = sbuf("s_rb", [128, 512], st=stC)
            s_y1t = sbuf("s_y1t", [128, 512], st=stC)
            s_isq = sbuf("s_isq", [128, 2], st=stC)

            seed_c = nc.inline_tensor(np.array([[2048.0, 8192.0]], np.float32), name="seedc").ap()
            s_seed = sbuf("s_seed", [1, 2], st=stC)
            cq = p.emit("sync", lambda e: e.dma_start(out=s_seed[:], in_=seed_c[:]),
                        inc=("qCc", 16))
            csq = p.emit("scalar", lambda e: e.activation(s_seed[:], s_seed[:], AF.Sqrt),
                         waits=[("qCc", cq)])
            crc = p.emit("vector", lambda e: e.reciprocal(s_seed[:], s_seed[:]),
                         waits=[("oc", csq)])
            cT = p.emit("tensor", lambda e: e.matmul(BANKS[0][:, 0:2], s_onesr[:],
                                                     s_seed[:], start=True, stop=True),
                        waits=bank_wait(0) + [("ov", crc)], inc=("ot", 1))
            cv = p.emit("vector", lambda e: e.tensor_copy(s_isq[:], BANKS[0][:, 0:2]),
                        waits=[("ot", cT)])
            bank_rec[0] = ("ov", cv)

            ghv2 = ghi_sp.rearrange("(i p) t -> p i t", p=128)
            glv2 = glo_sp.rearrange("(i p) t -> p i t", p=128)
            wov = wt_out.rearrange("(k p) e -> p k e", p=128)
            xv2 = x_fm.rearrange("(k p) t -> p k t", p=128)
            y1v = y1_sp.rearrange("(k p) t -> p k t", p=128)
            y1fv = y1f_sc.rearrange("c p t -> p c t")

            wo_rel = {}
            gh_rel = {}
            last_bounce = None
            for tg in range(4):
                t0 = tg * 512
                wg_ = [gh_rel[tg - 1]] if tg >= 1 else []
                cgh = p.emit("sync", lambda e, t0=t0: e.dma_start(
                    out=s_gh[:].rearrange("p (i t) -> p i t", i=32),
                    in_=ghv2[:, :, t0:t0 + 512]), waits=wg_, inc=("ghq", 16))
                cgl = p.emit("sync", lambda e, t0=t0: e.dma_start(
                    out=s_gl[:].rearrange("p (i t) -> p i t", i=32),
                    in_=glv2[:, :, t0:t0 + 512]), inc=("ghq", 16))
                cxc = p.emit("sync", lambda e, t0=t0: e.dma_start(
                    out=s_xc[:].rearrange("p (k t) -> p k t", k=16),
                    in_=xv2[:, :, HALO + t0: HALO + t0 + 512]), inc=("xcq", 16))
                res_done = {}
                for oi in range(16):
                    gwo = tg * 16 + oi
                    wb = gwo % 2
                    wbase = wb * 32 * 128
                    wfw = [wo_rel[gwo - 2]] if gwo >= 2 else []
                    cwf = p.emit("sync", lambda e, oi=oi, wbase=wbase: e.dma_start(
                        out=s_wof[:, wbase:wbase + 32 * 128].rearrange("p (k e) -> p k e", k=32),
                        in_=wov[:, :, oi * 128:(oi + 1) * 128]),
                        waits=wfw, inc=(f"woq{wb}", 16))
                    whw = [("ot", p.cur("ot"))] if gwo >= 2 else []
                    cwh = p.emit("vector", lambda e, wbase=wbase: e.tensor_copy(
                        s_wohi[:], s_wof[:, wbase:wbase + 32 * 128]),
                        waits=[(f"woq{wb}", cwf)] + ([] if gwo < 2 else [wo_mm_rel]))
                    cwl = p.emit("vector", lambda e, wbase=wbase: e.scalar_tensor_tensor(
                        s_wolo[:], s_wohi[:], -1.0, s_wof[:, wbase:wbase + 32 * 128],
                        ALU.mult, ALU.add))
                    wo_rel[gwo] = ("ov", cwl)
                    b = 2 + (gwo % 2)
                    tw = bank_wait(b)
                    first = True
                    cmm = None
                    for k in range(32):
                        for (wsb, gsb) in ((s_wohi, s_gh), (s_wohi, s_gl), (s_wolo, s_gh)):
                            w_ap = wsb[:, k * 128:(k + 1) * 128]
                            g_ap = gsb[:, k * 512:(k + 1) * 512]
                            last = (k == 31 and wsb is s_wolo)
                            wts = (tw + [("ov", cwl), ("ghq", cgl)]) if first else []
                            cmm = p.emit("tensor", lambda e, w=w_ap, g=g_ap, b=b, f=first,
                                         l=last: e.matmul(BANKS[b][:, 0:512], w, g,
                                                          start=f, stop=l),
                                         waits=wts, inc=(("ot", 1) if last else None))
                            first = False
                    wo_mm_rel = ("ot", cmm)
                    cre = p.emit("vector", lambda e, oi=oi, b=b: e.tensor_add(
                        s_res[:, oi * 512:(oi + 1) * 512], BANKS[b][:, 0:512],
                        s_xc[:, oi * 512:(oi + 1) * 512]),
                        waits=[("ot", cmm), ("xcq", cxc)])
                    bank_rec[b] = ("ov", cre)
                    res_done[oi] = cre
                    csqr = p.emit("scalar", lambda e, oi=oi: e.activation(
                        s_sq[:], s_res[:, oi * 512:(oi + 1) * 512], AF.Square),
                        waits=[("ov", cre)])
                    wst = bank_wait(4) if oi == 0 else []
                    p.emit("tensor", lambda e, oi=oi: e.matmul(
                        BANKS[4][0:1, 0:512], s_onescol[:],
                        s_res[:, oi * 512:(oi + 1) * 512],
                        start=(oi == 0), stop=(oi == 15)),
                        waits=(wst + [("ov", cre)]))
                    cst2 = p.emit("tensor", lambda e, oi=oi: e.matmul(
                        BANKS[4][0:1, 512:1024] if False else BANKS[5][0:1, 0:512],
                        s_onescol[:], s_sq[:],
                        start=(oi == 0), stop=(oi == 15)),
                        waits=(bank_wait(5) if oi == 0 else []) + [("oc", csqr)],
                        inc=(("ot", 1) if oi == 15 else None))
                cm1 = p.emit("vector", lambda e: e.tensor_scalar_mul(
                    s_st[0:1, 0:512], BANKS[4][0:1, 0:512], 1.0 / 2048.0),
                    waits=[("ot", cst2)])
                cm2 = p.emit("vector", lambda e: e.tensor_scalar_mul(
                    s_st[0:1, 512:1024], BANKS[5][0:1, 0:512], 1.0 / 2048.0))
                bank_rec[4] = ("ov", cm2)
                bank_rec[5] = ("ov", cm2)
                cm3 = p.emit("vector", lambda e: e.tensor_tensor(
                    s_st[0:1, 1024:1536], s_st[0:1, 0:512], s_st[0:1, 0:512], ALU.mult))
                cm4 = p.emit("vector", lambda e: e.scalar_tensor_tensor(
                    s_st[0:1, 512:1024], s_st[0:1, 512:1024], 1e-5,
                    s_st[0:1, 1024:1536], ALU.add, ALU.subtract))
                cm5 = p.emit("scalar", lambda e: e.activation(
                    s_st[0:1, 1024:1536], s_st[0:1, 512:1024], AF.Sqrt),
                    waits=[("ov", cm4)])
                cm6 = p.emit("vector", lambda e: e.reciprocal(
                    s_st[0:1, 1536:2048], s_st[0:1, 1024:1536]), waits=[("oc", cm5)])
                cb1 = p.emit("tensor", lambda e: e.matmul(
                    BANKS[6][:, 0:512], s_onesr[:], s_st[0:1, 0:512],
                    start=True, stop=True), waits=bank_wait(6) + [("ov", cm6)],
                    inc=("ot", 1))
                cb1v = p.emit("vector", lambda e: e.tensor_copy(s_mb[:], BANKS[6][:, 0:512]),
                              waits=[("ot", cb1)])
                bank_rec[6] = ("ov", cb1v)
                cb2 = p.emit("tensor", lambda e: e.matmul(
                    BANKS[7][:, 0:512], s_onesr[:], s_st[0:1, 1536:2048],
                    start=True, stop=True), waits=bank_wait(7) + [("ov", cm6)],
                    inc=("ot", 1))
                cb2v = p.emit("vector", lambda e: e.tensor_copy(s_rb[:], BANKS[7][:, 0:512]),
                              waits=[("ot", cb2)])
                bank_rec[7] = ("ov", cb2v)
                for oi in range(16):
                    wy1 = [("ov", cb2v)] + ([("y1q", last_bounce)] if last_bounce else [])
                    cy1 = p.emit("vector", lambda e, oi=oi: e.tensor_sub(
                        s_y1t[:], s_res[:, oi * 512:(oi + 1) * 512], s_mb[:]),
                        waits=wy1)
                    cy2 = p.emit("vector", lambda e: e.tensor_mul(
                        s_y1t[:], s_y1t[:], s_rb[:]))
                    cspl = p.emit("sync", lambda e, oi=oi, t0=t0: e.dma_start(
                        out=y1v[:, oi, t0:t0 + 512], in_=s_y1t[:]),
                        waits=[("ov", cy2)], inc=("y1q", 16))
                    b2 = 2 + (oi % 2)
                    cfa = p.emit("tensor", lambda e, b2=b2: e.matmul(
                        BANKS[b2][:, 0:512], s_h128[:], s_y1t[:], start=True, stop=True),
                        waits=bank_wait(b2) + [("ov", cy2)], inc=("ot", 1))
                    cfv = p.emit("vector", lambda e, b2=b2: e.tensor_copy(
                        s_y1t[:], BANKS[b2][:, 0:512]), waits=[("ot", cfa), ("y1q", cspl)])
                    bank_rec[b2] = ("ov", cfv)
                    last_bounce = p.emit("sync", lambda e, oi=oi, t0=t0: e.dma_start(
                        out=y1fv[:, oi, t0:t0 + 512], in_=s_y1t[:]),
                        waits=[("ov", cfv)], inc=("y1q", 16))
                gh_rel[tg] = ("ot", p.cur("ot"))

            p.barrier([("y1q", p.cur("y1q")), ("ov", p.cur("ov")), ("oc", p.cur("oc")),
                       ("ot", p.cur("ot"))])
            stC.close()

        if UPTO >= 4:
            s_rw = sbuf("s_rw", [128, 4])
            s_mwc = sbuf("s_mwc", [128, 4])
            stD0 = ExitStack()
            s_wsl = sbuf("s_wsl", [128, 2 * 8192], st=stD0)
            s_wac = sbuf("s_wac", [128, 100], st=stD0)
            s_rwr = sbuf("s_rwr", [1, 8], st=stD0)
            wgv = wg_p.rearrange("(k p) e -> p k e", p=128)
            wuv = wu_p.rearrange("(k p) e -> p k e", p=128)
            wdv = wd_p.rearrange("(b p) e -> p b e", p=128)
            red_rec = {}
            gsl = 0
            for wi, (wv_, nslab, fl, acc0) in enumerate([(wgv, 16, 8192, 0),
                                                         (wuv, 16, 8192, 16),
                                                         (wdv, 64, 2048, 32)]):
                for sl_ in range(nslab):
                    bb = gsl % 2
                    wfw = [red_rec[gsl - 2]] if gsl >= 2 else []
                    cwd = p.emit("sync", lambda e, wv_=wv_, sl_=sl_, bb=bb, fl=fl:
                                 e.dma_start(out=s_wsl[:, bb * 8192:bb * 8192 + fl],
                                             in_=wv_[:, sl_, :]),
                                 waits=wfw, inc=(f"wmq{bb}", 16))
                    col = acc0 + (sl_ if wi < 2 else sl_)
                    cgr = p.emit("vector", lambda e, bb=bb, fl=fl, col=col:
                                 e.reduce_sum(s_wac[:, col:col + 1].unsqueeze(2),
                                              s_wsl[:, bb * 8192:bb * 8192 + fl].unsqueeze(1),
                                              AX.X, apply_absolute_value=True),
                                 waits=[(f"wmq{bb}", cwd)])
                    red_rec[gsl] = ("ov", cgr)
                    gsl += 1
                nin = nslab
                cfold = p.emit("vector", lambda e, acc0=acc0, nin=nin, wi=wi: e.reduce_sum(
                    s_wac[:, 96 + wi:97 + wi].unsqueeze(2),
                    s_wac[:, acc0:acc0 + nin].unsqueeze(1), AX.X),
                    waits=[red_rec[gsl - 1]])
                cT = p.emit("tensor", lambda e, wi=wi: e.matmul(
                    BANKS[0][0:1, wi:wi + 1], s_onescol[:], s_wac[:, 96 + wi:97 + wi],
                    start=True, stop=True),
                    waits=bank_wait(0) + [("ov", cfold)], inc=("ot", 1))
                cmn = p.emit("vector", lambda e, wi=wi: e.tensor_scalar(
                    s_rwr[0:1, wi:wi + 1], BANKS[0][0:1, wi:wi + 1],
                    1.0 / 16777216.0, 1e-5, ALU.mult, ALU.max),
                    waits=[("ot", cT)])
                bank_rec[0] = ("ov", cmn)
                crc = p.emit("vector", lambda e, wi=wi: e.reciprocal(
                    s_rwr[0:1, 4 + wi:5 + wi], s_rwr[0:1, wi:wi + 1]))
            cT = p.emit("tensor", lambda e: e.matmul(
                BANKS[1][:, 0:3], s_onesr[:], s_rwr[0:1, 0:3], start=True, stop=True),
                waits=bank_wait(1) + [("ov", p.cur("ov"))], inc=("ot", 1))
            cT2 = p.emit("tensor", lambda e: e.matmul(
                BANKS[1][:, 4:7], s_onesr[:], s_rwr[0:1, 4:7], start=True, stop=True),
                inc=("ot", 1))
            cv = p.emit("vector", lambda e: e.tensor_copy(s_mwc[:, 0:3], BANKS[1][:, 0:3]),
                        waits=[("ot", cT2)])
            cv2 = p.emit("vector", lambda e: e.tensor_copy(s_rw[:, 0:3], BANKS[1][:, 4:7]))
            bank_rec[1] = ("ov", cv2)
            p.barrier([("ov", p.cur("ov")), ("ot", p.cur("ot")), ("og", p.cur("og"))])
            stD0.close()

        if UPTO >= 5:
            stD1 = ExitStack()
            s_q1 = sbuf("s_q1", [128, 16 * 2048], BF16, st=stD1)
            s_mx = sbuf("s_mx", [128, 2048], st=stD1)
            s_row1 = sbuf("s_row1", [128, 2048], st=stD1)
            s_fg = sbuf("s_fg", [128, 2048], st=stD1)
            s_fu = sbuf("s_fu", [128, 2048], st=stD1)
            s_shin = sbuf("s_shin", [128, 2 * 512], st=stD1)
            s_abs1 = sbuf("s_abs1", [128, 512], st=stD1)
            s_isqd = sbuf("s_isqd", [128, 2], st=stD1)
            s_wgf = sbuf("s_wgf", [128, 16 * 512], st=stD1)
            s_wgt = sbuf("s_wgt", [128, 16 * 512], BF16, st=stD1)
            s_wut = sbuf("s_wut", [128, 16 * 512], BF16, st=stD1)
            s_gv = sbuf("s_gv", [128, 512], st=stD1)
            s_sg = sbuf("s_sg", [128, 512], st=stD1)
            s_uv = sbuf("s_uv", [128, 512], st=stD1)
            s_mid = sbuf("s_mid", [128, 512], st=stD1)
            s_sam4 = sbuf("s_sam4", [128, 4 * 512], st=stD1)


            def cross_part_max_to_rows(src_mx, dst_row, ntile, mxc_buf, mxT_buf, s_oh16):
                recs = []
                for j in range(ntile):
                    bk2 = 2 + j % 2
                    cT_ = p.emit("tensor", lambda e, j=j, bk2=bk2: e.transpose(
                        BANKS[bk2][:, 0:128], src_mx[:, j * 128:(j + 1) * 128], s_i128[:]),
                        waits=bank_wait(bk2) + ([("ov", p.cur("ov")),
                                                 ("oc", p.cur("oc"))] if j == 0 else []),
                        inc=("ot", 1))
                    cr_ = p.emit("vector", lambda e, j=j, bk2=bk2: e.reduce_max(
                        mxc_buf[:, j:j + 1].unsqueeze(2),
                        BANKS[bk2][:, 0:128].unsqueeze(1), AX.X),
                        waits=[("ot", cT_)])
                    bank_rec[bk2] = ("ov", cr_)
                cT2_ = p.emit("tensor", lambda e: e.transpose(
                    BANKS[4][0:ntile, 0:128], mxc_buf[:, 0:ntile], s_i128[:]),
                    waits=bank_wait(4) + [("ov", p.cur("ov"))], inc=("ot", 1))
                cv2_ = p.emit("vector", lambda e: e.tensor_copy(
                    mxT_buf[0:ntile, 0:128], BANKS[4][0:ntile, 0:128]),
                    waits=[("ot", cT2_)])
                bank_rec[4] = ("ov", cv2_)
                for j in range(ntile):
                    bk2 = 5 + j % 2
                    cT3_ = p.emit("tensor", lambda e, j=j, bk2=bk2: e.matmul(
                        BANKS[bk2][:, 0:128], s_oh16[0:ntile, j * 128:(j + 1) * 128],
                        mxT_buf[0:ntile, 0:128], start=True, stop=True),
                        waits=bank_wait(bk2) + ([("ov", cv2_)] if j <= 1 else []),
                        inc=("ot", 1))
                    cv3_ = p.emit("vector", lambda e, j=j, bk2=bk2: e.tensor_copy(
                        dst_row[:, j * 128:(j + 1) * 128], BANKS[bk2][:, 0:128]),
                        waits=[("ot", cT3_)])
                    bank_rec[bk2] = ("ov", cv3_)
                return p.cur("ov")

            seed2_c = nc.inline_tensor(np.array([[2048.0, 8192.0]], np.float32), name="seed2c").ap()
            s_sd2 = sbuf("s_sd2", [1, 2], st=stD1)
            cq = p.emit("sync", lambda e: e.dma_start(out=s_sd2[:], in_=seed2_c[:]),
                        inc=("qDc", 16))
            csq = p.emit("scalar", lambda e: e.activation(s_sd2[:], s_sd2[:], AF.Sqrt),
                         waits=[("qDc", cq)])
            crc = p.emit("vector", lambda e: e.reciprocal(s_sd2[:], s_sd2[:]),
                         waits=[("oc", csq)])
            cT = p.emit("tensor", lambda e: e.matmul(BANKS[0][:, 0:2], s_onesr[:],
                                                     s_sd2[:], start=True, stop=True),
                        waits=bank_wait(0) + [("ov", crc)], inc=("ot", 1))
            cv = p.emit("vector", lambda e: e.tensor_copy(s_isqd[:], BANKS[0][:, 0:2]),
                        waits=[("ot", cT)])
            bank_rec[0] = ("ov", cv)

            y1f_shv = y1f_sc.rearrange("c (a b) t -> (c a) (b t)", b=16)
            shin_carry = {}
            def sweep1_q1(tag, rnd):
                shin_rel = dict(shin_carry)
                prev_mx = None
                for m in range(64):
                    slot = m % 2
                    bk = 2 + m % 2
                    wsh = [shin_rel[m - 2]] if (m - 2) in shin_rel else []
                    cin = p.emit("sync", lambda e, m=m, slot=slot: e.dma_start(
                        out=s_shin[:, slot * 512:(slot + 1) * 512],
                        in_=y1f_shv[:, m * 512:(m + 1) * 512]),
                        waits=wsh, inc=(f"s1q{tag}{slot}", 16))
                    cmm = p.emit("tensor", lambda e, slot=slot, bk=bk: e.matmul(
                        BANKS[bk][:, 0:512], s_m1[:], s_shin[:, slot * 512:(slot + 1) * 512],
                        start=True, stop=True),
                        waits=bank_wait(bk) + [(f"s1q{tag}{slot}", cin)], inc=("ot", 1))
                    tb = m % 4
                    bblk = m // 4
                    if not rnd:
                        if bblk == 0:
                            cev = p.emit("scalar", lambda e, bk=bk, tb=tb: e.activation(
                                s_mx[:, tb * 512:(tb + 1) * 512], BANKS[bk][:, 0:512],
                                AF.Abs), waits=[("ot", cmm)])
                            cev = ("oc", cev)
                        else:
                            cab = p.emit("scalar", lambda e, bk=bk: e.activation(
                                s_abs1[:], BANKS[bk][:, 0:512], AF.Abs),
                                waits=[("ot", cmm)] + ([prev_mx] if prev_mx else []))
                            cev = p.emit("vector", lambda e, tb=tb: e.tensor_max(
                                s_mx[:, tb * 512:(tb + 1) * 512],
                                s_mx[:, tb * 512:(tb + 1) * 512], s_abs1[:]),
                                waits=[("oc", cab)])
                            cev = ("ov", cev)
                            prev_mx = cev
                    else:
                        cev = p.emit("vector", lambda e, bk=bk, tb=tb, m=m: e._custom_dve(
                            RND_TT, out=s_q1[:, m * 512:(m + 1) * 512],
                            in0=BANKS[bk][:, 0:512],
                            in1=s_row1[:, tb * 512:(tb + 1) * 512], s0=MAGIC),
                            waits=[("ot", cmm)])
                        cev = ("ov", cev)
                    bank_rec[bk] = cev
                    shin_rel[m] = ("ot", cmm)
                shin_carry.clear()
                shin_carry[-2] = shin_rel[62]
                shin_carry[-1] = shin_rel[63]
            sweep1_q1("a", False)
            s_mxc1 = sbuf("s_mxc1", [128, 16], st=stD1)
            s_mxT1 = sbuf("s_mxT1", [16, 128], st=stD1)
            s_oh16a = sbuf("s_oh16a", [16, 16 * 128], st=stD1)
            coh = p.emit("sync", lambda e: e.dma_start(out=s_oh16a[:], in_=oh16_c[:]),
                         inc=("qFc", 16))
            p.barrier([("qFc", p.cur("qFc"))])
            cpm = cross_part_max_to_rows(s_mx, s_row1, 16, s_mxc1, s_mxT1, s_oh16a)
            ca = p.emit("vector", lambda e: e.tensor_scalar(
                s_row1[:], s_row1[:], s_isqd[:, 0:1], 1e-5, ALU.mult, ALU.max),
                waits=[("ov", cpm)])
            cfg = p.emit("vector", lambda e: e.tensor_scalar(
                s_fg[:], s_row1[:], s_mwc[:, 0:1], 1.0 / 127.0, ALU.mult, ALU.mult))
            cfu = p.emit("vector", lambda e: e.tensor_scalar(
                s_fu[:], s_row1[:], s_mwc[:, 1:2], 1.0 / 127.0, ALU.mult, ALU.mult))
            crr = p.emit("vector", lambda e: e.reciprocal(s_row1[:], s_row1[:]))
            cs1 = p.emit("vector", lambda e: e.tensor_scalar(
                s_row1[:], s_row1[:], s_isqd[:, 0:1], 127.0, ALU.mult, ALU.mult))
            sweep1_q1("b", True)
            q1_done = p.cur("ov")

            samv = sam_sc.rearrange("c p t -> p c t")
            tern_rel = None
            mm_rel_gu = {}
            for os_ in range(16):
                wts = [tern_rel] if tern_rel else []
                cda = p.emit("sync", lambda e, os_=os_: e.dma_start(
                    out=s_wgf[:].rearrange("p (k e) -> p k e", k=16),
                    in_=wgv[:, :, os_ * 512:(os_ + 1) * 512]),
                    waits=wts + ([mm_rel_gu[os_ - 1]] if os_ >= 1 else []),
                    inc=("wgq", 16))
                cwgq = p.cur("wgq")
                ctg = p.emit("vector", lambda e: e._custom_dve(
                    TERN_TS, out=s_wgt[:], in0=s_wgf[:], s0=MAGIC, s1=s_rw[:, 0:1],
                    imm2=1.0), waits=[("wgq", cwgq), ("ov", q1_done)])
                cdu = p.emit("sync", lambda e, os_=os_: e.dma_start(
                    out=s_wgf[:].rearrange("p (k e) -> p k e", k=16),
                    in_=wuv[:, :, os_ * 512:(os_ + 1) * 512]),
                    waits=[("ov", ctg)], inc=("wgq", 16))
                cwuq = p.cur("wgq")
                ctu = p.emit("vector", lambda e: e._custom_dve(
                    TERN_TS, out=s_wut[:], in0=s_wgf[:], s0=MAGIC, s1=s_rw[:, 1:2],
                    imm2=1.0), waits=[("wgq", cwuq)])
                tern_rel = ("ov", ctu)
                for tb in range(4):
                    for osub in range(4):
                        it = tb * 4 + osub
                        bg = 4 + it % 2
                        bu = 6 + it % 2
                        cmg = cmu = None
                        for b16 in range(16):
                            wg_ = (bank_wait(bg) + [("ov", ctu)]) if b16 == 0 else []
                            cmg = p.emit("tensor", lambda e, b16=b16, osub=osub, tb=tb,
                                         bg=bg: e.matmul(
                                             BANKS[bg][:, 0:512],
                                             s_wgt[:, b16 * 512 + osub * 128: b16 * 512 + (osub + 1) * 128],
                                             s_q1[:, b16 * 2048 + tb * 512: b16 * 2048 + (tb + 1) * 512],
                                             start=(b16 == 0), stop=(b16 == 15)),
                                         waits=wg_, inc=(("ot", 1) if b16 == 15 else None))
                        for b16 in range(16):
                            wu_ = bank_wait(bu) if b16 == 0 else []
                            cmu = p.emit("tensor", lambda e, b16=b16, osub=osub, tb=tb,
                                         bu=bu: e.matmul(
                                             BANKS[bu][:, 0:512],
                                             s_wut[:, b16 * 512 + osub * 128: b16 * 512 + (osub + 1) * 128],
                                             s_q1[:, b16 * 2048 + tb * 512: b16 * 2048 + (tb + 1) * 512],
                                             start=(b16 == 0), stop=(b16 == 15)),
                                         waits=wu_, inc=(("ot", 1) if b16 == 15 else None))
                        cg_ = p.emit("vector", lambda e, bg=bg, tb=tb: e.tensor_tensor(
                            s_gv[:], BANKS[bg][:, 0:512], s_fg[:, tb * 512:(tb + 1) * 512],
                            ALU.mult), waits=[("ot", cmu)])
                        bank_rec[bg] = ("ov", cg_)
                        csg = p.emit("scalar", lambda e: e.activation(
                            s_sg[:], s_gv[:], AF.Silu), waits=[("ov", cg_)])
                        cu_ = p.emit("vector", lambda e, bu=bu, tb=tb: e.tensor_tensor(
                            s_uv[:], BANKS[bu][:, 0:512], s_fu[:, tb * 512:(tb + 1) * 512],
                            ALU.mult))
                        bank_rec[bu] = ("ov", cu_)
                        cmid = p.emit("vector", lambda e: e.tensor_mul(
                            s_mid[:], s_sg[:], s_uv[:]), waits=[("oc", csg)])
                        bA = 2 + it % 2
                        cfa = p.emit("tensor", lambda e, bA=bA: e.matmul(
                            BANKS[bA][:, 0:512], s_h128[:], s_mid[:], start=True, stop=True),
                            waits=bank_wait(bA) + [("ov", cmid)], inc=("ot", 1))
                        cfv = p.emit("vector", lambda e, bA=bA, osub=osub: e.tensor_copy(
                            s_sam4[:, osub * 512:(osub + 1) * 512], BANKS[bA][:, 0:512]),
                            waits=[("ot", cfa)] + ([("smq", smq_last)] if os_ + tb > 0 else []))
                        bank_rec[bA] = ("ov", cfv)
                    smq_last = p.emit("sync", lambda e, os_=os_, tb=tb: e.dma_start(
                        out=samv[:, os_ * 4:(os_ + 1) * 4, tb * 512:(tb + 1) * 512],
                        in_=s_sam4[:].rearrange("p (c t) -> p c t", c=4)),
                        waits=[("ov", p.cur("ov"))], inc=("smq", 16))
                mm_rel_gu[os_] = ("ot", p.cur("ot"))
            s_wdf = s_wgf
            s_wdt = s_wgt
            wdtv = wdt_sp.rearrange("(b p) e -> p b e", p=128)
            wd_rel = {}
            for b2 in range(64):
                slot = b2 % 2
                wts = [wd_rel[b2 - 2]] if b2 >= 2 else (
                    [tern_rel, mm_rel_gu[15]] if b2 == 0 else [])
                cdd = p.emit("sync", lambda e, b2=b2, slot=slot: e.dma_start(
                    out=s_wdf[:, slot * 2048:(slot + 1) * 2048], in_=wdv[:, b2, :]),
                    waits=wts, inc=(f"wdq{slot}", 16))
                ctd = p.emit("vector", lambda e, slot=slot: e._custom_dve(
                    TERN_TS, out=s_wdt[:, slot * 2048:(slot + 1) * 2048],
                    in0=s_wdf[:, slot * 2048:(slot + 1) * 2048], s0=MAGIC,
                    s1=s_rw[:, 2:3], imm2=1.0), waits=[(f"wdq{slot}", cdd)])
                cds = p.emit("sync", lambda e, b2=b2, slot=slot: e.dma_start(
                    out=wdtv[:, b2, :], in_=s_wdt[:, slot * 2048:(slot + 1) * 2048]),
                    waits=[("ov", ctd)], inc=(f"wdsq{slot}", 16))
                wd_rel[b2] = (f"wdsq{slot}", cds)
            p.barrier([("smq", p.cur("smq")), ("wdsq0", p.cur("wdsq0")),
                       ("wdsq1", p.cur("wdsq1")), ("ov", p.cur("ov")),
                       ("oc", p.cur("oc")), ("ot", p.cur("ot"))])
            stD1.close()

        if UPTO >= 6:
            stD2 = ExitStack()
            s_q2 = sbuf("s_q2", [128, 64 * 512], BF16, st=stD2)
            s_mx2 = sbuf("s_mx2", [128, 512], st=stD2)
            s_row2 = sbuf("s_row2", [128, 512], st=stD2)
            s_fd = sbuf("s_fd", [128, 512], st=stD2)
            s_shin2 = sbuf("s_shin2", [128, 2 * 2 * 512], st=stD2)
            s_mxc2 = sbuf("s_mxc2", [128, 16], st=stD2)
            s_mxT2 = sbuf("s_mxT2", [16, 128], st=stD2)
            s_oh16b = sbuf("s_oh16b", [16, 4 * 128], st=stD2)
            s_wdts = sbuf("s_wdts", [128, 2 * 64 * 128], BF16, st=stD2)
            s_y1c = sbuf("s_y1c", [128, 16 * 512], st=stD2)
            s_res2 = sbuf("s_res2", [128, 16 * 512], st=stD2)
            s_sq2 = sbuf("s_sq2", [128, 512], st=stD2)
            s_st2 = sbuf("s_st2", [1, 3 * 512], st=stD2)
            s_mb2 = sbuf("s_mb2", [128, 512], st=stD2)
            s_rb2 = sbuf("s_rb2", [128, 512], st=stD2)
            s_ot = sbuf("s_ot", [128, 512], st=stD2)
            s_abs2 = s_ot
            s_isq2 = sbuf("s_isq2", [128, 2], st=stD2)

            seed3_c = nc.inline_tensor(np.array([[2048.0, 8192.0]], np.float32), name="seed3c").ap()
            s_sd3 = sbuf("s_sd3", [1, 2], st=stD2)
            coh2 = p.emit("sync", lambda e: e.dma_start(out=s_oh16b[:], in_=oh16_c[:, 0:512]),
                          inc=("qGc", 16))
            p.barrier([("qGc", p.cur("qGc"))])
            cq = p.emit("sync", lambda e: e.dma_start(out=s_sd3[:], in_=seed3_c[:]),
                        inc=("qEc", 16))
            csq = p.emit("scalar", lambda e: e.activation(s_sd3[:], s_sd3[:], AF.Sqrt),
                         waits=[("qEc", cq)])
            crc = p.emit("vector", lambda e: e.reciprocal(s_sd3[:], s_sd3[:]),
                         waits=[("oc", csq)])
            cT = p.emit("tensor", lambda e: e.matmul(BANKS[0][:, 0:2], s_onesr[:],
                                                     s_sd3[:], start=True, stop=True),
                        waits=bank_wait(0) + [("ov", crc)], inc=("ot", 1))
            cv = p.emit("vector", lambda e: e.tensor_copy(s_isq2[:], BANKS[0][:, 0:2]),
                        waits=[("ot", cT)])
            bank_rec[0] = ("ov", cv)

            last_out = None
            sh2_carry = {}
            sam_shv = sam_sc.rearrange("c (a b) t -> (c a) (b t)", b=64)
            y1v2 = y1_sp.rearrange("(k p) t -> p k t", p=128)
            ofv = out_fm.rearrange("(k p) t -> p k t", p=128)
            for qt in range(4):
                q0 = qt * 512
                prev_mx2 = None
                sam_3d = sam_shv[:].rearrange("p (b t) -> p b t", b=64)
                for rnd in (False, True):
                    tag = "r" if rnd else "m"
                    sh_rel = dict(sh2_carry)
                    for g in range(32):
                        slot = g % 2
                        wsh = [sh_rel[g - 2]] if (g - 2) in sh_rel else []
                        cin = p.emit("sync", lambda e, g=g, slot=slot, q0=q0: e.dma_start(
                            out=s_shin2[:, slot * 1024:(slot + 1) * 1024]
                            .rearrange("p (j t) -> p j t", j=2),
                            in_=sam_3d[:, g * 2:(g + 1) * 2, q0:q0 + 512]),
                            waits=wsh, inc=(f"s2q{tag}{qt}{slot}", 16))
                        cmm_g = None
                        for j in range(2):
                            b2 = g * 2 + j
                            bk = 2 + b2 % 2
                            cmm = p.emit("tensor", lambda e, slot=slot, j=j, bk=bk: e.matmul(
                                BANKS[bk][:, 0:512], s_m2[:],
                                s_shin2[:, slot * 1024 + j * 512: slot * 1024 + (j + 1) * 512],
                                start=True, stop=True),
                                waits=bank_wait(bk) +
                                      ([(f"s2q{tag}{qt}{slot}", cin)] if j == 0 else []),
                                inc=("ot", 1))
                            cmm_g = cmm
                            if not rnd:
                                if b2 == 0:
                                    cev = p.emit("scalar", lambda e, bk=bk: e.activation(
                                        s_mx2[:], BANKS[bk][:, 0:512], AF.Abs),
                                        waits=[("ot", cmm)])
                                    cev = ("oc", cev)
                                else:
                                    wab = [("ot", cmm)]
                                    if prev_mx2:
                                        wab.append(prev_mx2)
                                    elif last_out:
                                        wab.append(("qO", last_out))
                                    cab = p.emit("scalar", lambda e, bk=bk: e.activation(
                                        s_abs2[:], BANKS[bk][:, 0:512], AF.Abs),
                                        waits=wab)
                                    cev = p.emit("vector", lambda e: e.tensor_max(
                                        s_mx2[:], s_mx2[:], s_abs2[:]),
                                        waits=[("oc", cab)])
                                    cev = ("ov", cev)
                                    prev_mx2 = cev
                            else:
                                cev = p.emit("vector", lambda e, bk=bk, b2=b2: e._custom_dve(
                                    RND_TT, out=s_q2[:, b2 * 512:(b2 + 1) * 512],
                                    in0=BANKS[bk][:, 0:512], in1=s_row2[:], s0=MAGIC),
                                    waits=[("ot", cmm)])
                                cev = ("ov", cev)
                            bank_rec[bk] = cev
                        sh_rel[g] = ("ot", cmm_g)
                    sh2_carry.clear()
                    sh2_carry[-2] = sh_rel[30]
                    sh2_carry[-1] = sh_rel[31]
                    if not rnd:
                        cpm2 = cross_part_max_to_rows(s_mx2, s_row2, 4, s_mxc2, s_mxT2, s_oh16b)
                        ca = p.emit("vector", lambda e: e.tensor_scalar(
                            s_row2[:], s_row2[:], s_isq2[:, 1:2], 1e-5, ALU.mult, ALU.max),
                            waits=[("ov", cpm2)])
                        cfd = p.emit("vector", lambda e: e.tensor_scalar(
                            s_fd[:], s_row2[:], s_mwc[:, 2:3], 1.0 / 127.0, ALU.mult,
                            ALU.mult))
                        crr = p.emit("vector", lambda e: e.reciprocal(s_row2[:], s_row2[:]))
                        cs2 = p.emit("vector", lambda e: e.tensor_scalar(
                            s_row2[:], s_row2[:], s_isq2[:, 1:2], 127.0, ALU.mult, ALU.mult))
                q2_done = p.cur("ov")
                cy1q = p.emit("sync", lambda e, q0=q0: e.dma_start(
                    out=s_y1c[:].rearrange("p (k t) -> p k t", k=16),
                    in_=y1v2[:, :, q0:q0 + 512]), inc=(f"yq2{qt}", 16))
                wslab_rel = {}
                for oi in range(16):
                    slot = oi % 2
                    wts = [wslab_rel[oi - 2]] if oi >= 2 else []
                    cwd2 = p.emit("sync", lambda e, oi=oi, slot=slot: e.dma_start(
                        out=s_wdts[:, slot * 8192:(slot + 1) * 8192]
                        .rearrange("p (b e) -> p b e", b=64),
                        in_=wdtv[:, :, oi * 128:(oi + 1) * 128]),
                        waits=wts, inc=(f"wd2{qt}{slot}", 16))
                    bk = 4 + oi % 2
                    cmm = None
                    for b2 in range(64):
                        wmm = (bank_wait(bk) + [(f"wd2{qt}{slot}", cwd2),
                                                ("ov", q2_done)]) if b2 == 0 else []
                        cmm = p.emit("tensor", lambda e, b2=b2, bk=bk, slot=slot: e.matmul(
                            BANKS[bk][:, 0:512],
                            s_wdts[:, slot * 8192 + b2 * 128: slot * 8192 + (b2 + 1) * 128],
                            s_q2[:, b2 * 512:(b2 + 1) * 512],
                            start=(b2 == 0), stop=(b2 == 63)),
                            waits=wmm, inc=(("ot", 1) if b2 == 63 else None))
                    wslab_rel[oi] = ("ot", cmm)
                    cr1 = p.emit("vector", lambda e, bk=bk: e.tensor_mul(
                        s_ot[:], BANKS[bk][:, 0:512], s_fd[:]),
                        waits=[("ot", cmm), (f"yq2{qt}", cy1q)] +
                              ([("qO", last_out)] if last_out else []))
                    cr2 = p.emit("vector", lambda e, oi=oi: e.tensor_add(
                        s_res2[:, oi * 512:(oi + 1) * 512], s_ot[:],
                        s_y1c[:, oi * 512:(oi + 1) * 512]))
                    bank_rec[bk] = ("ov", cr2)
                    csqr = p.emit("scalar", lambda e, oi=oi: e.activation(
                        s_sq2[:], s_res2[:, oi * 512:(oi + 1) * 512], AF.Square),
                        waits=[("ov", cr2)])
                    p.emit("tensor", lambda e, oi=oi: e.matmul(
                        BANKS[6][0:1, 0:512], s_onescol[:],
                        s_res2[:, oi * 512:(oi + 1) * 512],
                        start=(oi == 0), stop=(oi == 15)),
                        waits=(bank_wait(6) if oi == 0 else []) + [("ov", cr2)])
                    cst2 = p.emit("tensor", lambda e, oi=oi: e.matmul(
                        BANKS[7][0:1, 0:512], s_onescol[:], s_sq2[:],
                        start=(oi == 0), stop=(oi == 15)),
                        waits=(bank_wait(7) if oi == 0 else []) + [("oc", csqr)],
                        inc=(("ot", 1) if oi == 15 else None))
                cm1 = p.emit("vector", lambda e: e.tensor_scalar_mul(
                    s_st2[0:1, 0:512], BANKS[6][0:1, 0:512], 1.0 / 2048.0),
                    waits=[("ot", cst2)])
                cm2 = p.emit("vector", lambda e: e.tensor_scalar_mul(
                    s_st2[0:1, 512:1024], BANKS[7][0:1, 0:512], 1.0 / 2048.0))
                bank_rec[6] = ("ov", cm2)
                bank_rec[7] = ("ov", cm2)
                cm3 = p.emit("vector", lambda e: e.tensor_tensor(
                    s_st2[0:1, 1024:1536], s_st2[0:1, 0:512], s_st2[0:1, 0:512], ALU.mult))
                cm4 = p.emit("vector", lambda e: e.scalar_tensor_tensor(
                    s_st2[0:1, 512:1024], s_st2[0:1, 512:1024], 1e-5,
                    s_st2[0:1, 1024:1536], ALU.add, ALU.subtract))
                cm5 = p.emit("scalar", lambda e: e.activation(
                    s_st2[0:1, 1024:1536], s_st2[0:1, 512:1024], AF.Sqrt),
                    waits=[("ov", cm4)])
                cm6 = p.emit("vector", lambda e: e.reciprocal(
                    s_st2[0:1, 512:1024], s_st2[0:1, 1024:1536]), waits=[("oc", cm5)])
                cb1 = p.emit("tensor", lambda e: e.matmul(
                    BANKS[2][:, 0:512], s_onesr[:], s_st2[0:1, 0:512],
                    start=True, stop=True), waits=bank_wait(2) + [("ov", cm6)],
                    inc=("ot", 1))
                cb1v = p.emit("vector", lambda e: e.tensor_copy(s_mb2[:], BANKS[2][:, 0:512]),
                              waits=[("ot", cb1)])
                bank_rec[2] = ("ov", cb1v)
                cb2 = p.emit("tensor", lambda e: e.matmul(
                    BANKS[3][:, 0:512], s_onesr[:], s_st2[0:1, 512:1024],
                    start=True, stop=True), waits=bank_wait(3) + [("ov", cm6)],
                    inc=("ot", 1))
                cb2v = p.emit("vector", lambda e: e.tensor_copy(s_rb2[:], BANKS[3][:, 0:512]),
                              waits=[("ot", cb2)])
                bank_rec[3] = ("ov", cb2v)
                for oi in range(16):
                    co1 = p.emit("vector", lambda e, oi=oi: e.tensor_sub(
                        s_ot[:], s_res2[:, oi * 512:(oi + 1) * 512], s_mb2[:]),
                        waits=[("ov", cb2v)] + ([("qO", last_out)] if last_out else []))
                    co2 = p.emit("vector", lambda e: e.tensor_mul(s_ot[:], s_ot[:], s_rb2[:]))
                    last_out = p.emit("sync", lambda e, oi=oi, q0=q0: e.dma_start(
                        out=ofv[:, oi, q0:q0 + 512], in_=s_ot[:]),
                        waits=[("ov", co2)], inc=("qO", 16))
            stD2.close()

        s_dbg = sbuf("s_dbg", [128, 2 * TT])
        cdbg = p.emit("sync", lambda e: e.dma_start(
            out=s_dbg[:, 0:TOWN], in_=y1_sp[0:128, :]), inc=("qD1", 16))
        cdbg2 = p.emit("sync", lambda e: e.dma_start(
            out=s_dbg[:, TT:TT + TOWN], in_=y1f_sc[0, :, :]), inc=("qD2", 16))
        p.emit("sync", lambda e: e.dma_start(out=dbg[:, 0:TOWN],
               in_=s_dbg[:, 0:TOWN]),
               waits=[("qD1", cdbg), ("qD2", cdbg2)], inc=("qD3", 16))
        p.emit("sync", lambda e: e.dma_start(out=dbg[:, TT:TT + TOWN],
               in_=s_dbg[:, TT:TT + TOWN]), inc=("qD3", 16))
        p.build_block(block, final_waits=[(n, p.cur(n)) for n in list(p.counts.keys())])
    return nc


def _shard_inputs(L0_attn_input_, L0_attn_in_proj_w, L0_attn_conv1d_weight,
                  L0_attn_conv1d_bias, L0_attn_z_bias, L0_attn_D, L0_attn_out_proj_w,
                  L0_ffn_w_gate, L0_ffn_w_up, L0_ffn_w_down):
    X = np.ascontiguousarray(np.asarray(L0_attn_input_, dtype=np.float32)[0])
    wt_in = np.ascontiguousarray(np.asarray(L0_attn_in_proj_w, np.float32).T)
    convw = np.ascontiguousarray(
        np.asarray(L0_attn_conv1d_weight, np.float32).reshape(CONVD, 4)
        .reshape(40, 128, 4).transpose(1, 0, 2).reshape(128, 160))
    convb = np.ascontiguousarray(
        np.asarray(L0_attn_conv1d_bias, np.float32).reshape(40, 128).T)
    zbias = np.ascontiguousarray(np.asarray(L0_attn_z_bias, np.float32).reshape(32, 128).T)
    dv = np.asarray(L0_attn_D, np.float32).reshape(1, 64)
    wt_out = np.ascontiguousarray(np.asarray(L0_attn_out_proj_w, np.float32).T)

    perm1 = np.empty(2048, np.int64)
    for b in range(16):
        for c in range(16):
            for a in range(8):
                perm1[b * 128 + c * 8 + a] = c * 128 + a * 16 + b
    perm2 = np.empty(8192, np.int64)
    for b2 in range(64):
        for c in range(64):
            for a2 in range(2):
                perm2[b2 * 128 + c * 2 + a2] = c * 128 + a2 * 64 + b2
    wg = np.ascontiguousarray(np.asarray(L0_ffn_w_gate, np.float32).T[perm1, :])
    wu = np.ascontiguousarray(np.asarray(L0_ffn_w_up, np.float32).T[perm1, :])
    wd = np.ascontiguousarray(np.asarray(L0_ffn_w_down, np.float32).T[perm2, :])

    in_maps = []
    for k in range(NCORE):
        lo = k * TOWN - HALO
        sl = np.zeros((TT, DM), np.float32)
        src0 = max(lo, 0)
        sl[TT - (k * TOWN + TOWN - src0):] = X[src0: (k + 1) * TOWN]
        x_fmk = np.ascontiguousarray(sl.T)
        hm = np.full((128, 1), 0.0 if k == 0 else 1.0, np.float32)
        in_maps.append({"x_fm": x_fmk, "wt_in": wt_in, "convw": convw, "convb": convb,
                        "zb": zbias, "dvec": dv, "wt_out": wt_out,
                        "wg_p": wg, "wu_p": wu, "wd_p": wd, "hm_fm": hm})
    return in_maps


_NC_CACHE = {}

def _get_nc():
    if "nc" not in _NC_CACHE:
        nc = build_program()
        lower_extended_insts(nc)
        _NC_CACHE["nc"] = nc
    return _NC_CACHE["nc"]


def kernel(**inputs):
    in_maps = _shard_inputs(**inputs)
    nc = _get_nc()
    res = run_bass_kernel_spmd(nc, in_maps, core_ids=list(range(NCORE)))
    out = np.empty((16384, 2048), np.float32)
    for k in range(NCORE):
        out[k * TOWN:(k + 1) * TOWN, :] = np.asarray(res.results[k]["out_fm"]).T
    return out.reshape(1, 16384, 2048)


# version 09
# speedup vs baseline: 1.2572x; 1.1475x over previous
import numpy as np
from contextlib import ExitStack

import concourse.bass as bass
import concourse.mybir as mybir
from concourse.bass_utils import run_bass_kernel_spmd
from concourse.library_overlay import lower_extended_insts
from concourse.bass_isa import ReduceOp
from concourse.dve_spec import Spec, Src0, Src1, C0, C1, C2, Zero, maxx, minn
from concourse.dve_op import DveOp
from concourse import dve_ops as _dvo
import concourse.dve_spec as _dsp
from concourse.library_config import all_libraries

F32 = mybir.dt.float32
BF16 = mybir.dt.bfloat16
AF = mybir.ActivationFunctionType
ALU = mybir.AluOpType
AX = mybir.AxisListType

MAGIC = 12582912.0

def _register_op(name, spec, ref):
    if name in _dvo._SUB_OPCODE_FOR_NAME:
        return next(o for o in _dvo.OPS if o.name == name)
    lowered = _dsp.lower(spec, ver="v3", perf_en=False, name=name)
    op = DveOp(name, spec, subdim=False, uops_sha={"v3": lowered.sha("v3")},
               reference=ref)
    row = max(_dvo._SUB_OPCODE_FOR_NAME.values()) + 1
    assert row < 0x20
    _dvo.OPS.append(op)
    _dvo._SUB_OPCODE_FOR_NAME[op.name] = row
    _dvo.CUSTOM_DVE_SPECS[op.name] = spec
    _dvo.CUSTOM_DVE_REFS[op.name] = ref
    return op

RND_TT = _register_op(
    "RND_TT_ANT", Spec(body=(Src0 * Src1 + C0) - C0),
    lambda in0, in1, s0, s1, imm2: np.rint(in0 * in1))
TERN_TS = _register_op(
    "TERN_TS_ANT", Spec(body=minn(maxx(((Src0 * C1) + C0) - C0, Zero - C2), C2)),
    lambda in0, in1, s0, s1, imm2: np.minimum(np.maximum(np.rint(in0 * s1), -imm2), imm2))

def _hadamard(n):
    H = np.ones((1, 1), np.float32)
    while H.shape[0] < n:
        H = np.block([[H, H], [H, -H]])
    return H

H128 = _hadamard(128)
H16 = _hadamard(16)
H64 = _hadamard(64)
M1 = np.zeros((128, 128), np.float32)
for c in range(16):
    for a in range(8):
        for cp in range(16):
            M1[c * 8 + a, cp * 8 + a] = H16[c, cp]
M2 = np.zeros((128, 128), np.float32)
for c in range(64):
    for a in range(2):
        for cp in range(64):
            M2[c * 2 + a, cp * 2 + a] = H64[c, cp]
L1K = np.triu(np.ones((128, 128), np.float32), 0)
PEN = np.tril(np.ones((128, 128), np.float32), -1) * (-10000.0)
I128 = np.eye(128, dtype=np.float32)
OH16 = np.zeros((16, 16 * 128), np.float32)
for j in range(16):
    OH16[j, j * 128:(j + 1) * 128] = 1.0
OH64 = np.zeros((64, 64 * 128), np.float32)
for h in range(64):
    OH64[h, h * 128:(h + 1) * 128] = 1.0

NCORE = 8
TOWN = 2048
HALO = 256
TT = TOWN + HALO
NCH = TT // 128
NOWNCH = TOWN // 128
DM = 2048
CONVD = 5120
DI = 4096
DS = 64
NG = 8
NH = 64
PH = 64
FFI = 8192


class P:
    ENGINES = ("sync", "gpsimd", "tensor", "vector", "scalar")
    CHAIN = {"vector": "ov", "scalar": "oc"}

    def __init__(self, nc, stack):
        self.nc = nc
        self.plans = {e: [] for e in self.ENGINES}
        self.sems = {}
        self.counts = {}
        self.stack = stack

    def sem(self, name):
        if name not in self.sems:
            self.sems[name] = self.stack.enter_context(self.nc.semaphore(name))
            self.counts[name] = 0
        return self.sems[name]

    def emit(self, eng, fn, waits=(), inc=None):
        w = list(waits)
        if eng in self.CHAIN:
            assert inc is None
            sname = self.CHAIN[eng]
            self.sem(sname)
            if self.counts[sname] > 0:
                w.append((sname, self.counts[sname]))
            self.counts[sname] += 1
            inc = (sname, 1)
            out_count = self.counts[sname]
        elif inc is not None:
            sname, amt = inc
            self.sem(sname)
            self.counts[sname] += amt
            out_count = self.counts[sname]
        else:
            out_count = None
        for sname2, _v in w:
            self.sem(sname2)
        self.plans[eng].append((w, fn, inc))
        return out_count

    def cur(self, sname):
        self.sem(sname)
        return self.counts[sname]

    def barrier(self, waits):
        for eng in self.ENGINES:
            if eng in self.CHAIN:
                self.emit(eng, lambda e: e.nop(),
                          waits=[x for x in waits if x[0] != self.CHAIN[eng]])
            else:
                self.emit(eng, lambda e: e.nop(), waits=list(waits))

    def build_block(self, block, final_waits):
        for eng_name in self.ENGINES:
            plan = self.plans[eng_name]
            dec = getattr(block, eng_name)

            def mk(eng_name, plan):
                def body(eng):
                    for waits, fn, inc in plan:
                        for sname, v in waits:
                            eng.wait_ge(self.sems[sname], v)
                        r = fn(eng)
                        if inc is not None:
                            r.then_inc(self.sems[inc[0]], inc[1])
                    if eng_name == "sync":
                        for sname, v in final_waits:
                            eng.wait_ge(self.sems[sname], v)
                return body
            dec(mk(eng_name, plan))


def build_program(upto_phase=None):
    import os
    UPTO = int(os.environ.get("K_UPTO", "9")) if upto_phase is None else upto_phase
    BSUB = int(os.environ.get("K_BSUB", "9"))
    nc = bass.Bass("TRN2", target_bir_lowering=False, debug=False)

    x_fm = nc.dram_tensor("x_fm", [DM, TT], F32, kind="ExternalInput").ap()
    wt_in = nc.dram_tensor("wt_in", [DM, 9280], F32, kind="ExternalInput").ap()
    convw = nc.dram_tensor("convw", [128, 40 * 4], F32, kind="ExternalInput").ap()
    convb = nc.dram_tensor("convb", [128, 40], F32, kind="ExternalInput").ap()
    zb = nc.dram_tensor("zb", [128, 32], F32, kind="ExternalInput").ap()
    dvec = nc.dram_tensor("dvec", [1, 64], F32, kind="ExternalInput").ap()
    wt_out = nc.dram_tensor("wt_out", [DI, DM], F32, kind="ExternalInput").ap()
    wg_p = nc.dram_tensor("wg_p", [DM, FFI], F32, kind="ExternalInput").ap()
    wu_p = nc.dram_tensor("wu_p", [DM, FFI], F32, kind="ExternalInput").ap()
    wd_p = nc.dram_tensor("wd_p", [FFI, DM], F32, kind="ExternalInput").ap()
    hm_fm = nc.dram_tensor("hm_fm", [128, 1], F32, kind="ExternalInput").ap()
    out_fm = nc.dram_tensor("out_fm", [DM, TOWN], F32, kind="ExternalOutput").ap()

    xbc_sp = nc.dram_tensor("xbc_sp", [CONVD, TT], F32, kind="Internal").ap()
    z_sp = nc.dram_tensor("z_sp", [DI, TT], F32, kind="Internal").ap()
    ghi_sp = nc.dram_tensor("ghi_sp", [DI, TOWN], BF16, kind="Internal").ap()
    glo_sp = nc.dram_tensor("glo_sp", [DI, TOWN], BF16, kind="Internal").ap()
    y1_sp = nc.dram_tensor("y1_sp", [DM, TOWN], F32, kind="Internal").ap()
    y1f_sc = nc.dram_tensor("y1f_sc", [16, 128, TOWN], F32, kind="Internal").ap()
    sam_sc = nc.dram_tensor("sam_sc", [64, 128, TOWN], F32, kind="Internal").ap()
    wdt_sp = nc.dram_tensor("wdt_sp", [FFI, DM], BF16, kind="Internal").ap()
    dbg = nc.dram_tensor("dbg", [128, 2 * TT], F32, kind="ExternalOutput").ap()

    h128_c = nc.inline_tensor(H128, name="h128c").ap()
    m1_c = nc.inline_tensor(M1, name="m1c").ap()
    m2_c = nc.inline_tensor(M2, name="m2c").ap()
    l1_c = nc.inline_tensor(L1K, name="l1c").ap()
    pen_c = nc.inline_tensor(PEN, name="penc").ap()
    i128_c = nc.inline_tensor(I128, name="i128c").ap()
    oh_c = nc.inline_tensor(OH64, name="ohc").ap()
    noh_c = nc.inline_tensor(-OH64, name="nohc").ap()
    oh16_c = nc.inline_tensor(OH16, name="oh16c").ap()
    onesr_c = nc.inline_tensor(np.ones((1, 128), np.float32), name="onesrc").ap()
    onescol_c = nc.inline_tensor(np.ones((128, 1), np.float32), name="onescolc").ap()

    with ExitStack() as stack:
        def sbuf(name, shape, dtype=F32, st=None):
            return (st or stack).enter_context(nc.sbuf_tensor(name, shape, dtype))
        def psum(name, shape, dtype=F32, st=None):
            return (st or stack).enter_context(nc.psum_tensor(name, shape, dtype))

        p = P(nc, stack)
        block = stack.enter_context(nc.Block())

        s_h128 = sbuf("s_h128", [128, 128])
        s_m1 = sbuf("s_m1", [128, 128])
        s_m2 = sbuf("s_m2", [128, 128])
        s_l1 = sbuf("s_l1", [128, 128])
        s_pen_bf = sbuf("s_pen_bf", [128, 128], BF16)
        s_i128 = sbuf("s_i128", [128, 128])
        s_i128bf = sbuf("s_i128bf", [128, 128], BF16)
        s_onesr = sbuf("s_onesr", [1, 128])
        s_onesr_bf = sbuf("s_onesr_bf", [1, 128], BF16)
        s_onescol = sbuf("s_onescol", [128, 1])
        s_convw = sbuf("s_convw", [128, 40 * 4])
        s_convb = sbuf("s_convb", [128, 40])
        s_zb = sbuf("s_zb", [128, 32])
        s_dvec = sbuf("s_dvec", [1, 64])
        s_alog = sbuf("s_alog", [64, TT])
        BANKS = [psum(f"pbk{i}", [128, 512]) for i in range(8)]
        pA = BANKS[:4]

        for src, dst in [(h128_c, s_h128), (m1_c, s_m1), (m2_c, s_m2), (l1_c, s_l1),
                         (i128_c, s_i128), (onesr_c, s_onesr), (onescol_c, s_onescol),
                         (convw, s_convw), (convb, s_convb), (zb, s_zb), (dvec, s_dvec)]:
            p.emit("sync", lambda e, s=src, d=dst: e.dma_start(out=d[:], in_=s[:]),
                   inc=("qC", 16))
        for src, dst in [(pen_c, s_pen_bf), (i128_c, s_i128bf), (onesr_c, s_onesr_bf)]:
            p.emit("gpsimd", lambda e, s=src, d=dst: e.dma_start(out=d[:], in_=s[:]),
                   inc=("qCg", 16))
        consts_qS = p.cur("qC")
        consts_qG = p.cur("qCg")
        p.barrier([("qC", consts_qS), ("qCg", consts_qG)])

        stA = ExitStack()
        EW = 128
        HLEN0, HOFF1, HLEN1 = 1152, 1149, 1155
        HL = max(HLEN0, HLEN1)
        s_xhi = sbuf("s_xhi", [128, 16 * HL], BF16, st=stA)
        s_xlo = sbuf("s_xlo", [128, 16 * HL], BF16, st=stA)
        s_xf = sbuf("s_xf", [128, 2 * HL], st=stA)
        s_wf = sbuf("s_wf", [128, 4 * 16 * EW], st=stA)
        s_whi = sbuf("s_whi", [128, 4 * 16 * EW], BF16, st=stA)
        s_wlo = sbuf("s_wlo", [128, 4 * 16 * EW], BF16, st=stA)
        s_acc = sbuf("s_acc", [128, 2 * HL], st=stA)
        s_cv = sbuf("s_cv", [128, HL], st=stA)
        s_out = sbuf("s_out", [128, 2 * HL], st=stA)

        xv = x_fm.rearrange("(k p) t -> p k t", p=128)
        wv = wt_in.rearrange("(k p) e -> p k e", p=128)
        ET = [("xbc", i) for i in range(40)] + [("z", i) for i in range(32)] + [("a", 0)]
        grp_seq = 0
        spq_count = {0: 0, 1: 0}

        last_half_ov = 0
        evac_rec = {}
        wsplit_done = {}

        def _w_tile(g):
            kind, idx = ET[g % len(ET)]
            if kind == "xbc":
                return idx * 128, 128
            elif kind == "z":
                return 5120 + idx * 128, 128
            return 9216, 64

        def emit_wload(g):
            e0, ewid = _w_tile(g)
            wb = g % 4
            wbase = wb * 16 * EW
            waits_wf = [("ov", wsplit_done[g - 4])] if g >= 4 else []
            waits_wb = [("ot", 3 * (g - 3))] if g >= 4 else []
            cS = p.emit("sync", lambda e, e0=e0, ewid=ewid, wbase=wbase: e.dma_start(
                out=s_wf[:, wbase:wbase + 16 * ewid].rearrange("p (k e) -> p k e", k=16),
                in_=wv[:, :, e0:e0 + ewid]), waits=waits_wf, inc=(f"wfq{wb}", 16))
            cG = p.emit("gpsimd", lambda e, e0=e0, ewid=ewid, wbase=wbase: e.dma_start(
                out=s_whi[:, wbase:wbase + 16 * ewid].rearrange("p (k e) -> p k e", k=16),
                in_=wv[:, :, e0:e0 + ewid]), waits=waits_wb, inc=(f"whq{wb}", 16))
            cWrd = p.emit("vector", lambda e, ewid=ewid, wbase=wbase: e.scalar_tensor_tensor(
                s_wlo[:, wbase:wbase + 16 * ewid], s_whi[:, wbase:wbase + 16 * ewid],
                -1.0, s_wf[:, wbase:wbase + 16 * ewid], ALU.mult, ALU.add),
                waits=[(f"wfq{wb}", cS), (f"whq{wb}", cG)])
            wsplit_done[g] = cWrd

        emit_wload(0)
        emit_wload(1)
        for th, (toff, tlen) in enumerate([(0, HLEN0), (HOFF1, HLEN1)]):
            xsplit_done = {}
            for k in range(16):
                buf = k % 2
                waits = [("ov", xsplit_done[k - 2])] if k >= 2 else (
                    [("ov", last_half_ov)] if th else [])
                c_q = p.emit("sync", lambda e, k=k, b=buf, toff=toff, tlen=tlen: e.dma_start(
                    out=s_xf[:, b * HL:b * HL + tlen], in_=xv[:, k, toff:toff + tlen]),
                    waits=waits, inc=(f"xq{buf}", 16))
                p.emit("vector", lambda e, k=k, b=buf, tlen=tlen: e.tensor_copy(
                    s_xhi[:, k * HL:k * HL + tlen], s_xf[:, b * HL:b * HL + tlen]),
                    waits=[(f"xq{buf}", c_q)])
                xsplit_done[k] = p.emit("vector", lambda e, k=k, b=buf, tlen=tlen:
                    e.scalar_tensor_tensor(
                        s_xlo[:, k * HL:k * HL + tlen], s_xhi[:, k * HL:k * HL + tlen],
                        -1.0, s_xf[:, b * HL:b * HL + tlen], ALU.mult, ALU.add))
            ngw = [512] * (tlen // 512) + ([tlen % 512] if tlen % 512 else [])
            x_ready = p.cur("ov")

            for ei, (kind, idx) in enumerate(ET):
                gei = th * len(ET) + ei
                e0, ewid = _w_tile(gei)
                wb = gei % 4
                wbase = wb * 16 * EW
                if gei + 2 < 2 * len(ET):
                    emit_wload(gei + 2)
                cWrd = wsplit_done[gei]

                t0 = 0
                for tg, tw in enumerate(ngw):
                    bank = grp_seq % 4
                    waits_mm = [("ov", max(cWrd, x_ready))]
                    if grp_seq >= 4:
                        waits_mm.append(evac_rec[grp_seq - 4])
                    first = True
                    for k in range(16):
                        for pi, (wsb, xsb) in enumerate(((s_whi, s_xhi), (s_whi, s_xlo),
                                                         (s_wlo, s_xhi))):
                            w_ap = wsb[:, wbase + k * ewid: wbase + (k + 1) * ewid]
                            x_ap = xsb[:, k * HL + t0: k * HL + t0 + tw]
                            last = (k == 15 and pi == 2)
                            teA_c = p.emit("tensor", lambda e, w=w_ap, x=x_ap, b=bank,
                                   ewid=ewid, tw=tw, f=first, l=last:
                                   e.matmul(pA[b][0:ewid, 0:tw], w, x, start=f, stop=l),
                                   waits=(waits_mm if first else []),
                                   inc=(("ot", 1) if last else None))
                            first = False
                    ob = (gei % 2) * HL
                    if kind == "a":
                        cv_ = p.emit("vector", lambda e, b=bank, t0=t0, tw=tw, toff=toff:
                               e.tensor_copy(s_alog[:, toff + t0: toff + t0 + tw],
                                             pA[b][0:64, 0:tw]),
                               waits=[("ot", teA_c)])
                        evac_rec[grp_seq] = ("ov", cv_)
                    elif kind == "z":
                        wz = [("ot", teA_c)]
                        if spq_count[gei % 2] > 0:
                            wz.append((f"spq{gei % 2}", 16 * spq_count[gei % 2]))
                        cs_ = p.emit("scalar", lambda e, b=bank, t0=t0, tw=tw, ob=ob, idx=idx:
                               e.activation(s_out[:, ob + t0: ob + t0 + tw], pA[b][:, 0:tw],
                                            AF.Silu, bias=s_zb[:, idx:idx + 1]),
                               waits=wz)
                        evac_rec[grp_seq] = ("oc", cs_)
                    else:
                        cv_ = p.emit("vector", lambda e, b=bank, t0=t0, tw=tw, ob=ob:
                               e.tensor_copy(s_acc[:, ob + t0: ob + t0 + tw],
                                             pA[b][:, 0:tw]),
                               waits=[("ot", teA_c)])
                        evac_rec[grp_seq] = ("ov", cv_)
                    t0 += tw
                    grp_seq += 1

                evA_after = [evac_rec[g] for g in range(grp_seq - len(ngw), grp_seq)]
                ob = (gei % 2) * HL
                sp0 = 0 if th == 0 else HLEN0
                lcl0 = 0 if th == 0 else 3
                splen = tlen - lcl0
                if kind == "xbc":
                    w3 = s_convw[:, idx * 4 + 3: idx * 4 + 4]
                    p.emit("vector", lambda e, ob=ob, w3=w3, tlen=tlen: e.tensor_scalar(
                        s_cv[:, 0:tlen], s_acc[:, ob:ob + tlen], w3, None, ALU.mult),
                        waits=list(evA_after))
                    cv_last = None
                    for kk, sh in ((2, 1), (1, 2), (0, 3)):
                        wk = s_convw[:, idx * 4 + kk: idx * 4 + kk + 1]
                        cv_last = p.emit("vector", lambda e, ob=ob, wk=wk, sh=sh, tlen=tlen:
                               e.scalar_tensor_tensor(
                                   s_cv[:, sh:tlen], s_acc[:, ob: ob + tlen - sh], wk,
                                   s_cv[:, sh:tlen], ALU.mult, ALU.add))
                    wsl = [("ov", cv_last)]
                    if spq_count[gei % 2] > 0:
                        wsl.append((f"spq{gei % 2}", 16 * spq_count[gei % 2]))
                    cs = p.emit("scalar", lambda e, ob=ob, idx=idx, tlen=tlen: e.activation(
                        s_out[:, ob:ob + tlen], s_cv[:, 0:tlen], AF.Silu,
                        bias=s_convb[:, idx:idx + 1]),
                        waits=wsl)
                    p.emit("sync", lambda e, ob=ob, idx=idx, sp0=sp0, lcl0=lcl0,
                           splen=splen: e.dma_start(
                        out=xbc_sp[idx * 128:(idx + 1) * 128, sp0:sp0 + splen],
                        in_=s_out[:, ob + lcl0: ob + lcl0 + splen]),
                        waits=[("oc", cs)], inc=(f"spq{gei % 2}", 16))
                    spq_count[gei % 2] += 1
                elif kind == "z":
                    p.emit("sync", lambda e, ob=ob, idx=idx, sp0=sp0, lcl0=lcl0,
                           splen=splen: e.dma_start(
                        out=z_sp[idx * 128:(idx + 1) * 128, sp0:sp0 + splen],
                        in_=s_out[:, ob + lcl0: ob + lcl0 + splen]),
                        waits=list(evA_after), inc=(f"spq{gei % 2}", 16))
                    spq_count[gei % 2] += 1

            last_half_ov = p.cur("ov")
        p.barrier([("spq0", p.cur("spq0")), ("spq1", p.cur("spq1")),
                   ("ov", p.cur("ov")), ("oc", p.cur("oc")), ("ot", p.cur("ot"))])
        stA.close()

        if UPTO >= 2:
            stB = ExitStack()
            s_dtT = sbuf("s_dtT", [128, NCH * 64], st=stB)
            s_DdT = sbuf("s_DdT", [128, NCH * 64], st=stB)
            s_dstT = sbuf("s_dstT", [128, NCH * 64], st=stB)
            s_ahi = sbuf("s_ahi", [64, NCH * 128], BF16, st=stB)
            s_alo = sbuf("s_alo", [64, NCH * 128], BF16, st=stB)
            s_nahi = sbuf("s_nahi", [64, NCH * 128], BF16, st=stB)
            s_nalo = sbuf("s_nalo", [64, NCH * 128], BF16, st=stB)
            s_ash = sbuf("s_ash", [128, NCH * 64], st=stB)
            s_dout = sbuf("s_dout", [128, NCH * 64], st=stB)
            s_d32 = sbuf("s_d32", [128, NCH * 32], st=stB)
            s_dbc = sbuf("s_dbc", [128, 64], st=stB)
            s_ohbf = sbuf("s_ohbf", [64, 64 * 128], BF16, st=stB)
            s_spk = sbuf("s_spk", [128, 32 * 64], st=stB)
            s_u1 = sbuf("s_u1", [128, 128], st=stB)
            s_hm = sbuf("s_hm", [128, 1], st=stB)

            stPre = ExitStack()
            s_prb = sbuf("s_prb", [128, NCH * 64], st=stPre)
            s_dtr = sbuf("s_dtr", [1, NCH * 64], st=stPre)
            s_ahsf = sbuf("s_ahsf", [64, NCH * 128], st=stPre)
            bank_rec = {}

            def bank_wait(i):
                return [bank_rec[i]] if i in bank_rec else []

            cg1 = p.emit("gpsimd", lambda e: e.dma_start(out=s_ohbf[:], in_=oh_c[:]),
                         inc=("qBg", 16))
            p.emit("gpsimd", lambda e: e.dma_start(out=s_hm[:], in_=hm_fm[:]),
                   inc=("qBg", 16))
            p.emit("vector", lambda e: e.tensor_scalar(s_u1[:], s_l1[:], -1.0, 1.0,
                                                       ALU.mult, ALU.add))
            p.emit("vector", lambda e: e.memset(s_spk[:], 0.0))
            p.barrier([("qBg", p.cur("qBg"))])

            for c in range(NCH):
                b = c % 2
                tw = bank_wait(b)
                cT = p.emit("tensor", lambda e, c=c, b=b: e.transpose(
                    BANKS[b][:, 0:64], s_alog[:, c * 128:(c + 1) * 128],
                    s_i128[0:64, 0:64]), waits=tw, inc=("ot", 1))
                cv = p.emit("vector", lambda e, c=c, b=b: e.tensor_copy(
                    s_prb[:, c * 64:(c + 1) * 64], BANKS[b][:, 0:64]),
                    waits=[("ot", cT)])
                bank_rec[b] = ("ov", cv)
            c1 = p.emit("scalar", lambda e: e.activation(s_dtT[:], s_prb[:], AF.Abs),
                        waits=[("ov", p.cur("ov"))])
            c2 = p.emit("scalar", lambda e: e.activation(s_dtT[:], s_dtT[:], AF.Exp, scale=-1.0))
            c3 = p.emit("vector", lambda e: e.tensor_scalar_add(s_dtT[:], s_dtT[:], 1.0),
                        waits=[("oc", c2)])
            c4 = p.emit("scalar", lambda e: e.activation(s_dtT[:], s_dtT[:], AF.Ln),
                        waits=[("ov", c3)])
            c5 = p.emit("scalar", lambda e: e.activation(s_DdT[:], s_prb[:], AF.Relu))
            c6 = p.emit("vector", lambda e: e.tensor_add(s_dtT[:], s_dtT[:], s_DdT[:]),
                        waits=[("oc", c5), ("oc", c4)])
            c7 = p.emit("vector", lambda e: e.reciprocal(s_DdT[:], s_dtT[:]))
            cT = p.emit("tensor", lambda e: e.matmul(BANKS[0][:, 0:64], s_onesr[:],
                                                     s_dvec[:], start=True, stop=True),
                        waits=bank_wait(0), inc=("ot", 1))
            cv = p.emit("vector", lambda e: e.tensor_copy(s_dbc[:], BANKS[0][:, 0:64]),
                        waits=[("ot", cT)])
            bank_rec[0] = ("ov", cv)
            c8 = p.emit("vector", lambda e: e.tensor_tensor(
                s_DdT[:].rearrange("p (c h) -> p c h", c=NCH),
                s_DdT[:].rearrange("p (c h) -> p c h", c=NCH),
                s_dbc[:].unsqueeze(1).broadcast_to([128, NCH, 64]), ALU.mult))
            c9 = p.emit("vector", lambda e: e.tensor_scalar_mul(s_prb[:], s_dtT[:], -1.0))
            for c in range(NCH):
                b0, b1 = c % 2, 2 + (c % 2)
                ndt = s_prb[:, c * 64:(c + 1) * 64]
                w0 = bank_wait(b0) + ([("ov", c9)] if c <= 1 else [])
                cT0 = p.emit("tensor", lambda e, ndt=ndt, b0=b0: e.matmul(
                    BANKS[b0][0:64, 0:128], ndt, s_l1[:], start=True, stop=True),
                    waits=w0, inc=("ot", 1))
                cv0 = p.emit("vector", lambda e, c=c, b0=b0: e.tensor_copy(
                    s_ahsf[:, c * 128:(c + 1) * 128], BANKS[b0][0:64, 0:128]),
                    waits=[("ot", cT0)])
                bank_rec[b0] = ("ov", cv0)
                w1 = bank_wait(b1) + ([("ov", c9)] if c <= 1 else [])
                cT1 = p.emit("tensor", lambda e, ndt=ndt, b1=b1: e.matmul(
                    BANKS[b1][:, 0:64], s_l1[:], ndt, start=True, stop=True),
                    waits=w1, inc=("ot", 1))
                cT2 = p.emit("tensor", lambda e, ndt=ndt, b1=b1: e.matmul(
                    BANKS[b1][:, 64:128], s_u1[:], ndt, start=True, stop=True),
                    inc=("ot", 1))
                cv1 = p.emit("vector", lambda e, c=c, b1=b1: e.tensor_copy(
                    s_ash[:, c * 64:(c + 1) * 64], BANKS[b1][:, 0:64]),
                    waits=[("ot", cT2)])
                cv2 = p.emit("vector", lambda e, c=c, b1=b1: e.tensor_copy(
                    s_dstT[:, c * 64:(c + 1) * 64], BANKS[b1][:, 64:128]))
                bank_rec[b1] = ("ov", cv2)
            cdd = p.emit("scalar", lambda e: e.activation(s_dstT[:], s_dstT[:], AF.Exp),
                         waits=[("ov", p.cur("ov"))])
            cd1 = p.emit("scalar", lambda e: e.activation(s_dout[:], s_ash[:], AF.Exp))
            for j in range(3):
                b = j % 2
                tw = bank_wait(b)
                cT = p.emit("tensor", lambda e, j=j, b=b: e.matmul(
                    BANKS[b][0:1, 0:384], s_onescol[:], s_prb[:, j * 384:(j + 1) * 384],
                    start=True, stop=True), waits=tw, inc=("ot", 1))
                cv = p.emit("vector", lambda e, j=j, b=b: e.tensor_copy(
                    s_dtr[:, j * 384:(j + 1) * 384], BANKS[b][0:1, 0:384]),
                    waits=[("ot", cT)])
                bank_rec[b] = ("ov", cv)
            cd2 = p.emit("scalar", lambda e: e.activation(s_dtr[:], s_dtr[:], AF.Exp),
                         waits=[("ov", p.cur("ov"))])
            ch1 = p.emit("vector", lambda e: e.tensor_copy(s_ahi[:], s_ahsf[:]))
            ch2 = p.emit("vector", lambda e: e.scalar_tensor_tensor(
                s_alo[:], s_ahi[:], -1.0, s_ahsf[:], ALU.mult, ALU.add))
            ch3 = p.emit("vector", lambda e: e.tensor_scalar_mul(s_nahi[:], s_ahi[:], -1.0))
            ch4 = p.emit("vector", lambda e: e.tensor_scalar_mul(s_nalo[:], s_alo[:], -1.0))
            for c in range(NCH):
                b = c % 2
                tw = bank_wait(b) + ([("oc", cd2)] if c <= 1 else [])
                r0 = s_dtr[:, c * 64:(c + 1) * 64].rearrange("o (g m) -> o g m", g=8)
                cT0 = p.emit("tensor", lambda e, r0=r0, b=b: e.matmul(
                    BANKS[b][0:64, 0:32], s_onesr[:, 0:64], r0[:, 0::2, :],
                    start=True, stop=True), waits=tw, inc=("ot", 1))
                cT1 = p.emit("tensor", lambda e, r0=r0, b=b: e.matmul(
                    BANKS[b][64:128, 0:32], s_onesr[:, 0:64], r0[:, 1::2, :], start=True,
                    stop=True, tile_position=(0, 64)), inc=("ot", 1))
                cv = p.emit("vector", lambda e, c=c, b=b: e.tensor_copy(
                    s_d32[:, c * 32:(c + 1) * 32], BANKS[b][:, 0:32]), waits=[("ot", cT1)])
                bank_rec[b] = ("ov", cv)
            p.barrier([("ov", p.cur("ov")), ("oc", p.cur("oc")), ("ot", p.cur("ot"))])
            stPre.close()

            s_xbcr = sbuf("s_xbcr", [128, 2 * 40 * 128], st=stB)
            s_zr = sbuf("s_zr", [128, 32 * 128], st=stB)
            s_xtm = sbuf("s_xtm", [128, 32 * 128], st=stB)
            s_btm = sbuf("s_btm", [128, 4 * 128], st=stB)
            s_xd = sbuf("s_xd", [128, 32 * 128], st=stB)
            s_sct = sbuf("s_sct", [128, 8 * 128], st=stB)
            s_lmq = sbuf("s_lmq", [128, 2 * 512], st=stB)
            s_lsq = sbuf("s_lsq", [128, 2 * 512], st=stB)
            s_tmp1 = sbuf("s_tmp1", [128, 512], st=stB)
            s_yf = sbuf("s_yf", [128, 2 * 512], st=stB)
            s_ghi = sbuf("s_ghi", [128, 32 * 128], BF16, st=stB)
            s_glo = sbuf("s_glo", [128, 32 * 128], BF16, st=stB)

            xbv = xbc_sp.rearrange("(i p) t -> p i t", p=128)
            zv = z_sp.rearrange("(i p) t -> p i t", p=128)
            gh_v = ghi_sp.rearrange("(i p) t -> p i t", p=128)
            gl_v = glo_sp.rearrange("(i p) t -> p i t", p=128)
            xb_rel = {}
            z_rel = {}
            gat_rel = {}
            gsp_cnt = 0
            for c in range(NCH):
                own = c >= 2
                xsl_ = (c % 2) * 5120
                xb = s_xbcr[:, xsl_:xsl_ + 5120]
                zr = s_zr[:, 0:4096]
                wx = [xb_rel[c - 2]] if c >= 2 else []
                xbs = f"xbq{c % 2}"
                cxq = p.emit("sync", lambda e, c=c, xb=xb: e.dma_start(
                    out=xb.rearrange("p (i t) -> p i t", i=40),
                    in_=xbv[:, :, c * 128:(c + 1) * 128]),
                    waits=wx, inc=(xbs, 16))
                if own:
                    wz = [z_rel[c - 1]] if (c - 1) in z_rel else []
                    czq = p.emit("sync", lambda e, c=c, zr=zr: e.dma_start(
                        out=zr.rearrange("p (i t) -> p i t", i=32),
                        in_=zv[:, :, c * 128:(c + 1) * 128]),
                        waits=wz, inc=("zbq", 16))

                if c < 2:
                    chm = p.emit("vector", lambda e, xb=xb: e.tensor_scalar_mul(
                        xb, xb, s_hm[:, 0:1]), waits=[(xbs, cxq)])
                    xwait = [("ov", chm)]
                else:
                    xwait = [(xbs, cxq)]

                for grp in range(9 if BSUB >= 2 else 0):
                    b = grp % 2
                    tw = bank_wait(b)
                    cT = None
                    for j in range(4):
                        i = grp * 4 + j
                        src = xb[:, i * 128:(i + 1) * 128]
                        cT = p.emit("tensor", lambda e, src=src, b=b, j=j: e.transpose(
                            BANKS[b][:, j * 128:(j + 1) * 128], src, s_i128[:]),
                            waits=((tw + xwait) if j == 0 else []),
                            inc=(("ot", 1) if j == 3 else None))
                    dst = (s_xtm[:, grp * 512:(grp + 1) * 512] if grp < 8
                           else s_btm[:, 0:512])
                    cv = p.emit("vector", lambda e, dst=dst, b=b: e.tensor_copy(
                        dst, BANKS[b][:, 0:512]), waits=[("ot", cT)])
                    bank_rec[b] = ("ov", cv)
                xtm_done = p.cur("ov")

                if BSUB < 2:
                    xb_rel[c] = (xbs, cxq)
                    if own:
                        z_rel[c] = ("zbq", czq)
                    continue
                cxd = p.emit("vector", lambda e, c=c: e.tensor_tensor(
                    s_xd[:].rearrange("p (h q) -> p h q", h=64),
                    s_xtm[:].rearrange("p (h q) -> p h q", h=64),
                    s_dstT[:, c * 64:(c + 1) * 64].unsqueeze(2).broadcast_to([128, 64, 64]),
                    ALU.mult))

                if own and BSUB >= 3:
                    for par in range(2):
                        b = 2 + par
                        tw = bank_wait(b)
                        cT = None
                        for j in range(4):
                            ti_b = 32 + j
                            ti_c = 36 + j
                            pr = par * 64
                            lb = xb[pr:pr + 64, ti_b * 128:(ti_b + 1) * 128]
                            rc_ = xb[pr:pr + 64, ti_c * 128:(ti_c + 1) * 128]
                            cT = p.emit("tensor", lambda e, lb=lb, rc_=rc_, b=b, j=j:
                                        e.matmul(BANKS[b][:, j * 128:(j + 1) * 128], lb, rc_,
                                                 start=True, stop=True),
                                        waits=(tw if j == 0 else []),
                                        inc=(("ot", 1) if j == 3 else None))
                        dst = s_sct[:].rearrange("p (j q) -> p j q", j=4)[
                            :, :, par * 128:(par + 1) * 128]
                        cv = p.emit("vector", lambda e, dst=dst, b=b: e.tensor_copy(
                            dst, BANKS[b][:, 0:512].rearrange("p (j s) -> p j s", j=4)),
                            waits=[("ot", cT)])
                        bank_rec[b] = ("ov", cv)
                    sct_done = p.cur("ov")

                    for q in range(16 if BSUB >= 4 else 0):
                        qb = q % 2
                        b = qb
                        tw = bank_wait(b)
                        cT = None
                        for j in range(4):
                            h = q * 4 + j
                            sl = slice(j * 128, (j + 1) * 128)
                            ohs = s_ohbf[:, h * 128:(h + 1) * 128]
                            ahi = s_ahi[:, c * 128:(c + 1) * 128]
                            alo = s_alo[:, c * 128:(c + 1) * 128]
                            nahi = s_nahi[:, c * 128:(c + 1) * 128]
                            nalo = s_nalo[:, c * 128:(c + 1) * 128]
                            p.emit("tensor", lambda e, ohs=ohs, ahi=ahi, b=b, sl=sl:
                                   e.matmul(BANKS[b][:, sl], ohs, ahi, start=True, stop=False),
                                   waits=(tw if j == 0 else []))
                            p.emit("tensor", lambda e, ohs=ohs, alo=alo, b=b, sl=sl:
                                   e.matmul(BANKS[b][:, sl], ohs, alo, start=False, stop=False))
                            p.emit("tensor", lambda e, ohs=ohs, nahi=nahi, b=b, sl=sl:
                                   e.matmul(BANKS[b][:, sl], nahi, ohs, start=False, stop=False))
                            p.emit("tensor", lambda e, ohs=ohs, nalo=nalo, b=b, sl=sl:
                                   e.matmul(BANKS[b][:, sl], nalo, ohs, start=False, stop=False))
                            cT = p.emit("tensor", lambda e, b=b, sl=sl:
                                        e.matmul(BANKS[b][:, sl], s_i128bf[:], s_pen_bf[:],
                                                 start=False, stop=True),
                                        inc=(("ot", 1) if j == 3 else None))
                        lq = (q % 2) * 512
                        cs = p.emit("scalar", lambda e, b=b, lq=lq: e.activation(
                            s_lmq[:, lq:lq + 512], BANKS[b][:, 0:512], AF.Exp),
                            waits=[("ot", cT)])
                        bank_rec[b] = ("oc", cs)
                        g = q >> 1
                        cls = p.emit("vector", lambda e, g=g, lq=lq: e.tensor_tensor(
                            s_lsq[:, lq:lq + 512].rearrange("p (j s) -> p j s", j=4),
                            s_lmq[:, lq:lq + 512].rearrange("p (j s) -> p j s", j=4),
                            s_sct[:, g * 128:(g + 1) * 128].unsqueeze(1)
                            .broadcast_to([128, 4, 128]), ALU.mult),
                            waits=[("oc", cs), ("ov", sct_done)])

                        yb = 4 + qb
                        ob_ = 6 + qb
                        for j in range(4):
                            h = q * 4 + j
                            lsl = s_lsq[:, lq + j * 128: lq + (j + 1) * 128]
                            xsl = s_xtm[:, h * 64:(h + 1) * 64]
                            w0 = (bank_wait(yb) + [("ov", cls)]) if j == 0 else []
                            p.emit("tensor", lambda e, lsl=lsl, xsl=xsl, yb=yb, j=j:
                                   e.matmul(BANKS[yb][:, j * 64:(j + 1) * 64],
                                            lsl, xsl, start=True, stop=True),
                                   waits=w0)
                        for j in range(4):
                            h = q * 4 + j
                            g2 = h >> 3
                            ti_c = 36 + g2 // 2
                            pr = (g2 % 2) * 64
                            csl = xb[pr:pr + 64, ti_c * 128:(ti_c + 1) * 128]
                            jj = 8 * (g2 >> 1) + (h % 8)
                            hrow = (g2 % 2) * 64
                            ssl = s_spk[hrow:hrow + 64, jj * 64:(jj + 1) * 64]
                            w1 = bank_wait(ob_) if j == 0 else []
                            cyo = p.emit("tensor", lambda e, csl=csl, ssl=ssl, ob_=ob_, j=j:
                                   e.matmul(BANKS[ob_][:, j * 64:(j + 1) * 64], csl, ssl,
                                            start=True, stop=True),
                                   waits=w1, inc=(("ot", 1) if j == 3 else None))
                        dv_ = s_dout[:, c * 64 + q * 4: c * 64 + (q + 1) * 4]
                        ddv = s_DdT[:, c * 64 + q * 4: c * 64 + (q + 1) * 4]
                        c1_ = p.emit("vector", lambda e, ob_=ob_, dv_=dv_: e.tensor_tensor(
                            s_tmp1[:, 0:256].rearrange("p (j q2) -> p j q2", j=4),
                            BANKS[ob_][:, 0:256].rearrange("p (j q2) -> p j q2", j=4),
                            dv_.unsqueeze(2).broadcast_to([128, 4, 64]), ALU.mult),
                            waits=[("ot", cyo)])
                        yq = (q % 2) * 512
                        c2_ = p.emit("vector", lambda e, yb=yb, yq=yq: e.tensor_add(
                            s_yf[:, yq: yq + 256], s_tmp1[:, 0:256], BANKS[yb][:, 0:256]))
                        bank_rec[yb] = ("ov", c2_)
                        bank_rec[ob_] = ("ov", c2_)
                        c3_ = p.emit("vector", lambda e, q=q, ddv=ddv: e.tensor_tensor(
                            s_tmp1[:, 0:256].rearrange("p (j q2) -> p j q2", j=4),
                            s_xtm[:, q * 256:(q + 1) * 256].rearrange("p (j q2) -> p j q2", j=4),
                            ddv.unsqueeze(2).broadcast_to([128, 4, 64]), ALU.mult))
                        c4_ = p.emit("vector", lambda e, yq=yq: e.tensor_add(
                            s_yf[:, yq: yq + 256], s_yf[:, yq: yq + 256], s_tmp1[:, 0:256]))
                        gb2 = 2 + qb
                        tw2 = bank_wait(gb2)
                        cTg = None
                        for j in range(2):
                            src = s_yf[:, yq + j * 128: yq + (j + 1) * 128]
                            cTg = p.emit("tensor", lambda e, src=src, gb2=gb2, j=j:
                                         e.transpose(BANKS[gb2][:, j * 128:(j + 1) * 128],
                                                     src, s_i128[:]),
                                         waits=((tw2 + [("ov", c4_)]) if j == 0 else []),
                                         inc=(("ot", 1) if j == 1 else None))
                        cvg = p.emit("vector", lambda e, q=q, gb2=gb2, zr=zr: e.tensor_tensor(
                            zr[:, q * 256:(q + 1) * 256], BANKS[gb2][:, 0:256],
                            zr[:, q * 256:(q + 1) * 256], ALU.mult),
                            waits=[("ot", cTg), ("zbq", czq)])
                        bank_rec[gb2] = ("ov", cvg)
                    z_rel[c] = ("ov", p.cur("ov"))
                if own and BSUB >= 5:

                    wgs = [] if gsp_cnt == 0 else [("gsq", 16 * gsp_cnt)]
                    cg1_ = p.emit("vector", lambda e, zr=zr: e.tensor_copy(s_ghi[:], zr),
                                  waits=wgs)
                    cg2_ = p.emit("vector", lambda e, zr=zr: e.scalar_tensor_tensor(
                        s_glo[:], s_ghi[:], -1.0, zr, ALU.mult, ALU.add))
                    p.emit("sync", lambda e, c=c: e.dma_start(
                        out=gh_v[:, :, (c - 2) * 128:(c - 1) * 128],
                        in_=s_ghi[:].rearrange("p (i t) -> p i t", i=32)),
                        waits=[("ov", cg2_)], inc=("gsq", 16))
                    p.emit("sync", lambda e, c=c: e.dma_start(
                        out=gl_v[:, :, (c - 2) * 128:(c - 1) * 128],
                        in_=s_glo[:].rearrange("p (i t) -> p i t", i=32)),
                        inc=("gsq", 16))
                    gsp_cnt += 2
                    z_rel[c] = ("ov", cg2_)

                for sb in range(4):
                    b = [2, 3, 4, 5][sb] if own else [2, 3, 4, 5][sb]
                    tw = bank_wait(b)
                    cst = None
                    for j8 in range(8):
                        jj = sb * 8 + j8
                        for hh in range(2):
                            h = 16 * (jj >> 3) + (jj % 8) + hh * 8
                            g = h >> 3
                            bsl = s_btm[:, g * 64:(g + 1) * 64]
                            xdsl = s_xd[:, h * 64:(h + 1) * 64]
                            pr = hh * 64
                            lastst = (j8 == 7 and hh == 1)
                            w0 = (tw + [("ov", cxd)]) if (j8 == 0 and hh == 0) else []
                            kw = {"tile_position": (0, 64)} if pr else {}
                            cst = p.emit("tensor", lambda e, bsl=bsl, xdsl=xdsl, b=b, j8=j8,
                                   pr=pr, kw=kw: e.matmul(
                                       BANKS[b][pr:pr + 64, j8 * 64:(j8 + 1) * 64],
                                       bsl, xdsl, start=True, stop=True, **kw),
                                   waits=w0, inc=(("ot", 1) if lastst else None))
                    cvu = None
                    for j8 in range(8):
                        jj = sb * 8 + j8
                        dcol = s_d32[:, c * 32 + jj: c * 32 + jj + 1]
                        cvu = p.emit("vector", lambda e, jj=jj, b=b, j8=j8, dcol=dcol:
                                     e.scalar_tensor_tensor(
                                         s_spk[:, jj * 64:(jj + 1) * 64],
                                         s_spk[:, jj * 64:(jj + 1) * 64], dcol,
                                         BANKS[b][:, j8 * 64:(j8 + 1) * 64],
                                         ALU.mult, ALU.add),
                                     waits=([("ot", cst)] if j8 == 0 else []))
                    bank_rec[b] = ("ov", cvu)
                if own and c not in z_rel:
                    z_rel[c] = ("zbq", czq)
                xb_rel[c] = ("ot", p.cur("ot"))

            p.barrier([("gsq", p.cur("gsq")),
                       ("ov", p.cur("ov")), ("oc", p.cur("oc")), ("ot", p.cur("ot"))])
            stB.close()

        if UPTO >= 3:
            stC = ExitStack()
            s_gh = sbuf("s_gh", [128, 32 * 512], BF16, st=stC)
            s_gl = sbuf("s_gl", [128, 32 * 512], BF16, st=stC)
            s_wof = sbuf("s_wof", [128, 2 * 32 * 128], st=stC)
            s_wohi = sbuf("s_wohi", [128, 32 * 128], BF16, st=stC)
            s_wolo = sbuf("s_wolo", [128, 32 * 128], BF16, st=stC)
            s_xc = sbuf("s_xc", [128, 16 * 512], st=stC)
            s_res = sbuf("s_res", [128, 16 * 512], st=stC)
            s_sq = sbuf("s_sq", [128, 512], st=stC)
            s_st = sbuf("s_st", [1, 4 * 512], st=stC)
            s_mb = sbuf("s_mb", [128, 512], st=stC)
            s_rb = sbuf("s_rb", [128, 512], st=stC)
            s_y1t = sbuf("s_y1t", [128, 512], st=stC)
            s_isq = sbuf("s_isq", [128, 2], st=stC)

            seed_c = nc.inline_tensor(np.array([[2048.0, 8192.0]], np.float32), name="seedc").ap()
            s_seed = sbuf("s_seed", [1, 2], st=stC)
            cq = p.emit("sync", lambda e: e.dma_start(out=s_seed[:], in_=seed_c[:]),
                        inc=("qCc", 16))
            csq = p.emit("scalar", lambda e: e.activation(s_seed[:], s_seed[:], AF.Sqrt),
                         waits=[("qCc", cq)])
            crc = p.emit("vector", lambda e: e.reciprocal(s_seed[:], s_seed[:]),
                         waits=[("oc", csq)])
            cT = p.emit("tensor", lambda e: e.matmul(BANKS[0][:, 0:2], s_onesr[:],
                                                     s_seed[:], start=True, stop=True),
                        waits=bank_wait(0) + [("ov", crc)], inc=("ot", 1))
            cv = p.emit("vector", lambda e: e.tensor_copy(s_isq[:], BANKS[0][:, 0:2]),
                        waits=[("ot", cT)])
            bank_rec[0] = ("ov", cv)

            ghv2 = ghi_sp.rearrange("(i p) t -> p i t", p=128)
            glv2 = glo_sp.rearrange("(i p) t -> p i t", p=128)
            wov = wt_out.rearrange("(k p) e -> p k e", p=128)
            xv2 = x_fm.rearrange("(k p) t -> p k t", p=128)
            y1v = y1_sp.rearrange("(k p) t -> p k t", p=128)
            y1fv = y1f_sc.rearrange("c p t -> p c t")

            wo_rel = {}
            gh_rel = {}
            last_bounce = None
            for tg in range(4):
                t0 = tg * 512
                wg_ = [gh_rel[tg - 1]] if tg >= 1 else []
                cgh = p.emit("sync", lambda e, t0=t0: e.dma_start(
                    out=s_gh[:].rearrange("p (i t) -> p i t", i=32),
                    in_=ghv2[:, :, t0:t0 + 512]), waits=wg_, inc=("ghq", 16))
                cgl = p.emit("sync", lambda e, t0=t0: e.dma_start(
                    out=s_gl[:].rearrange("p (i t) -> p i t", i=32),
                    in_=glv2[:, :, t0:t0 + 512]), inc=("ghq", 16))
                cxc = p.emit("sync", lambda e, t0=t0: e.dma_start(
                    out=s_xc[:].rearrange("p (k t) -> p k t", k=16),
                    in_=xv2[:, :, HALO + t0: HALO + t0 + 512]), inc=("xcq", 16))
                res_done = {}
                for oi in range(16):
                    gwo = tg * 16 + oi
                    wb = gwo % 2
                    wbase = wb * 32 * 128
                    wfw = [wo_rel[gwo - 2]] if gwo >= 2 else []
                    cwf = p.emit("sync", lambda e, oi=oi, wbase=wbase: e.dma_start(
                        out=s_wof[:, wbase:wbase + 32 * 128].rearrange("p (k e) -> p k e", k=32),
                        in_=wov[:, :, oi * 128:(oi + 1) * 128]),
                        waits=wfw, inc=(f"woq{wb}", 16))
                    whw = [("ot", p.cur("ot"))] if gwo >= 2 else []
                    cwh = p.emit("vector", lambda e, wbase=wbase: e.tensor_copy(
                        s_wohi[:], s_wof[:, wbase:wbase + 32 * 128]),
                        waits=[(f"woq{wb}", cwf)] + ([] if gwo < 2 else [wo_mm_rel]))
                    cwl = p.emit("vector", lambda e, wbase=wbase: e.scalar_tensor_tensor(
                        s_wolo[:], s_wohi[:], -1.0, s_wof[:, wbase:wbase + 32 * 128],
                        ALU.mult, ALU.add))
                    wo_rel[gwo] = ("ov", cwl)
                    b = 2 + (gwo % 2)
                    tw = bank_wait(b)
                    first = True
                    cmm = None
                    for k in range(32):
                        for (wsb, gsb) in ((s_wohi, s_gh), (s_wohi, s_gl), (s_wolo, s_gh)):
                            w_ap = wsb[:, k * 128:(k + 1) * 128]
                            g_ap = gsb[:, k * 512:(k + 1) * 512]
                            last = (k == 31 and wsb is s_wolo)
                            wts = (tw + [("ov", cwl), ("ghq", cgl)]) if first else []
                            cmm = p.emit("tensor", lambda e, w=w_ap, g=g_ap, b=b, f=first,
                                         l=last: e.matmul(BANKS[b][:, 0:512], w, g,
                                                          start=f, stop=l),
                                         waits=wts, inc=(("ot", 1) if last else None))
                            first = False
                    wo_mm_rel = ("ot", cmm)
                    cre = p.emit("vector", lambda e, oi=oi, b=b: e.tensor_add(
                        s_res[:, oi * 512:(oi + 1) * 512], BANKS[b][:, 0:512],
                        s_xc[:, oi * 512:(oi + 1) * 512]),
                        waits=[("ot", cmm), ("xcq", cxc)])
                    bank_rec[b] = ("ov", cre)
                    res_done[oi] = cre
                    csqr = p.emit("scalar", lambda e, oi=oi: e.activation(
                        s_sq[:], s_res[:, oi * 512:(oi + 1) * 512], AF.Square),
                        waits=[("ov", cre)])
                    wst = bank_wait(4) if oi == 0 else []
                    p.emit("tensor", lambda e, oi=oi: e.matmul(
                        BANKS[4][0:1, 0:512], s_onescol[:],
                        s_res[:, oi * 512:(oi + 1) * 512],
                        start=(oi == 0), stop=(oi == 15)),
                        waits=(wst + [("ov", cre)]))
                    cst2 = p.emit("tensor", lambda e, oi=oi: e.matmul(
                        BANKS[4][0:1, 512:1024] if False else BANKS[5][0:1, 0:512],
                        s_onescol[:], s_sq[:],
                        start=(oi == 0), stop=(oi == 15)),
                        waits=(bank_wait(5) if oi == 0 else []) + [("oc", csqr)],
                        inc=(("ot", 1) if oi == 15 else None))
                cm1 = p.emit("vector", lambda e: e.tensor_scalar_mul(
                    s_st[0:1, 0:512], BANKS[4][0:1, 0:512], 1.0 / 2048.0),
                    waits=[("ot", cst2)])
                cm2 = p.emit("vector", lambda e: e.tensor_scalar_mul(
                    s_st[0:1, 512:1024], BANKS[5][0:1, 0:512], 1.0 / 2048.0))
                bank_rec[4] = ("ov", cm2)
                bank_rec[5] = ("ov", cm2)
                cm3 = p.emit("vector", lambda e: e.tensor_tensor(
                    s_st[0:1, 1024:1536], s_st[0:1, 0:512], s_st[0:1, 0:512], ALU.mult))
                cm4 = p.emit("vector", lambda e: e.scalar_tensor_tensor(
                    s_st[0:1, 512:1024], s_st[0:1, 512:1024], 1e-5,
                    s_st[0:1, 1024:1536], ALU.add, ALU.subtract))
                cm5 = p.emit("scalar", lambda e: e.activation(
                    s_st[0:1, 1024:1536], s_st[0:1, 512:1024], AF.Sqrt),
                    waits=[("ov", cm4)])
                cm6 = p.emit("vector", lambda e: e.reciprocal(
                    s_st[0:1, 1536:2048], s_st[0:1, 1024:1536]), waits=[("oc", cm5)])
                cb1 = p.emit("tensor", lambda e: e.matmul(
                    BANKS[6][:, 0:512], s_onesr[:], s_st[0:1, 0:512],
                    start=True, stop=True), waits=bank_wait(6) + [("ov", cm6)],
                    inc=("ot", 1))
                cb1v = p.emit("vector", lambda e: e.tensor_copy(s_mb[:], BANKS[6][:, 0:512]),
                              waits=[("ot", cb1)])
                bank_rec[6] = ("ov", cb1v)
                cb2 = p.emit("tensor", lambda e: e.matmul(
                    BANKS[7][:, 0:512], s_onesr[:], s_st[0:1, 1536:2048],
                    start=True, stop=True), waits=bank_wait(7) + [("ov", cm6)],
                    inc=("ot", 1))
                cb2v = p.emit("vector", lambda e: e.tensor_copy(s_rb[:], BANKS[7][:, 0:512]),
                              waits=[("ot", cb2)])
                bank_rec[7] = ("ov", cb2v)
                for oi in range(16):
                    wy1 = [("ov", cb2v)] + ([("y1q", last_bounce)] if last_bounce else [])
                    cy1 = p.emit("vector", lambda e, oi=oi: e.tensor_sub(
                        s_y1t[:], s_res[:, oi * 512:(oi + 1) * 512], s_mb[:]),
                        waits=wy1)
                    cy2 = p.emit("vector", lambda e: e.tensor_mul(
                        s_y1t[:], s_y1t[:], s_rb[:]))
                    cspl = p.emit("sync", lambda e, oi=oi, t0=t0: e.dma_start(
                        out=y1v[:, oi, t0:t0 + 512], in_=s_y1t[:]),
                        waits=[("ov", cy2)], inc=("y1q", 16))
                    b2 = 2 + (oi % 2)
                    cfa = p.emit("tensor", lambda e, b2=b2: e.matmul(
                        BANKS[b2][:, 0:512], s_h128[:], s_y1t[:], start=True, stop=True),
                        waits=bank_wait(b2) + [("ov", cy2)], inc=("ot", 1))
                    cfv = p.emit("vector", lambda e, b2=b2: e.tensor_copy(
                        s_y1t[:], BANKS[b2][:, 0:512]), waits=[("ot", cfa), ("y1q", cspl)])
                    bank_rec[b2] = ("ov", cfv)
                    last_bounce = p.emit("sync", lambda e, oi=oi, t0=t0: e.dma_start(
                        out=y1fv[:, oi, t0:t0 + 512], in_=s_y1t[:]),
                        waits=[("ov", cfv)], inc=("y1q", 16))
                gh_rel[tg] = ("ot", p.cur("ot"))

            p.barrier([("y1q", p.cur("y1q")), ("ov", p.cur("ov")), ("oc", p.cur("oc")),
                       ("ot", p.cur("ot"))])
            stC.close()

        if UPTO >= 4:
            s_rw = sbuf("s_rw", [128, 4])
            s_mwc = sbuf("s_mwc", [128, 4])
            stD0 = ExitStack()
            s_wsl = sbuf("s_wsl", [128, 2 * 8192], st=stD0)
            s_wac = sbuf("s_wac", [128, 100], st=stD0)
            s_rwr = sbuf("s_rwr", [1, 8], st=stD0)
            wgv = wg_p.rearrange("(k p) e -> p k e", p=128)
            wuv = wu_p.rearrange("(k p) e -> p k e", p=128)
            wdv = wd_p.rearrange("(b p) e -> p b e", p=128)
            red_rec = {}
            gsl = 0
            for wi, (wv_, nslab, fl, acc0) in enumerate([(wgv, 16, 8192, 0),
                                                         (wuv, 16, 8192, 16),
                                                         (wdv, 64, 2048, 32)]):
                for sl_ in range(nslab):
                    bb = gsl % 2
                    wfw = [red_rec[gsl - 2]] if gsl >= 2 else []
                    cwd = p.emit("sync", lambda e, wv_=wv_, sl_=sl_, bb=bb, fl=fl:
                                 e.dma_start(out=s_wsl[:, bb * 8192:bb * 8192 + fl],
                                             in_=wv_[:, sl_, :]),
                                 waits=wfw, inc=(f"wmq{bb}", 16))
                    col = acc0 + (sl_ if wi < 2 else sl_)
                    cgr = p.emit("vector", lambda e, bb=bb, fl=fl, col=col:
                                 e.reduce_sum(s_wac[:, col:col + 1].unsqueeze(2),
                                              s_wsl[:, bb * 8192:bb * 8192 + fl].unsqueeze(1),
                                              AX.X, apply_absolute_value=True),
                                 waits=[(f"wmq{bb}", cwd)])
                    red_rec[gsl] = ("ov", cgr)
                    gsl += 1
                nin = nslab
                cfold = p.emit("vector", lambda e, acc0=acc0, nin=nin, wi=wi: e.reduce_sum(
                    s_wac[:, 96 + wi:97 + wi].unsqueeze(2),
                    s_wac[:, acc0:acc0 + nin].unsqueeze(1), AX.X),
                    waits=[red_rec[gsl - 1]])
                cT = p.emit("tensor", lambda e, wi=wi: e.matmul(
                    BANKS[0][0:1, wi:wi + 1], s_onescol[:], s_wac[:, 96 + wi:97 + wi],
                    start=True, stop=True),
                    waits=bank_wait(0) + [("ov", cfold)], inc=("ot", 1))
                cmn = p.emit("vector", lambda e, wi=wi: e.tensor_scalar(
                    s_rwr[0:1, wi:wi + 1], BANKS[0][0:1, wi:wi + 1],
                    1.0 / 16777216.0, 1e-5, ALU.mult, ALU.max),
                    waits=[("ot", cT)])
                bank_rec[0] = ("ov", cmn)
                crc = p.emit("vector", lambda e, wi=wi: e.reciprocal(
                    s_rwr[0:1, 4 + wi:5 + wi], s_rwr[0:1, wi:wi + 1]))
            cT = p.emit("tensor", lambda e: e.matmul(
                BANKS[1][:, 0:3], s_onesr[:], s_rwr[0:1, 0:3], start=True, stop=True),
                waits=bank_wait(1) + [("ov", p.cur("ov"))], inc=("ot", 1))
            cT2 = p.emit("tensor", lambda e: e.matmul(
                BANKS[1][:, 4:7], s_onesr[:], s_rwr[0:1, 4:7], start=True, stop=True),
                inc=("ot", 1))
            cv = p.emit("vector", lambda e: e.tensor_copy(s_mwc[:, 0:3], BANKS[1][:, 0:3]),
                        waits=[("ot", cT2)])
            cv2 = p.emit("vector", lambda e: e.tensor_copy(s_rw[:, 0:3], BANKS[1][:, 4:7]))
            bank_rec[1] = ("ov", cv2)
            p.barrier([("ov", p.cur("ov")), ("ot", p.cur("ot")), ("og", p.cur("og"))])
            stD0.close()

        if UPTO >= 5:
            stD1 = ExitStack()
            s_q1 = sbuf("s_q1", [128, 16 * 2048], BF16, st=stD1)
            s_mx = sbuf("s_mx", [128, 2048], st=stD1)
            s_row1 = sbuf("s_row1", [128, 2048], st=stD1)
            s_fg = sbuf("s_fg", [128, 2048], st=stD1)
            s_fu = sbuf("s_fu", [128, 2048], st=stD1)
            s_shin = sbuf("s_shin", [128, 2 * 512], st=stD1)
            s_abs1 = sbuf("s_abs1", [128, 512], st=stD1)
            s_isqd = sbuf("s_isqd", [128, 2], st=stD1)
            s_wgf = sbuf("s_wgf", [128, 16 * 512], st=stD1)
            s_wgt = sbuf("s_wgt", [128, 16 * 512], BF16, st=stD1)
            s_wut = sbuf("s_wut", [128, 16 * 512], BF16, st=stD1)
            s_gv = sbuf("s_gv", [128, 512], st=stD1)
            s_sg = sbuf("s_sg", [128, 512], st=stD1)
            s_uv = sbuf("s_uv", [128, 512], st=stD1)
            s_mid = sbuf("s_mid", [128, 512], st=stD1)
            s_sam4 = sbuf("s_sam4", [128, 4 * 512], st=stD1)


            def cross_part_max_to_rows(src_mx, dst_row, ntile, mxc_buf, mxT_buf, s_oh16):
                recs = []
                for j in range(ntile):
                    bk2 = 2 + j % 2
                    cT_ = p.emit("tensor", lambda e, j=j, bk2=bk2: e.transpose(
                        BANKS[bk2][:, 0:128], src_mx[:, j * 128:(j + 1) * 128], s_i128[:]),
                        waits=bank_wait(bk2) + ([("ov", p.cur("ov")),
                                                 ("oc", p.cur("oc"))] if j == 0 else []),
                        inc=("ot", 1))
                    cr_ = p.emit("vector", lambda e, j=j, bk2=bk2: e.reduce_max(
                        mxc_buf[:, j:j + 1].unsqueeze(2),
                        BANKS[bk2][:, 0:128].unsqueeze(1), AX.X),
                        waits=[("ot", cT_)])
                    bank_rec[bk2] = ("ov", cr_)
                cT2_ = p.emit("tensor", lambda e: e.transpose(
                    BANKS[4][0:ntile, 0:128], mxc_buf[:, 0:ntile], s_i128[:]),
                    waits=bank_wait(4) + [("ov", p.cur("ov"))], inc=("ot", 1))
                cv2_ = p.emit("vector", lambda e: e.tensor_copy(
                    mxT_buf[0:ntile, 0:128], BANKS[4][0:ntile, 0:128]),
                    waits=[("ot", cT2_)])
                bank_rec[4] = ("ov", cv2_)
                for j in range(ntile):
                    bk2 = 5 + j % 2
                    cT3_ = p.emit("tensor", lambda e, j=j, bk2=bk2: e.matmul(
                        BANKS[bk2][:, 0:128], s_oh16[0:ntile, j * 128:(j + 1) * 128],
                        mxT_buf[0:ntile, 0:128], start=True, stop=True),
                        waits=bank_wait(bk2) + ([("ov", cv2_)] if j <= 1 else []),
                        inc=("ot", 1))
                    cv3_ = p.emit("vector", lambda e, j=j, bk2=bk2: e.tensor_copy(
                        dst_row[:, j * 128:(j + 1) * 128], BANKS[bk2][:, 0:128]),
                        waits=[("ot", cT3_)])
                    bank_rec[bk2] = ("ov", cv3_)
                return p.cur("ov")

            seed2_c = nc.inline_tensor(np.array([[2048.0, 8192.0]], np.float32), name="seed2c").ap()
            s_sd2 = sbuf("s_sd2", [1, 2], st=stD1)
            cq = p.emit("sync", lambda e: e.dma_start(out=s_sd2[:], in_=seed2_c[:]),
                        inc=("qDc", 16))
            csq = p.emit("scalar", lambda e: e.activation(s_sd2[:], s_sd2[:], AF.Sqrt),
                         waits=[("qDc", cq)])
            crc = p.emit("vector", lambda e: e.reciprocal(s_sd2[:], s_sd2[:]),
                         waits=[("oc", csq)])
            cT = p.emit("tensor", lambda e: e.matmul(BANKS[0][:, 0:2], s_onesr[:],
                                                     s_sd2[:], start=True, stop=True),
                        waits=bank_wait(0) + [("ov", crc)], inc=("ot", 1))
            cv = p.emit("vector", lambda e: e.tensor_copy(s_isqd[:], BANKS[0][:, 0:2]),
                        waits=[("ot", cT)])
            bank_rec[0] = ("ov", cv)

            y1f_shv = y1f_sc.rearrange("c (a b) t -> (c a) (b t)", b=16)
            shin_carry = {}
            def sweep1_q1(tag, rnd):
                shin_rel = dict(shin_carry)
                prev_mx = None
                for m in range(64):
                    slot = m % 2
                    bk = 2 + m % 2
                    wsh = [shin_rel[m - 2]] if (m - 2) in shin_rel else []
                    cin = p.emit("sync", lambda e, m=m, slot=slot: e.dma_start(
                        out=s_shin[:, slot * 512:(slot + 1) * 512],
                        in_=y1f_shv[:, m * 512:(m + 1) * 512]),
                        waits=wsh, inc=(f"s1q{tag}{slot}", 16))
                    cmm = p.emit("tensor", lambda e, slot=slot, bk=bk: e.matmul(
                        BANKS[bk][:, 0:512], s_m1[:], s_shin[:, slot * 512:(slot + 1) * 512],
                        start=True, stop=True),
                        waits=bank_wait(bk) + [(f"s1q{tag}{slot}", cin)], inc=("ot", 1))
                    tb = m % 4
                    bblk = m // 4
                    if not rnd:
                        if bblk == 0:
                            cev = p.emit("scalar", lambda e, bk=bk, tb=tb: e.activation(
                                s_mx[:, tb * 512:(tb + 1) * 512], BANKS[bk][:, 0:512],
                                AF.Abs), waits=[("ot", cmm)])
                            cev = ("oc", cev)
                        else:
                            cab = p.emit("scalar", lambda e, bk=bk: e.activation(
                                s_abs1[:], BANKS[bk][:, 0:512], AF.Abs),
                                waits=[("ot", cmm)] + ([prev_mx] if prev_mx else []))
                            cev = p.emit("vector", lambda e, tb=tb: e.tensor_max(
                                s_mx[:, tb * 512:(tb + 1) * 512],
                                s_mx[:, tb * 512:(tb + 1) * 512], s_abs1[:]),
                                waits=[("oc", cab)])
                            cev = ("ov", cev)
                            prev_mx = cev
                    else:
                        cev = p.emit("vector", lambda e, bk=bk, tb=tb, m=m: e._custom_dve(
                            RND_TT, out=s_q1[:, m * 512:(m + 1) * 512],
                            in0=BANKS[bk][:, 0:512],
                            in1=s_row1[:, tb * 512:(tb + 1) * 512], s0=MAGIC),
                            waits=[("ot", cmm)])
                        cev = ("ov", cev)
                    bank_rec[bk] = cev
                    shin_rel[m] = ("ot", cmm)
                shin_carry.clear()
                shin_carry[-2] = shin_rel[62]
                shin_carry[-1] = shin_rel[63]
            sweep1_q1("a", False)
            s_mxc1 = sbuf("s_mxc1", [128, 16], st=stD1)
            s_mxT1 = sbuf("s_mxT1", [16, 128], st=stD1)
            s_oh16a = sbuf("s_oh16a", [16, 16 * 128], st=stD1)
            coh = p.emit("sync", lambda e: e.dma_start(out=s_oh16a[:], in_=oh16_c[:]),
                         inc=("qFc", 16))
            p.barrier([("qFc", p.cur("qFc"))])
            cpm = cross_part_max_to_rows(s_mx, s_row1, 16, s_mxc1, s_mxT1, s_oh16a)
            ca = p.emit("vector", lambda e: e.tensor_scalar(
                s_row1[:], s_row1[:], s_isqd[:, 0:1], 1e-5, ALU.mult, ALU.max),
                waits=[("ov", cpm)])
            cfg = p.emit("vector", lambda e: e.tensor_scalar(
                s_fg[:], s_row1[:], s_mwc[:, 0:1], 1.0 / 127.0, ALU.mult, ALU.mult))
            cfu = p.emit("vector", lambda e: e.tensor_scalar(
                s_fu[:], s_row1[:], s_mwc[:, 1:2], 1.0 / 127.0, ALU.mult, ALU.mult))
            crr = p.emit("vector", lambda e: e.reciprocal(s_row1[:], s_row1[:]))
            cs1 = p.emit("vector", lambda e: e.tensor_scalar(
                s_row1[:], s_row1[:], s_isqd[:, 0:1], 127.0, ALU.mult, ALU.mult))
            sweep1_q1("b", True)
            q1_done = p.cur("ov")

            samv = sam_sc.rearrange("c p t -> p c t")
            tern_rel = None
            mm_rel_gu = {}
            for os_ in range(16):
                wts = [tern_rel] if tern_rel else []
                cda = p.emit("sync", lambda e, os_=os_: e.dma_start(
                    out=s_wgf[:].rearrange("p (k e) -> p k e", k=16),
                    in_=wgv[:, :, os_ * 512:(os_ + 1) * 512]),
                    waits=wts + ([mm_rel_gu[os_ - 1]] if os_ >= 1 else []),
                    inc=("wgq", 16))
                cwgq = p.cur("wgq")
                ctg = p.emit("vector", lambda e: e._custom_dve(
                    TERN_TS, out=s_wgt[:], in0=s_wgf[:], s0=MAGIC, s1=s_rw[:, 0:1],
                    imm2=1.0), waits=[("wgq", cwgq), ("ov", q1_done)])
                cdu = p.emit("sync", lambda e, os_=os_: e.dma_start(
                    out=s_wgf[:].rearrange("p (k e) -> p k e", k=16),
                    in_=wuv[:, :, os_ * 512:(os_ + 1) * 512]),
                    waits=[("ov", ctg)], inc=("wgq", 16))
                cwuq = p.cur("wgq")
                ctu = p.emit("vector", lambda e: e._custom_dve(
                    TERN_TS, out=s_wut[:], in0=s_wgf[:], s0=MAGIC, s1=s_rw[:, 1:2],
                    imm2=1.0), waits=[("wgq", cwuq)])
                tern_rel = ("ov", ctu)
                for tb in range(4):
                    for osub in range(4):
                        it = tb * 4 + osub
                        bg = 4 + it % 2
                        bu = 6 + it % 2
                        cmg = cmu = None
                        for b16 in range(16):
                            wg_ = (bank_wait(bg) + [("ov", ctu)]) if b16 == 0 else []
                            cmg = p.emit("tensor", lambda e, b16=b16, osub=osub, tb=tb,
                                         bg=bg: e.matmul(
                                             BANKS[bg][:, 0:512],
                                             s_wgt[:, b16 * 512 + osub * 128: b16 * 512 + (osub + 1) * 128],
                                             s_q1[:, b16 * 2048 + tb * 512: b16 * 2048 + (tb + 1) * 512],
                                             start=(b16 == 0), stop=(b16 == 15)),
                                         waits=wg_, inc=(("ot", 1) if b16 == 15 else None))
                        for b16 in range(16):
                            wu_ = bank_wait(bu) if b16 == 0 else []
                            cmu = p.emit("tensor", lambda e, b16=b16, osub=osub, tb=tb,
                                         bu=bu: e.matmul(
                                             BANKS[bu][:, 0:512],
                                             s_wut[:, b16 * 512 + osub * 128: b16 * 512 + (osub + 1) * 128],
                                             s_q1[:, b16 * 2048 + tb * 512: b16 * 2048 + (tb + 1) * 512],
                                             start=(b16 == 0), stop=(b16 == 15)),
                                         waits=wu_, inc=(("ot", 1) if b16 == 15 else None))
                        cg_ = p.emit("vector", lambda e, bg=bg, tb=tb: e.tensor_tensor(
                            s_gv[:], BANKS[bg][:, 0:512], s_fg[:, tb * 512:(tb + 1) * 512],
                            ALU.mult), waits=[("ot", cmu)])
                        bank_rec[bg] = ("ov", cg_)
                        csg = p.emit("scalar", lambda e: e.activation(
                            s_sg[:], s_gv[:], AF.Silu), waits=[("ov", cg_)])
                        cu_ = p.emit("vector", lambda e, bu=bu, tb=tb: e.tensor_tensor(
                            s_uv[:], BANKS[bu][:, 0:512], s_fu[:, tb * 512:(tb + 1) * 512],
                            ALU.mult))
                        bank_rec[bu] = ("ov", cu_)
                        cmid = p.emit("vector", lambda e: e.tensor_mul(
                            s_mid[:], s_sg[:], s_uv[:]), waits=[("oc", csg)])
                        bA = 2 + it % 2
                        cfa = p.emit("tensor", lambda e, bA=bA: e.matmul(
                            BANKS[bA][:, 0:512], s_h128[:], s_mid[:], start=True, stop=True),
                            waits=bank_wait(bA) + [("ov", cmid)], inc=("ot", 1))
                        cfv = p.emit("vector", lambda e, bA=bA, osub=osub: e.tensor_copy(
                            s_sam4[:, osub * 512:(osub + 1) * 512], BANKS[bA][:, 0:512]),
                            waits=[("ot", cfa)] + ([("smq", smq_last)] if os_ + tb > 0 else []))
                        bank_rec[bA] = ("ov", cfv)
                    smq_last = p.emit("sync", lambda e, os_=os_, tb=tb: e.dma_start(
                        out=samv[:, os_ * 4:(os_ + 1) * 4, tb * 512:(tb + 1) * 512],
                        in_=s_sam4[:].rearrange("p (c t) -> p c t", c=4)),
                        waits=[("ov", p.cur("ov"))], inc=("smq", 16))
                mm_rel_gu[os_] = ("ot", p.cur("ot"))
            s_wdf = s_wgf
            s_wdt = s_wgt
            wdtv = wdt_sp.rearrange("(b p) e -> p b e", p=128)
            wd_rel = {}
            for b2 in range(64):
                slot = b2 % 2
                wts = [wd_rel[b2 - 2]] if b2 >= 2 else (
                    [tern_rel, mm_rel_gu[15]] if b2 == 0 else [])
                cdd = p.emit("sync", lambda e, b2=b2, slot=slot: e.dma_start(
                    out=s_wdf[:, slot * 2048:(slot + 1) * 2048], in_=wdv[:, b2, :]),
                    waits=wts, inc=(f"wdq{slot}", 16))
                ctd = p.emit("vector", lambda e, slot=slot: e._custom_dve(
                    TERN_TS, out=s_wdt[:, slot * 2048:(slot + 1) * 2048],
                    in0=s_wdf[:, slot * 2048:(slot + 1) * 2048], s0=MAGIC,
                    s1=s_rw[:, 2:3], imm2=1.0), waits=[(f"wdq{slot}", cdd)])
                cds = p.emit("sync", lambda e, b2=b2, slot=slot: e.dma_start(
                    out=wdtv[:, b2, :], in_=s_wdt[:, slot * 2048:(slot + 1) * 2048]),
                    waits=[("ov", ctd)], inc=(f"wdsq{slot}", 16))
                wd_rel[b2] = (f"wdsq{slot}", cds)
            p.barrier([("smq", p.cur("smq")), ("wdsq0", p.cur("wdsq0")),
                       ("wdsq1", p.cur("wdsq1")), ("ov", p.cur("ov")),
                       ("oc", p.cur("oc")), ("ot", p.cur("ot"))])
            stD1.close()

        if UPTO >= 6:
            stD2 = ExitStack()
            s_q2 = sbuf("s_q2", [128, 64 * 512], BF16, st=stD2)
            s_mx2 = sbuf("s_mx2", [128, 512], st=stD2)
            s_row2 = sbuf("s_row2", [128, 512], st=stD2)
            s_fd = sbuf("s_fd", [128, 512], st=stD2)
            s_shin2 = sbuf("s_shin2", [128, 2 * 2 * 512], st=stD2)
            s_mxc2 = sbuf("s_mxc2", [128, 16], st=stD2)
            s_mxT2 = sbuf("s_mxT2", [16, 128], st=stD2)
            s_oh16b = sbuf("s_oh16b", [16, 4 * 128], st=stD2)
            s_wdts = sbuf("s_wdts", [128, 2 * 64 * 128], BF16, st=stD2)
            s_y1c = sbuf("s_y1c", [128, 16 * 512], st=stD2)
            s_res2 = sbuf("s_res2", [128, 16 * 512], st=stD2)
            s_sq2 = sbuf("s_sq2", [128, 512], st=stD2)
            s_st2 = sbuf("s_st2", [1, 3 * 512], st=stD2)
            s_mb2 = sbuf("s_mb2", [128, 512], st=stD2)
            s_rb2 = sbuf("s_rb2", [128, 512], st=stD2)
            s_ot = sbuf("s_ot", [128, 512], st=stD2)
            s_abs2 = s_ot
            s_isq2 = sbuf("s_isq2", [128, 2], st=stD2)

            seed3_c = nc.inline_tensor(np.array([[2048.0, 8192.0]], np.float32), name="seed3c").ap()
            s_sd3 = sbuf("s_sd3", [1, 2], st=stD2)
            coh2 = p.emit("sync", lambda e: e.dma_start(out=s_oh16b[:], in_=oh16_c[:, 0:512]),
                          inc=("qGc", 16))
            p.barrier([("qGc", p.cur("qGc"))])
            cq = p.emit("sync", lambda e: e.dma_start(out=s_sd3[:], in_=seed3_c[:]),
                        inc=("qEc", 16))
            csq = p.emit("scalar", lambda e: e.activation(s_sd3[:], s_sd3[:], AF.Sqrt),
                         waits=[("qEc", cq)])
            crc = p.emit("vector", lambda e: e.reciprocal(s_sd3[:], s_sd3[:]),
                         waits=[("oc", csq)])
            cT = p.emit("tensor", lambda e: e.matmul(BANKS[0][:, 0:2], s_onesr[:],
                                                     s_sd3[:], start=True, stop=True),
                        waits=bank_wait(0) + [("ov", crc)], inc=("ot", 1))
            cv = p.emit("vector", lambda e: e.tensor_copy(s_isq2[:], BANKS[0][:, 0:2]),
                        waits=[("ot", cT)])
            bank_rec[0] = ("ov", cv)

            last_out = None
            sh2_carry = {}
            sam_shv = sam_sc.rearrange("c (a b) t -> (c a) (b t)", b=64)
            y1v2 = y1_sp.rearrange("(k p) t -> p k t", p=128)
            ofv = out_fm.rearrange("(k p) t -> p k t", p=128)
            for qt in range(4):
                q0 = qt * 512
                prev_mx2 = None
                sam_3d = sam_shv[:].rearrange("p (b t) -> p b t", b=64)
                for rnd in (False, True):
                    tag = "r" if rnd else "m"
                    sh_rel = dict(sh2_carry)
                    for g in range(32):
                        slot = g % 2
                        wsh = [sh_rel[g - 2]] if (g - 2) in sh_rel else []
                        cin = p.emit("sync", lambda e, g=g, slot=slot, q0=q0: e.dma_start(
                            out=s_shin2[:, slot * 1024:(slot + 1) * 1024]
                            .rearrange("p (j t) -> p j t", j=2),
                            in_=sam_3d[:, g * 2:(g + 1) * 2, q0:q0 + 512]),
                            waits=wsh, inc=(f"s2q{tag}{qt}{slot}", 16))
                        cmm_g = None
                        for j in range(2):
                            b2 = g * 2 + j
                            bk = 2 + b2 % 2
                            cmm = p.emit("tensor", lambda e, slot=slot, j=j, bk=bk: e.matmul(
                                BANKS[bk][:, 0:512], s_m2[:],
                                s_shin2[:, slot * 1024 + j * 512: slot * 1024 + (j + 1) * 512],
                                start=True, stop=True),
                                waits=bank_wait(bk) +
                                      ([(f"s2q{tag}{qt}{slot}", cin)] if j == 0 else []),
                                inc=("ot", 1))
                            cmm_g = cmm
                            if not rnd:
                                if b2 == 0:
                                    cev = p.emit("scalar", lambda e, bk=bk: e.activation(
                                        s_mx2[:], BANKS[bk][:, 0:512], AF.Abs),
                                        waits=[("ot", cmm)])
                                    cev = ("oc", cev)
                                else:
                                    wab = [("ot", cmm)]
                                    if prev_mx2:
                                        wab.append(prev_mx2)
                                    elif last_out:
                                        wab.append(("qO", last_out))
                                    cab = p.emit("scalar", lambda e, bk=bk: e.activation(
                                        s_abs2[:], BANKS[bk][:, 0:512], AF.Abs),
                                        waits=wab)
                                    cev = p.emit("vector", lambda e: e.tensor_max(
                                        s_mx2[:], s_mx2[:], s_abs2[:]),
                                        waits=[("oc", cab)])
                                    cev = ("ov", cev)
                                    prev_mx2 = cev
                            else:
                                cev = p.emit("vector", lambda e, bk=bk, b2=b2: e._custom_dve(
                                    RND_TT, out=s_q2[:, b2 * 512:(b2 + 1) * 512],
                                    in0=BANKS[bk][:, 0:512], in1=s_row2[:], s0=MAGIC),
                                    waits=[("ot", cmm)])
                                cev = ("ov", cev)
                            bank_rec[bk] = cev
                        sh_rel[g] = ("ot", cmm_g)
                    sh2_carry.clear()
                    sh2_carry[-2] = sh_rel[30]
                    sh2_carry[-1] = sh_rel[31]
                    if not rnd:
                        cpm2 = cross_part_max_to_rows(s_mx2, s_row2, 4, s_mxc2, s_mxT2, s_oh16b)
                        ca = p.emit("vector", lambda e: e.tensor_scalar(
                            s_row2[:], s_row2[:], s_isq2[:, 1:2], 1e-5, ALU.mult, ALU.max),
                            waits=[("ov", cpm2)])
                        cfd = p.emit("vector", lambda e: e.tensor_scalar(
                            s_fd[:], s_row2[:], s_mwc[:, 2:3], 1.0 / 127.0, ALU.mult,
                            ALU.mult))
                        crr = p.emit("vector", lambda e: e.reciprocal(s_row2[:], s_row2[:]))
                        cs2 = p.emit("vector", lambda e: e.tensor_scalar(
                            s_row2[:], s_row2[:], s_isq2[:, 1:2], 127.0, ALU.mult, ALU.mult))
                q2_done = p.cur("ov")
                cy1q = p.emit("sync", lambda e, q0=q0: e.dma_start(
                    out=s_y1c[:].rearrange("p (k t) -> p k t", k=16),
                    in_=y1v2[:, :, q0:q0 + 512]), inc=(f"yq2{qt}", 16))
                wslab_rel = {}
                for oi in range(16):
                    slot = oi % 2
                    wts = [wslab_rel[oi - 2]] if oi >= 2 else []
                    cwd2 = p.emit("sync", lambda e, oi=oi, slot=slot: e.dma_start(
                        out=s_wdts[:, slot * 8192:(slot + 1) * 8192]
                        .rearrange("p (b e) -> p b e", b=64),
                        in_=wdtv[:, :, oi * 128:(oi + 1) * 128]),
                        waits=wts, inc=(f"wd2{qt}{slot}", 16))
                    bk = 4 + oi % 2
                    cmm = None
                    for b2 in range(64):
                        wmm = (bank_wait(bk) + [(f"wd2{qt}{slot}", cwd2),
                                                ("ov", q2_done)]) if b2 == 0 else []
                        cmm = p.emit("tensor", lambda e, b2=b2, bk=bk, slot=slot: e.matmul(
                            BANKS[bk][:, 0:512],
                            s_wdts[:, slot * 8192 + b2 * 128: slot * 8192 + (b2 + 1) * 128],
                            s_q2[:, b2 * 512:(b2 + 1) * 512],
                            start=(b2 == 0), stop=(b2 == 63)),
                            waits=wmm, inc=(("ot", 1) if b2 == 63 else None))
                    wslab_rel[oi] = ("ot", cmm)
                    cr1 = p.emit("vector", lambda e, bk=bk: e.tensor_mul(
                        s_ot[:], BANKS[bk][:, 0:512], s_fd[:]),
                        waits=[("ot", cmm), (f"yq2{qt}", cy1q)] +
                              ([("qO", last_out)] if last_out else []))
                    cr2 = p.emit("vector", lambda e, oi=oi: e.tensor_add(
                        s_res2[:, oi * 512:(oi + 1) * 512], s_ot[:],
                        s_y1c[:, oi * 512:(oi + 1) * 512]))
                    bank_rec[bk] = ("ov", cr2)
                    csqr = p.emit("scalar", lambda e, oi=oi: e.activation(
                        s_sq2[:], s_res2[:, oi * 512:(oi + 1) * 512], AF.Square),
                        waits=[("ov", cr2)])
                    p.emit("tensor", lambda e, oi=oi: e.matmul(
                        BANKS[6][0:1, 0:512], s_onescol[:],
                        s_res2[:, oi * 512:(oi + 1) * 512],
                        start=(oi == 0), stop=(oi == 15)),
                        waits=(bank_wait(6) if oi == 0 else []) + [("ov", cr2)])
                    cst2 = p.emit("tensor", lambda e, oi=oi: e.matmul(
                        BANKS[7][0:1, 0:512], s_onescol[:], s_sq2[:],
                        start=(oi == 0), stop=(oi == 15)),
                        waits=(bank_wait(7) if oi == 0 else []) + [("oc", csqr)],
                        inc=(("ot", 1) if oi == 15 else None))
                cm1 = p.emit("vector", lambda e: e.tensor_scalar_mul(
                    s_st2[0:1, 0:512], BANKS[6][0:1, 0:512], 1.0 / 2048.0),
                    waits=[("ot", cst2)])
                cm2 = p.emit("vector", lambda e: e.tensor_scalar_mul(
                    s_st2[0:1, 512:1024], BANKS[7][0:1, 0:512], 1.0 / 2048.0))
                bank_rec[6] = ("ov", cm2)
                bank_rec[7] = ("ov", cm2)
                cm3 = p.emit("vector", lambda e: e.tensor_tensor(
                    s_st2[0:1, 1024:1536], s_st2[0:1, 0:512], s_st2[0:1, 0:512], ALU.mult))
                cm4 = p.emit("vector", lambda e: e.scalar_tensor_tensor(
                    s_st2[0:1, 512:1024], s_st2[0:1, 512:1024], 1e-5,
                    s_st2[0:1, 1024:1536], ALU.add, ALU.subtract))
                cm5 = p.emit("scalar", lambda e: e.activation(
                    s_st2[0:1, 1024:1536], s_st2[0:1, 512:1024], AF.Sqrt),
                    waits=[("ov", cm4)])
                cm6 = p.emit("vector", lambda e: e.reciprocal(
                    s_st2[0:1, 512:1024], s_st2[0:1, 1024:1536]), waits=[("oc", cm5)])
                cb1 = p.emit("tensor", lambda e: e.matmul(
                    BANKS[2][:, 0:512], s_onesr[:], s_st2[0:1, 0:512],
                    start=True, stop=True), waits=bank_wait(2) + [("ov", cm6)],
                    inc=("ot", 1))
                cb1v = p.emit("vector", lambda e: e.tensor_copy(s_mb2[:], BANKS[2][:, 0:512]),
                              waits=[("ot", cb1)])
                bank_rec[2] = ("ov", cb1v)
                cb2 = p.emit("tensor", lambda e: e.matmul(
                    BANKS[3][:, 0:512], s_onesr[:], s_st2[0:1, 512:1024],
                    start=True, stop=True), waits=bank_wait(3) + [("ov", cm6)],
                    inc=("ot", 1))
                cb2v = p.emit("vector", lambda e: e.tensor_copy(s_rb2[:], BANKS[3][:, 0:512]),
                              waits=[("ot", cb2)])
                bank_rec[3] = ("ov", cb2v)
                for oi in range(16):
                    co1 = p.emit("vector", lambda e, oi=oi: e.tensor_sub(
                        s_ot[:], s_res2[:, oi * 512:(oi + 1) * 512], s_mb2[:]),
                        waits=[("ov", cb2v)] + ([("qO", last_out)] if last_out else []))
                    co2 = p.emit("vector", lambda e: e.tensor_mul(s_ot[:], s_ot[:], s_rb2[:]))
                    last_out = p.emit("sync", lambda e, oi=oi, q0=q0: e.dma_start(
                        out=ofv[:, oi, q0:q0 + 512], in_=s_ot[:]),
                        waits=[("ov", co2)], inc=("qO", 16))
            stD2.close()

        s_dbg = sbuf("s_dbg", [128, 2 * TT])
        cdbg = p.emit("sync", lambda e: e.dma_start(
            out=s_dbg[:, 0:TOWN], in_=y1_sp[0:128, :]), inc=("qD1", 16))
        cdbg2 = p.emit("sync", lambda e: e.dma_start(
            out=s_dbg[:, TT:TT + TOWN], in_=y1f_sc[0, :, :]), inc=("qD2", 16))
        p.emit("sync", lambda e: e.dma_start(out=dbg[:, 0:TOWN],
               in_=s_dbg[:, 0:TOWN]),
               waits=[("qD1", cdbg), ("qD2", cdbg2)], inc=("qD3", 16))
        p.emit("sync", lambda e: e.dma_start(out=dbg[:, TT:TT + TOWN],
               in_=s_dbg[:, TT:TT + TOWN]), inc=("qD3", 16))
        p.build_block(block, final_waits=[(n, p.cur(n)) for n in list(p.counts.keys())])
    return nc


def _shard_inputs(L0_attn_input_, L0_attn_in_proj_w, L0_attn_conv1d_weight,
                  L0_attn_conv1d_bias, L0_attn_z_bias, L0_attn_D, L0_attn_out_proj_w,
                  L0_ffn_w_gate, L0_ffn_w_up, L0_ffn_w_down):
    X = np.ascontiguousarray(np.asarray(L0_attn_input_, dtype=np.float32)[0])
    wt_in = np.ascontiguousarray(np.asarray(L0_attn_in_proj_w, np.float32).T)
    convw = np.ascontiguousarray(
        np.asarray(L0_attn_conv1d_weight, np.float32).reshape(CONVD, 4)
        .reshape(40, 128, 4).transpose(1, 0, 2).reshape(128, 160))
    convb = np.ascontiguousarray(
        np.asarray(L0_attn_conv1d_bias, np.float32).reshape(40, 128).T)
    zbias = np.ascontiguousarray(np.asarray(L0_attn_z_bias, np.float32).reshape(32, 128).T)
    dv = np.asarray(L0_attn_D, np.float32).reshape(1, 64)
    wt_out = np.ascontiguousarray(np.asarray(L0_attn_out_proj_w, np.float32).T)

    perm1 = np.empty(2048, np.int64)
    for b in range(16):
        for c in range(16):
            for a in range(8):
                perm1[b * 128 + c * 8 + a] = c * 128 + a * 16 + b
    perm2 = np.empty(8192, np.int64)
    for b2 in range(64):
        for c in range(64):
            for a2 in range(2):
                perm2[b2 * 128 + c * 2 + a2] = c * 128 + a2 * 64 + b2
    wg = np.ascontiguousarray(np.asarray(L0_ffn_w_gate, np.float32).T[perm1, :])
    wu = np.ascontiguousarray(np.asarray(L0_ffn_w_up, np.float32).T[perm1, :])
    wd = np.ascontiguousarray(np.asarray(L0_ffn_w_down, np.float32).T[perm2, :])

    in_maps = []
    for k in range(NCORE):
        lo = k * TOWN - HALO
        sl = np.zeros((TT, DM), np.float32)
        src0 = max(lo, 0)
        sl[TT - (k * TOWN + TOWN - src0):] = X[src0: (k + 1) * TOWN]
        x_fmk = np.ascontiguousarray(sl.T)
        hm = np.full((128, 1), 0.0 if k == 0 else 1.0, np.float32)
        in_maps.append({"x_fm": x_fmk, "wt_in": wt_in, "convw": convw, "convb": convb,
                        "zb": zbias, "dvec": dv, "wt_out": wt_out,
                        "wg_p": wg, "wu_p": wu, "wd_p": wd, "hm_fm": hm})
    return in_maps


_NC_CACHE = {}

def _get_nc():
    if "nc" not in _NC_CACHE:
        nc = build_program()
        lower_extended_insts(nc)
        _NC_CACHE["nc"] = nc
    return _NC_CACHE["nc"]


def kernel(**inputs):
    in_maps = _shard_inputs(**inputs)
    nc = _get_nc()
    res = run_bass_kernel_spmd(nc, in_maps, core_ids=list(range(NCORE)))
    out = np.empty((16384, 2048), np.float32)
    for k in range(NCORE):
        out[k * TOWN:(k + 1) * TOWN, :] = np.asarray(res.results[k]["out_fm"]).T
    return out.reshape(1, 16384, 2048)
